# version 01
# baseline (speedup 1.0000x reference)
import numpy as np
import ml_dtypes
from contextlib import ExitStack

import concourse.bass as bass
import concourse.mybir as mybir
from concourse.alu_op_type import AluOpType as OP
from concourse.bass_utils import run_bass_kernel_spmd

F32 = mybir.dt.float32
BF16 = mybir.dt.bfloat16
I16 = mybir.dt.int16
AF = mybir.ActivationFunctionType
AX = mybir.AxisListType.X

P = 128
T = 4096
D = 1024
FF = 3072
DC = D // P
FCN = FF // P
SL = 512
NSL = T // SL
TCN = T // P
RC = 1.5 * 2.0 ** 23
QA = 32767.0
QW = 7.0
ENGS = ("sync", "vector", "scalar", "tensor", "gpsimd")


class Ser:

    def __init__(self, nc, chain):
        self.nc = nc
        self.chain = chain
        self.c = 0
        self.gap_c = 0
        self.auto = True
        self.q = None

    def start_phase(self):
        self.q = {e: [] for e in ENGS}

    def do(self, ekey, fn, inc=1):
        tgt = self.gap_c

        def emit(eng, fn=fn, tgt=tgt, inc=inc):
            if tgt > 0:
                eng.wait_ge(self.chain, tgt)
            fn(eng).then_inc(self.chain, inc)

        self.q[ekey].append(emit)
        self.c += inc
        if self.auto:
            self.gap_c = self.c

    def gap(self):
        self.gap_c = self.c

    def wait_all(self, ekey):
        tgt = self.c

        def emit(eng, tgt=tgt):
            eng.wait_ge(self.chain, tgt)

        self.q[ekey].append(emit)

    def end_phase(self, blk):
        for e in ENGS:
            items = self.q[e]
            if items:
                getattr(blk, e)(lambda eng, items=items: [f(eng) for f in items])
        self.q = None


def build(debug=False):
    nc = bass.Bass("TRN2", target_bir_lowering=False, debug=False)
    import itertools
    _uid = itertools.count()

    def nm(s):
        return f"{s}_{next(_uid)}"


    x_in = nc.dram_tensor("x", [T, D], F32, kind="ExternalInput").ap()
    wq_h = nc.dram_tensor("wq", [D, D], F32, kind="ExternalInput").ap()
    wk_h = nc.dram_tensor("wk", [D, D], F32, kind="ExternalInput").ap()
    wv_h = nc.dram_tensor("wv", [D, D], F32, kind="ExternalInput").ap()
    wg_h = nc.dram_tensor("wg", [D, D], F32, kind="ExternalInput").ap()
    wo_h = nc.dram_tensor("wo", [D, D], F32, kind="ExternalInput").ap()
    bq_h = nc.dram_tensor("bq", [D, 1], F32, kind="ExternalInput").ap()
    bk_h = nc.dram_tensor("bk", [1, D], F32, kind="ExternalInput").ap()
    bv_h = nc.dram_tensor("bv", [1, D], F32, kind="ExternalInput").ap()
    bg_h = nc.dram_tensor("bg", [D, 1], F32, kind="ExternalInput").ap()
    bo_h = nc.dram_tensor("bo", [D, 1], F32, kind="ExternalInput").ap()
    wgate_h = nc.dram_tensor("wgate", [FF, D], F32, kind="ExternalInput").ap()
    wup_h = nc.dram_tensor("wup", [FF, D], F32, kind="ExternalInput").ap()
    wdown_h = nc.dram_tensor("wdown", [D, FF], F32, kind="ExternalInput").ap()
    id32_h = nc.dram_tensor("id32", [P, P], F32, kind="ExternalInput").ap()
    idbf_h = nc.dram_tensor("idbf", [P, P], BF16, kind="ExternalInput").ap()
    onesc_h = nc.dram_tensor("onesc", [P, 1], F32, kind="ExternalInput").ap()
    onesr_h = nc.dram_tensor("onesr", [1, P], F32, kind="ExternalInput").ap()
    onescb_h = nc.dram_tensor("onescb", [P, 1], BF16, kind="ExternalInput").ap()
    patt2_h = nc.dram_tensor("patt2", [2, P], F32, kind="ExternalInput").ap()

    y_out = nc.dram_tensor("y", [T, D], F32, kind="ExternalOutput").ap()

    xfm = nc.dram_tensor("xfm", [DC * P, T], F32).ap()
    mg = nc.dram_tensor("mg", [FCN * P, T], F32).ap()
    mup = nc.dram_tensor("mup", [FCN * P, T], F32).ap()
    ci = nc.dram_tensor("ci", [FCN * P, T], F32).ap()
    g2s = nc.dram_tensor("g2s", [FCN * P, T], I16).ap()
    ups = nc.dram_tensor("ups", [FCN * P, T], I16).ap()
    wgT_c = nc.dram_tensor("wgT_c", [DC * P, FF], BF16).ap()
    wupT_c = nc.dram_tensor("wupT_c", [DC * P, FF], BF16).ap()
    wdT_c = nc.dram_tensor("wdT_c", [FCN * P, D], BF16).ap()
    cc_in = nc.dram_tensor("cc_in", [1, 8], F32)
    cc_out = nc.dram_tensor("cc_out", [1, 8], F32)
    if debug:
        dbg = [nc.dram_tensor(f"dbg{i}", [DC * P, T], F32, kind="ExternalOutput").ap() for i in range(3)]
        dbgs = nc.dram_tensor("dbgs", [1, 64], F32, kind="ExternalOutput").ap()

    with ExitStack() as top:
        chain = top.enter_context(nc.semaphore("chain"))
        S = Ser(nc, chain)

        id32 = top.enter_context(nc.sbuf_tensor(nm("id32s"), [P, P], F32))
        idbf = top.enter_context(nc.sbuf_tensor(nm("idbfs"), [P, P], BF16))
        onesc = top.enter_context(nc.sbuf_tensor(nm("onescs"), [P, 1], F32))
        onesr = top.enter_context(nc.sbuf_tensor(nm("onesrs"), [1, P], F32))
        onescb = top.enter_context(nc.sbuf_tensor(nm("onescbs"), [P, 1], BF16))
        patt2 = top.enter_context(nc.sbuf_tensor(nm("patt2s"), [2, P], F32))
        crnd = top.enter_context(nc.sbuf_tensor(nm("crnd"), [P, 1], F32))
        c65k = top.enter_context(nc.sbuf_tensor(nm("c65k"), [P, 1], F32))
        ceps = top.enter_context(nc.sbuf_tensor(nm("ceps"), [1, 1], F32))
        crnd_r = top.enter_context(nc.sbuf_tensor(nm("crnd_r"), [1, 1], F32))
        czero = top.enter_context(nc.sbuf_tensor(nm("czero"), [P, 1], F32))
        scrow = top.enter_context(nc.sbuf_tensor(nm("scrow"), [1, 32], F32))
        sc = top.enter_context(nc.sbuf_tensor(nm("sc"), [P, 16], F32))
        ccl = top.enter_context(nc.sbuf_tensor(nm("ccl"), [1, 8], F32))
        ccr = top.enter_context(nc.sbuf_tensor(nm("ccr"), [1, 8], F32))
        bq_sb = top.enter_context(nc.sbuf_tensor(nm("bq_sb"), [P, DC], F32))
        bg_sb = top.enter_context(nc.sbuf_tensor(nm("bg_sb"), [P, DC], F32))
        bo_sb = top.enter_context(nc.sbuf_tensor(nm("bo_sb"), [P, DC], F32))
        bkr = top.enter_context(nc.sbuf_tensor(nm("bkr"), [1, D], F32))
        bvr = top.enter_context(nc.sbuf_tensor(nm("bvr"), [1, D], F32))
        lmax = top.enter_context(nc.sbuf_tensor(nm("lmax"), [P, 8], F32))
        t128 = top.enter_context(nc.sbuf_tensor(nm("t128"), [P, 2], F32))

        (IV_S1, IS_S1, IS_S3, IS_S2, IS_S4, IS_S5, IS_T0, IS_T1, IS_T2) = range(9)
        (SC_K1, SC_S3, SC_K2, SC_K3, SC_K4, SC_BETA) = range(6)

        def dma(dst, src):
            S.do("sync", lambda e, d=dst, s=src: e.dma_start(out=d, in_=s), 16)

        def vop(fn):
            S.do("vector", fn)

        def sop(fn):
            S.do("scalar", fn)

        def top_(fn):
            S.do("tensor", fn)

        def gop(fn):
            S.do("gpsimd", fn)

        def sb_row(ap):
            return ap

        H = {}

        def cross_part_absmax(src_col, dst_row):
            top_(lambda e: e.transpose(H["psx"][0:1, 0:P], src_col, id32[:]))
            vop(lambda e: e.reduce_max(dst_row, H["psx"][0:1, 0:P], axis=AX, apply_absolute_value=True))

        def bcast128(src11, slot):
            top_(lambda e: e.matmul(H["pssc"][:, slot:slot + 1], onesr[0:1, :], src11, start=True, stop=True))
            sop(lambda e: e.copy(sc[:, slot:slot + 1], H["pssc"][:, slot:slot + 1]))

        def allreduce_max(n):
            S.do("gpsimd", lambda e: e.dma_start(out=cc_in.ap(), in_=ccl[:]), 16)
            S.do("gpsimd", lambda e: e.collective_compute(
                "AllReduce", OP.max, replica_groups=[list(range(8))],
                ins=[cc_in.ap().opt()], outs=[cc_out.ap().opt()]), 1)
            S.do("gpsimd", lambda e: e.dma_start(out=ccr[:], in_=cc_out.ap()), 16)

        with nc.Block() as blk:
            S.start_phase()
            dma(id32[:], id32_h)
            dma(idbf[:], idbf_h)
            dma(onesc[:], onesc_h)
            dma(onesr[:], onesr_h)
            dma(onescb[:], onescb_h)
            dma(patt2[:], patt2_h)
            vop(lambda e: e.memset(crnd[:], RC))
            vop(lambda e: e.memset(c65k[:], 65536.0))
            vop(lambda e: e.memset(ceps[:], 1e-5))
            vop(lambda e: e.memset(crnd_r[:], RC))
            vop(lambda e: e.memset(czero[:], 0.0))
            vop(lambda e: e.memset(ccl[:], 0.0))
            for dc in range(DC):
                dma(bq_sb[:, dc:dc + 1], bq_h[dc * P:(dc + 1) * P, 0:1])
                dma(bg_sb[:, dc:dc + 1], bg_h[dc * P:(dc + 1) * P, 0:1])
                dma(bo_sb[:, dc:dc + 1], bo_h[dc * P:(dc + 1) * P, 0:1])
            dma(bkr[:], bk_h)
            dma(bvr[:], bv_h)
            S.end_phase(blk)

        with ExitStack() as _st:
            xtm = _st.enter_context(nc.sbuf_tensor(nm("xtm"), [P, D], F32))
            stf = _st.enter_context(nc.sbuf_tensor(nm("stf"), [P, DC * 1024], F32))
            pst_in = _st.enter_context(nc.psum_tensor(nm("pst_in"), [P, P], F32))
            blk = _st.enter_context(nc.Block())
            S.start_phase()
            for tc in range(TCN):
                dma(xtm[:], x_in[tc * P:(tc + 1) * P, :])
                g = tc % 8
                for dc in range(DC):
                    top_(lambda e, dc=dc: e.transpose(pst_in[:], xtm[:, dc * P:(dc + 1) * P], id32[:]))
                    sop(lambda e, dc=dc, g=g: e.copy(stf[:, dc * 1024 + g * P: dc * 1024 + (g + 1) * P], pst_in[:]))
                if g == 7:
                    base = (tc - 7) * P
                    for dc in range(DC):
                        dma(xfm[dc * P:(dc + 1) * P, base:base + 1024], stf[:, dc * 1024:(dc + 1) * 1024])
            S.end_phase(blk)

        def ln_stats(src, rows_a, rows_b):
            with ExitStack() as _st:
                lnx = _st.enter_context(nc.sbuf_tensor(nm("lnx"), [P, SL * 3], F32))
                lnq = _st.enter_context(nc.sbuf_tensor(nm("lnq"), [P, SL], F32))
                lnr = _st.enter_context(nc.sbuf_tensor(nm("lnr"), [1, SL * 3], F32))
                psS = _st.enter_context(nc.psum_tensor(nm("psS"), [1, SL], F32))
                psQ = _st.enter_context(nc.psum_tensor(nm("psQ"), [1, SL], F32))
                blk = _st.enter_context(nc.Block())
                S.start_phase()
                for sl in range(NSL):
                    for dc in range(DC):
                        xb = lnx[:, (dc % 3) * SL:(dc % 3 + 1) * SL]
                        dma(xb, src[dc * P:(dc + 1) * P, sl * SL:(sl + 1) * SL])
                        sop(lambda e, xb=xb: e.activation(lnq[:], xb, AF.Square))
                        top_(lambda e, xb=xb, dc=dc: e.matmul(
                            psS[0:1, :], onesc[:], xb, start=(dc == 0), stop=(dc == DC - 1)))
                        top_(lambda e, dc=dc: e.matmul(
                            psQ[0:1, :], onesc[:], lnq[:], start=(dc == 0), stop=(dc == DC - 1)))
                    mur = lnr[0:1, 0:SL]
                    tmp = lnr[0:1, SL:2 * SL]
                    sdr = lnr[0:1, 2 * SL:3 * SL]
                    vop(lambda e, mur=mur: e.tensor_scalar(mur, psS[0:1, :], 1.0 / D, None, OP.mult))
                    vop(lambda e, tmp=tmp: e.tensor_scalar(tmp, psQ[0:1, :], 1.0 / D, None, OP.mult))
                    vop(lambda e, mur=mur, sdr=sdr: e.tensor_tensor(sdr, mur, mur, OP.mult))
                    vop(lambda e, tmp=tmp, sdr=sdr: e.tensor_tensor(tmp, tmp, sdr, OP.subtract))
                    sop(lambda e, sdr=sdr, tmp=tmp: e.activation(sdr, tmp, AF.Sqrt, bias=ceps[:]))
                    vop(lambda e, sdr=sdr: e.reciprocal(sdr, sdr))
                    vop(lambda e, sdr=sdr, sl=sl: e.tensor_copy(rows_a[0:1, sl * SL:(sl + 1) * SL], sdr))
                    vop(lambda e, mur=mur, sdr=sdr, sl=sl: e.tensor_tensor(
                        rows_b[0:1, sl * SL:(sl + 1) * SL], mur, sdr, OP.mult))
                S.end_phase(blk)

        def ln_absmax_and_ar1(src, rows_a, rows_b):
            with ExitStack() as _st:
                amx = _st.enter_context(nc.sbuf_tensor(nm("amx"), [P, SL * 3], F32))
                amu = _st.enter_context(nc.sbuf_tensor(nm("amu"), [P, SL], F32))
                amt = _st.enter_context(nc.sbuf_tensor(nm("amt"), [P, 2], F32))
                psA = _st.enter_context(nc.psum_tensor(nm("psA"), [P, SL], F32))
                psB = _st.enter_context(nc.psum_tensor(nm("psB"), [P, SL], F32))
                psx_l = _st.enter_context(nc.psum_tensor(nm("psx_"), [1, P], F32))
                pssc_l = _st.enter_context(nc.psum_tensor(nm("pssc_"), [P, 8], F32))
                blk = _st.enter_context(nc.Block())
                H["psx"], H["pssc"] = psx_l, pssc_l
                S.start_phase()
                vop(lambda e: e.memset(lmax[:, 0:1], -3.0e38))
                vop(lambda e: e.memset(lmax[:, 1:2], 3.0e38))
                for sl in range(NSL):
                    top_(lambda e, sl=sl: e.matmul(psA[:], onesr[0:1, :], rows_a[0:1, sl * SL:(sl + 1) * SL], start=True, stop=True))
                    top_(lambda e, sl=sl: e.matmul(psB[:], onesr[0:1, :], rows_b[0:1, sl * SL:(sl + 1) * SL], start=True, stop=True))
                    for dc in range(DC):
                        xb = amx[:, (dc % 3) * SL:(dc % 3 + 1) * SL]
                        dma(xb, src[dc * P:(dc + 1) * P, sl * SL:(sl + 1) * SL])
                        vop(lambda e, xb=xb: e.tensor_tensor(amu[:], xb, psA[:], OP.mult))
                        vop(lambda e: e.tensor_tensor(amu[:], amu[:], psB[:], OP.subtract))
                        vop(lambda e: e.tensor_scalar(amu[:], amu[:], 1.0, -3.0e38, OP.mult, OP.max, accum_out=amt[:, 0:1]))
                        vop(lambda e: e.tensor_scalar(amu[:], amu[:], 1.0, 3.0e38, OP.mult, OP.min, accum_out=amt[:, 1:2]))
                        vop(lambda e: e.tensor_tensor(lmax[:, 0:1], lmax[:, 0:1], amt[:, 0:1], OP.max))
                        vop(lambda e: e.tensor_tensor(lmax[:, 1:2], lmax[:, 1:2], amt[:, 1:2], OP.min))
                vop(lambda e: e.tensor_scalar(lmax[:, 1:2], lmax[:, 1:2], -1.0, None, OP.mult))
                vop(lambda e: e.tensor_tensor(lmax[:, 0:1], lmax[:, 0:1], lmax[:, 1:2], OP.max))
                cross_part_absmax(lmax[:, 0:1], scrow[0:1, IS_T0:IS_T0 + 1])
                vop(lambda e: e.tensor_copy(ccl[0:1, 0:1], scrow[0:1, IS_T0:IS_T0 + 1]))
                allreduce_max(1)
                vop(lambda e: e.tensor_scalar(scrow[0:1, IS_S1:IS_S1 + 1], ccr[0:1, 0:1], 1.0 / QA, 1e-12, OP.mult, OP.max))
                vop(lambda e: e.reciprocal(scrow[0:1, IV_S1:IV_S1 + 1], scrow[0:1, IS_S1:IS_S1 + 1]))
                vop(lambda e: e.tensor_scalar(rows_a[:], rows_a[:], scrow[0:1, IV_S1:IV_S1 + 1], None, OP.mult))
                vop(lambda e: e.tensor_scalar(rows_b[:], rows_b[:], scrow[0:1, IV_S1:IV_S1 + 1], crnd_r[:], OP.mult, OP.subtract))
                S.end_phase(blk)

        def xq_pass(src, dst_bf, rows_a, rows_b):
            with ExitStack() as _st:
                xqx = _st.enter_context(nc.sbuf_tensor(nm("xqx"), [P, SL * 3], F32))
                xqu = _st.enter_context(nc.sbuf_tensor(nm("xqu"), [P, SL], F32))
                psA = _st.enter_context(nc.psum_tensor(nm("psA2"), [P, SL], F32))
                psB = _st.enter_context(nc.psum_tensor(nm("psB2"), [P, SL], F32))
                blk = _st.enter_context(nc.Block())
                S.start_phase()
                for sl in range(NSL):
                    top_(lambda e, sl=sl: e.matmul(psA[:], onesr[0:1, :], rows_a[0:1, sl * SL:(sl + 1) * SL], start=True, stop=True))
                    top_(lambda e, sl=sl: e.matmul(psB[:], onesr[0:1, :], rows_b[0:1, sl * SL:(sl + 1) * SL], start=True, stop=True))
                    for dc in range(DC):
                        xb = xqx[:, (dc % 3) * SL:(dc % 3 + 1) * SL]
                        dma(xb, src[dc * P:(dc + 1) * P, sl * SL:(sl + 1) * SL])
                        vop(lambda e, xb=xb: e.tensor_tensor(xqu[:], xb, psA[:], OP.mult))
                        vop(lambda e: e.tensor_tensor(xqu[:], xqu[:], psB[:], OP.subtract))
                        vop(lambda e, dc=dc, sl=sl: e.tensor_scalar(
                            dst_bf[:, dc * T + sl * SL: dc * T + (sl + 1) * SL], xqu[:], RC, None, OP.subtract))
                S.end_phase(blk)

        def xn_pass(src, dst_bf, rows_a, rows_b):
            with ExitStack() as _st:
                xnx = _st.enter_context(nc.sbuf_tensor(nm("xnx"), [P, SL * 3], F32))
                xnu = _st.enter_context(nc.sbuf_tensor(nm("xnu"), [P, SL], F32))
                psA = _st.enter_context(nc.psum_tensor(nm("psA3"), [P, SL], F32))
                psB = _st.enter_context(nc.psum_tensor(nm("psB3"), [P, SL], F32))
                blk = _st.enter_context(nc.Block())
                S.start_phase()
                for sl in range(NSL):
                    top_(lambda e, sl=sl: e.matmul(psA[:], onesr[0:1, :], rows_a[0:1, sl * SL:(sl + 1) * SL], start=True, stop=True))
                    top_(lambda e, sl=sl: e.matmul(psB[:], onesr[0:1, :], rows_b[0:1, sl * SL:(sl + 1) * SL], start=True, stop=True))
                    for dc in range(DC):
                        xb = xnx[:, (dc % 3) * SL:(dc % 3 + 1) * SL]
                        dma(xb, src[dc * P:(dc + 1) * P, sl * SL:(sl + 1) * SL])
                        vop(lambda e, xb=xb: e.tensor_tensor(xnu[:], xb, psA[:], OP.mult))
                        vop(lambda e, dc=dc, sl=sl: e.tensor_tensor(
                            dst_bf[:, dc * T + sl * SL: dc * T + (sl + 1) * SL], xnu[:], psB[:], OP.subtract))
                S.end_phase(blk)

        def ffn(inv, src):
            with nc.sbuf_tensor(f"xqib{inv}", [P, DC * T], BF16) as xqib:
                with nc.sbuf_tensor(nm("rows_a"), [1, T], F32) as rows_a, nc.sbuf_tensor(nm("rows_b"), [1, T], F32) as rows_b:
                    ln_stats(src, rows_a, rows_b)
                    ln_absmax_and_ar1(src, rows_a, rows_b)
                    xq_pass(src, xqib, rows_a, rows_b)

                with ExitStack() as _st:
                    wnat = _st.enter_context(nc.sbuf_tensor(nm("wnat"), [P, D * 2], F32))
                    wqb = _st.enter_context(nc.sbuf_tensor(nm("wqb"), [P, D], F32))
                    wpb = _st.enter_context(nc.sbuf_tensor(nm("wpb"), [P, D], BF16))
                    bm = _st.enter_context(nc.sbuf_tensor(nm("bm"), [P, 64], F32))
                    wT = _st.enter_context(nc.sbuf_tensor(nm("wT"), [P, 2 * DC * P], BF16))
                    mst = _st.enter_context(nc.sbuf_tensor(nm("mst"), [P, T], F32))
                    mtmp = _st.enter_context(nc.sbuf_tensor(nm("mtmp"), [P, 2], F32))
                    psM = _st.enter_context(nc.psum_tensor(nm("psM"), [P, SL], F32))
                    psM2 = _st.enter_context(nc.psum_tensor(nm("psM2"), [P, SL], F32))
                    psW = _st.enter_context(nc.psum_tensor(nm("psW"), [P, P], BF16))
                    psx_l = _st.enter_context(nc.psum_tensor(nm("psx4"), [1, P], F32))
                    pssc_l = _st.enter_context(nc.psum_tensor(nm("pssc4"), [P, 8], F32))
                    blk = _st.enter_context(nc.Block())
                    H["psx"], H["pssc"] = psx_l, pssc_l
                    S.start_phase()
                    for wi, (wsrc, spill, wtc) in enumerate(
                        [(wgate_h, mg, wgT_c), (wup_h, mup, wupT_c)]
                    ):
                        vop(lambda e, wi=wi: e.memset(lmax[:, 2 + 2 * wi:3 + 2 * wi], -3.0e38))
                        vop(lambda e, wi=wi: e.memset(lmax[:, 3 + 2 * wi:4 + 2 * wi], 3.0e38))
                        for fc in range(FCN):
                            wr = wT[:, (fc % 2) * DC * P:(fc % 2 + 1) * DC * P]
                            if inv == 0:
                                wn = wnat[:, (fc % 2) * D:(fc % 2 + 1) * D]
                                dma(wn, wsrc[fc * P:(fc + 1) * P, :])
                                vop(lambda e, wn=wn: e.reduce_max(
                                    bm[:, 0:32], wn.rearrange("p (b e) -> p b e", e=32),
                                    axis=AX, apply_absolute_value=True))
                                vop(lambda e: e.tensor_scalar(bm[:, 0:32], bm[:, 0:32], 1.0 / QW, 1e-12, OP.mult, OP.max))
                                vop(lambda e: e.reciprocal(bm[:, 32:64], bm[:, 0:32]))
                                vop(lambda e, wn=wn: e.tensor_tensor(
                                    wqb[:].rearrange("p (b e) -> p b e", e=32),
                                    wn.rearrange("p (b e) -> p b e", e=32),
                                    bm[:, 32:64].unsqueeze(2).broadcast_to([P, 32, 32]), OP.mult))
                                vop(lambda e: e.tensor_scalar(wqb[:], wqb[:], RC, RC, OP.add, OP.subtract))
                                vop(lambda e: e.tensor_tensor(
                                    wpb[:].rearrange("p (b e) -> p b e", e=32),
                                    wqb[:].rearrange("p (b e) -> p b e", e=32),
                                    bm[:, 0:32].unsqueeze(2).broadcast_to([P, 32, 32]), OP.mult))
                                for dc in range(DC):
                                    top_(lambda e, dc=dc: e.transpose(psW[:], wpb[:, dc * P:(dc + 1) * P], idbf[:]))
                                    sop(lambda e, dc=dc, wr=wr: e.copy(wr[:, dc * P:(dc + 1) * P], psW[:]))
                                    dma(wtc[dc * P:(dc + 1) * P, fc * P:(fc + 1) * P], wr[:, dc * P:(dc + 1) * P])
                            else:
                                for dc in range(DC):
                                    dma(wr[:, dc * P:(dc + 1) * P], wtc[dc * P:(dc + 1) * P, fc * P:(fc + 1) * P])
                            for sl in range(NSL):
                                ps = psM if sl % 2 == 0 else psM2
                                for dc in range(DC):
                                    top_(lambda e, dc=dc, sl=sl, wr=wr, ps=ps: e.matmul(
                                        ps[:], wr[:, dc * P:(dc + 1) * P],
                                        xqib[:, dc * T + sl * SL: dc * T + (sl + 1) * SL],
                                        start=(dc == 0), stop=(dc == DC - 1)))
                                vop(lambda e, sl=sl, ps=ps: e.tensor_scalar(
                                    mst[:, sl * SL:(sl + 1) * SL], ps[:], 1.0, -3.0e38, OP.mult, OP.max, accum_out=mtmp[:, 0:1]))
                                vop(lambda e, sl=sl, ps=ps: e.tensor_scalar(
                                    mst[:, sl * SL:(sl + 1) * SL], ps[:], 1.0, 3.0e38, OP.mult, OP.min, accum_out=mtmp[:, 1:2]))
                                vop(lambda e, wi=wi: e.tensor_tensor(lmax[:, 2 + 2 * wi:3 + 2 * wi], lmax[:, 2 + 2 * wi:3 + 2 * wi], mtmp[:, 0:1], OP.max))
                                vop(lambda e, wi=wi: e.tensor_tensor(lmax[:, 3 + 2 * wi:4 + 2 * wi], lmax[:, 3 + 2 * wi:4 + 2 * wi], mtmp[:, 1:2], OP.min))
                            dma(spill[fc * P:(fc + 1) * P, :], mst[:])
                        vop(lambda e, wi=wi: e.tensor_scalar(lmax[:, 3 + 2 * wi:4 + 2 * wi], lmax[:, 3 + 2 * wi:4 + 2 * wi], -1.0, None, OP.mult))
                        vop(lambda e, wi=wi: e.tensor_tensor(lmax[:, 2 + 2 * wi:3 + 2 * wi], lmax[:, 2 + 2 * wi:3 + 2 * wi], lmax[:, 3 + 2 * wi:4 + 2 * wi], OP.max))
                        cross_part_absmax(lmax[:, 2 + 2 * wi:3 + 2 * wi], scrow[0:1, IS_T0 + wi:IS_T0 + wi + 1])
                        vop(lambda e, wi=wi: e.tensor_copy(ccl[0:1, wi:wi + 1], scrow[0:1, IS_T0 + wi:IS_T0 + wi + 1]))

                    if inv == 0:
                        with ExitStack() as _st:
                            wd3 = _st.enter_context(nc.sbuf_tensor(nm("wd3"), [P, FF], F32))
                            wdq = _st.enter_context(nc.sbuf_tensor(nm("wdq"), [P, FF], F32))
                            wdb = _st.enter_context(nc.sbuf_tensor(nm("wdb"), [P, FF], BF16))
                            bm3 = _st.enter_context(nc.sbuf_tensor(nm("bm3"), [P, 192], F32))
                            wds = _st.enter_context(nc.sbuf_tensor(nm("wds"), [P, P], BF16))
                            for r in range(DC):
                                dma(wd3[:], wdown_h[r * P:(r + 1) * P, :])
                                vop(lambda e: e.reduce_max(
                                    bm3[:, 0:96], wd3[:].rearrange("p (b e) -> p b e", e=32),
                                    axis=AX, apply_absolute_value=True))
                                vop(lambda e: e.tensor_scalar(bm3[:, 0:96], bm3[:, 0:96], 1.0 / QW, 1e-12, OP.mult, OP.max))
                                vop(lambda e: e.reciprocal(bm3[:, 96:192], bm3[:, 0:96]))
                                vop(lambda e: e.tensor_tensor(
                                    wdq[:].rearrange("p (b e) -> p b e", e=32),
                                    wd3[:].rearrange("p (b e) -> p b e", e=32),
                                    bm3[:, 96:192].unsqueeze(2).broadcast_to([P, 96, 32]), OP.mult))
                                vop(lambda e: e.tensor_scalar(wdq[:], wdq[:], RC, RC, OP.add, OP.subtract))
                                vop(lambda e: e.tensor_tensor(
                                    wdb[:].rearrange("p (b e) -> p b e", e=32),
                                    wdq[:].rearrange("p (b e) -> p b e", e=32),
                                    bm3[:, 0:96].unsqueeze(2).broadcast_to([P, 96, 32]), OP.mult))
                                for fc in range(FCN):
                                    top_(lambda e, fc=fc: e.transpose(psW[:], wdb[:, fc * P:(fc + 1) * P], idbf[:]))
                                    sop(lambda e: e.copy(wds[:], psW[:]))
                                    dma(wdT_c[fc * P:(fc + 1) * P, r * P:(r + 1) * P], wds[:])

                    allreduce_max(2)
                    vop(lambda e: e.tensor_scalar(ccr[0:1, 0:2], ccr[0:1, 0:2], scrow[0:1, IS_S1:IS_S1 + 1], None, OP.mult))
                    vop(lambda e: e.tensor_scalar(scrow[0:1, IS_S3:IS_S3 + 1], ccr[0:1, 0:1], 1.0 / QA, 1e-12, OP.mult, OP.max))
                    vop(lambda e: e.tensor_scalar(scrow[0:1, IS_S2:IS_S2 + 1], ccr[0:1, 1:2], 1.0 / QA, 1e-12, OP.mult, OP.max))
                    vop(lambda e: e.reciprocal(scrow[0:1, IS_T1:IS_T1 + 1], scrow[0:1, IS_S3:IS_S3 + 1]))
                    vop(lambda e: e.tensor_scalar(scrow[0:1, IS_T1:IS_T1 + 1], scrow[0:1, IS_T1:IS_T1 + 1], scrow[0:1, IS_S1:IS_S1 + 1], None, OP.mult))
                    bcast128(scrow[0:1, IS_T1:IS_T1 + 1], SC_K1)
                    vop(lambda e: e.reciprocal(scrow[0:1, IS_T2:IS_T2 + 1], scrow[0:1, IS_S2:IS_S2 + 1]))
                    vop(lambda e: e.tensor_scalar(scrow[0:1, IS_T2:IS_T2 + 1], scrow[0:1, IS_T2:IS_T2 + 1], scrow[0:1, IS_S1:IS_S1 + 1], None, OP.mult))
                    bcast128(scrow[0:1, IS_T2:IS_T2 + 1], SC_K2)
                    bcast128(scrow[0:1, IS_S3:IS_S3 + 1], SC_S3)
                    S.end_phase(blk)

                with ExitStack() as _st:
                    e1m = _st.enter_context(nc.sbuf_tensor(nm("e1m"), [P, T], F32))
                    e1a = _st.enter_context(nc.sbuf_tensor(nm("e1a"), [P, T], F32))
                    e1b = _st.enter_context(nc.sbuf_tensor(nm("e1b"), [P, T], F32))
                    e1c = _st.enter_context(nc.sbuf_tensor(nm("e1c"), [P, T], F32))
                    psx_l = _st.enter_context(nc.psum_tensor(nm("psx5"), [1, P], F32))
                    pssc_l = _st.enter_context(nc.psum_tensor(nm("pssc5"), [P, 8], F32))
                    blk = _st.enter_context(nc.Block())
                    H["psx"], H["pssc"] = psx_l, pssc_l
                    S.start_phase()
                    vop(lambda e: e.memset(lmax[:, 6:7], 0.0))
                    for fc in range(FCN):
                        dma(e1m[:], mg[fc * P:(fc + 1) * P, :])
                        sop(lambda e: e.activation(e1a[:], e1m[:], AF.Identity, bias=crnd[:], scale=sc[:, SC_K1:SC_K1 + 1]))
                        vop(lambda e: e.tensor_scalar(e1a[:], e1a[:], RC, None, OP.subtract))
                        sop(lambda e: e.activation(e1b[:], e1a[:], AF.Sigmoid, bias=czero[:], scale=sc[:, SC_S3:SC_S3 + 1]))
                        sop(lambda e: e.activation(e1b[:], e1b[:], AF.Identity, bias=crnd[:], scale=c65k[:]))
                        vop(lambda e: e.tensor_scalar(e1b[:], e1b[:], RC, 65535.0, OP.subtract, OP.min))
                        vop(lambda e: e.tensor_tensor(e1c[:], e1a[:], e1b[:], OP.mult))
                        vop(lambda e: e.reduce_max(lmax[:, 7:8], e1c[:], axis=AX, apply_absolute_value=True))
                        vop(lambda e: e.tensor_tensor(lmax[:, 6:7], lmax[:, 6:7], lmax[:, 7:8], OP.max))
                        dma(ci[fc * P:(fc + 1) * P, :], e1c[:])
                    cross_part_absmax(lmax[:, 6:7], scrow[0:1, IS_T0:IS_T0 + 1])
                    vop(lambda e: e.tensor_copy(ccl[0:1, 0:1], scrow[0:1, IS_T0:IS_T0 + 1]))
                    allreduce_max(1)
                    vop(lambda e: e.tensor_scalar(scrow[0:1, IS_T1:IS_T1 + 1], scrow[0:1, IS_S3:IS_S3 + 1], 1.0 / 65536.0, None, OP.mult))
                    vop(lambda e: e.tensor_scalar(scrow[0:1, IS_T2:IS_T2 + 1], ccr[0:1, 0:1], scrow[0:1, IS_T1:IS_T1 + 1], None, OP.mult))
                    vop(lambda e: e.tensor_scalar(scrow[0:1, IS_S4:IS_S4 + 1], scrow[0:1, IS_T2:IS_T2 + 1], 1.0 / QA, 1e-12, OP.mult, OP.max))
                    vop(lambda e: e.reciprocal(scrow[0:1, IS_T2:IS_T2 + 1], scrow[0:1, IS_S4:IS_S4 + 1]))
                    vop(lambda e: e.tensor_scalar(scrow[0:1, IS_T2:IS_T2 + 1], scrow[0:1, IS_T2:IS_T2 + 1], scrow[0:1, IS_T1:IS_T1 + 1], None, OP.mult))
                    bcast128(scrow[0:1, IS_T2:IS_T2 + 1], SC_K3)
                    S.end_phase(blk)

                with ExitStack() as _st:
                    e2a = _st.enter_context(nc.sbuf_tensor(nm("e2a"), [P, T], F32))
                    e2b = _st.enter_context(nc.sbuf_tensor(nm("e2b"), [P, T], F32))
                    e2c = _st.enter_context(nc.sbuf_tensor(nm("e2c"), [P, T], F32))
                    e2i = _st.enter_context(nc.sbuf_tensor(nm("e2i"), [P, T], I16))
                    e2j = _st.enter_context(nc.sbuf_tensor(nm("e2j"), [P, T], I16))
                    psx_l = _st.enter_context(nc.psum_tensor(nm("psx6"), [1, P], F32))
                    pssc_l = _st.enter_context(nc.psum_tensor(nm("pssc6"), [P, 8], F32))
                    blk = _st.enter_context(nc.Block())
                    H["psx"], H["pssc"] = psx_l, pssc_l
                    S.start_phase()
                    vop(lambda e: e.memset(lmax[:, 6:7], 0.0))
                    for fc in range(FCN):
                        dma(e2a[:], ci[fc * P:(fc + 1) * P, :])
                        sop(lambda e: e.activation(e2a[:], e2a[:], AF.Identity, bias=crnd[:], scale=sc[:, SC_K3:SC_K3 + 1]))
                        vop(lambda e: e.tensor_scalar(e2a[:], e2a[:], RC, None, OP.subtract))
                        vop(lambda e: e.tensor_copy(e2i[:], e2a[:]))
                        dma(g2s[fc * P:(fc + 1) * P, :], e2i[:])
                        dma(e2b[:], mup[fc * P:(fc + 1) * P, :])
                        sop(lambda e: e.activation(e2b[:], e2b[:], AF.Identity, bias=crnd[:], scale=sc[:, SC_K2:SC_K2 + 1]))
                        vop(lambda e: e.tensor_scalar(e2b[:], e2b[:], RC, None, OP.subtract))
                        vop(lambda e: e.tensor_copy(e2j[:], e2b[:]))
                        dma(ups[fc * P:(fc + 1) * P, :], e2j[:])
                        vop(lambda e: e.tensor_tensor(e2c[:], e2a[:], e2b[:], OP.mult))
                        vop(lambda e: e.reduce_max(lmax[:, 7:8], e2c[:], axis=AX, apply_absolute_value=True))
                        vop(lambda e: e.tensor_tensor(lmax[:, 6:7], lmax[:, 6:7], lmax[:, 7:8], OP.max))
                    cross_part_absmax(lmax[:, 6:7], scrow[0:1, IS_T0:IS_T0 + 1])
                    vop(lambda e: e.tensor_copy(ccl[0:1, 0:1], scrow[0:1, IS_T0:IS_T0 + 1]))
                    allreduce_max(1)
                    vop(lambda e: e.tensor_scalar(scrow[0:1, IS_T1:IS_T1 + 1], scrow[0:1, IS_S4:IS_S4 + 1], scrow[0:1, IS_S2:IS_S2 + 1], None, OP.mult))
                    vop(lambda e: e.tensor_scalar(scrow[0:1, IS_T2:IS_T2 + 1], ccr[0:1, 0:1], scrow[0:1, IS_T1:IS_T1 + 1], None, OP.mult))
                    vop(lambda e: e.tensor_scalar(scrow[0:1, IS_S5:IS_S5 + 1], scrow[0:1, IS_T2:IS_T2 + 1], 1.0 / QA, 1e-12, OP.mult, OP.max))
                    vop(lambda e: e.reciprocal(scrow[0:1, IS_T2:IS_T2 + 1], scrow[0:1, IS_S5:IS_S5 + 1]))
                    vop(lambda e: e.tensor_scalar(scrow[0:1, IS_T2:IS_T2 + 1], scrow[0:1, IS_T2:IS_T2 + 1], scrow[0:1, IS_T1:IS_T1 + 1], None, OP.mult))
                    bcast128(scrow[0:1, IS_T2:IS_T2 + 1], SC_K4)
                    vop(lambda e: e.tensor_scalar(scrow[0:1, IS_T1:IS_T1 + 1], scrow[0:1, IS_S5:IS_S5 + 1], 0.5, None, OP.mult))
                    bcast128(scrow[0:1, IS_T1:IS_T1 + 1], SC_BETA)
                    S.end_phase(blk)

                with ExitStack() as _st:
                    wdTs = _st.enter_context(nc.sbuf_tensor(nm("wdTs"), [P, FCN * D], BF16))
                    oib = _st.enter_context(nc.sbuf_tensor(nm("oib"), [P, FCN * SL], BF16))
                    e3g = _st.enter_context(nc.sbuf_tensor(nm("e3g"), [P, SL], I16))
                    e3u = _st.enter_context(nc.sbuf_tensor(nm("e3u"), [P, SL], I16))
                    e3d = _st.enter_context(nc.sbuf_tensor(nm("e3d"), [P, SL], F32))
                    e3o = _st.enter_context(nc.sbuf_tensor(nm("e3o"), [P, SL], F32))
                    e3x = _st.enter_context(nc.sbuf_tensor(nm("e3x"), [P, SL], F32))
                    e3s = _st.enter_context(nc.sbuf_tensor(nm("e3s"), [P, SL], F32))
                    psD = _st.enter_context(nc.psum_tensor(nm("psD"), [P, SL], F32))
                    psD2 = _st.enter_context(nc.psum_tensor(nm("psD2"), [P, SL], F32))
                    blk = _st.enter_context(nc.Block())
                    S.start_phase()
                    for fc in range(FCN):
                        dma(wdTs[:, fc * D:(fc + 1) * D], wdT_c[fc * P:(fc + 1) * P, :])
                    for sl in range(NSL):
                        for fc in range(FCN):
                            dma(e3g[:], g2s[fc * P:(fc + 1) * P, sl * SL:(sl + 1) * SL])
                            dma(e3u[:], ups[fc * P:(fc + 1) * P, sl * SL:(sl + 1) * SL])
                            vop(lambda e: e.tensor_tensor(e3d[:], e3g[:], e3u[:], OP.mult))
                            sop(lambda e: e.activation(e3o[:], e3d[:], AF.Identity, bias=crnd[:], scale=sc[:, SC_K4:SC_K4 + 1]))
                            vop(lambda e, fc=fc: e.tensor_scalar(
                                oib[:, fc * SL:(fc + 1) * SL], e3o[:], RC, None, OP.subtract))
                        for dc in range(DC):
                            ps = psD if dc % 2 == 0 else psD2
                            for fc in range(FCN):
                                top_(lambda e, fc=fc, dc=dc, ps=ps: e.matmul(
                                    ps[:], wdTs[:, fc * D + dc * P: fc * D + (dc + 1) * P],
                                    oib[:, fc * SL:(fc + 1) * SL],
                                    start=(fc == 0), stop=(fc == FCN - 1)))
                            dma(e3x[:], src[dc * P:(dc + 1) * P, sl * SL:(sl + 1) * SL])
                            sop(lambda e, ps=ps: e.activation(e3s[:], ps[:], AF.Identity, bias=czero[:], scale=sc[:, SC_BETA:SC_BETA + 1]))
                            vop(lambda e: e.tensor_tensor(e3s[:], e3s[:], e3x[:], OP.add))
                            dma(src[dc * P:(dc + 1) * P, sl * SL:(sl + 1) * SL], e3s[:])
                    S.end_phase(blk)

        def attn(src):
            with ExitStack() as _st:
                xn2b = _st.enter_context(nc.sbuf_tensor(nm("xn2b"), [P, DC * T], BF16))
                kvb = _st.enter_context(nc.sbuf_tensor(nm("kvb"), [P, 16 * P], BF16))
                kdb = _st.enter_context(nc.sbuf_tensor(nm("kdb"), [P, 32], BF16))
                with nc.sbuf_tensor(nm("rows_a"), [1, T], F32) as rows_a, nc.sbuf_tensor(nm("rows_b"), [1, T], F32) as rows_b:
                    ln_stats(src, rows_a, rows_b)
                    xn_pass(src, xn2b, rows_a, rows_b)

                with ExitStack() as _st:
                    wkb = _st.enter_context(nc.sbuf_tensor(nm("wkb"), [P, DC * D], BF16))
                    wvb = _st.enter_context(nc.sbuf_tensor(nm("wvb"), [P, DC * D], BF16))
                    wldf = _st.enter_context(nc.sbuf_tensor(nm("wldf"), [P, D], F32))
                    bkb = _st.enter_context(nc.sbuf_tensor(nm("bkb"), [P, D], F32))
                    bvb = _st.enter_context(nc.sbuf_tensor(nm("bvb"), [P, D], F32))
                    ktf = _st.enter_context(nc.sbuf_tensor(nm("ktf"), [P, SL], F32))
                    kte = _st.enter_context(nc.sbuf_tensor(nm("kte"), [P, SL], F32))
                    ktb = _st.enter_context(nc.sbuf_tensor(nm("ktb"), [P, D], BF16))
                    vtb = _st.enter_context(nc.sbuf_tensor(nm("vtb"), [P, D], BF16))
                    psK = _st.enter_context(nc.psum_tensor(nm("psK"), [P, SL], F32))
                    psV = _st.enter_context(nc.psum_tensor(nm("psV"), [P, SL], F32))
                    psKV0 = _st.enter_context(nc.psum_tensor(nm("psKV0"), [P, SL], F32))
                    psKV1 = _st.enter_context(nc.psum_tensor(nm("psKV1"), [P, SL], F32))
                    psKV2 = _st.enter_context(nc.psum_tensor(nm("psKV2"), [P, SL], F32))
                    psKV3 = _st.enter_context(nc.psum_tensor(nm("psKV3"), [P, SL], F32))
                    psBC = _st.enter_context(nc.psum_tensor(nm("psBC"), [P, SL], F32))
                    blk = _st.enter_context(nc.Block())
                    psKV = [psKV0, psKV1, psKV2, psKV3]
                    S.start_phase()
                    for dk in range(DC):
                        dma(wldf[:], wk_h[dk * P:(dk + 1) * P, :])
                        vop(lambda e, dk=dk: e.tensor_copy(wkb[:, dk * D:(dk + 1) * D], wldf[:]))
                        dma(wldf[:], wv_h[dk * P:(dk + 1) * P, :])
                        vop(lambda e, dk=dk: e.tensor_copy(wvb[:, dk * D:(dk + 1) * D], wldf[:]))
                    for h in range(2):
                        top_(lambda e, h=h: e.matmul(psBC[:], onesr[0:1, :], bkr[0:1, h * SL:(h + 1) * SL], start=True, stop=True))
                        sop(lambda e, h=h: e.copy(bkb[:, h * SL:(h + 1) * SL], psBC[:]))
                        top_(lambda e, h=h: e.matmul(psBC[:], onesr[0:1, :], bvr[0:1, h * SL:(h + 1) * SL], start=True, stop=True))
                        sop(lambda e, h=h: e.copy(bvb[:, h * SL:(h + 1) * SL], psBC[:]))
                    for tc in range(TCN):
                        sq, ti = tc // 16, tc % 16
                        for half in range(2):
                            for dk in range(DC):
                                top_(lambda e, dk=dk, tc=tc, half=half: e.matmul(
                                    psK[:], xn2b[:, dk * T + tc * P:dk * T + (tc + 1) * P],
                                    wkb[:, dk * D + half * SL: dk * D + (half + 1) * SL],
                                    start=(dk == 0), stop=(dk == DC - 1)))
                            vop(lambda e, half=half: e.tensor_tensor(ktf[:], psK[:], bkb[:, half * SL:(half + 1) * SL], OP.add))
                            vop(lambda e: e.tensor_scalar(kte[:], ktf[:], 0.0, None, OP.min))
                            sop(lambda e: e.activation(kte[:], kte[:], AF.Exp))
                            vop(lambda e, half=half: e.scalar_tensor_tensor(
                                ktb[:, half * SL:(half + 1) * SL], ktf[:], 0.0, kte[:], OP.max, OP.add))
                            for dk in range(DC):
                                top_(lambda e, dk=dk, tc=tc, half=half: e.matmul(
                                    psV[:], xn2b[:, dk * T + tc * P:dk * T + (tc + 1) * P],
                                    wvb[:, dk * D + half * SL: dk * D + (half + 1) * SL],
                                    start=(dk == 0), stop=(dk == DC - 1)))
                            vop(lambda e, half=half: e.tensor_tensor(
                                vtb[:, half * SL:(half + 1) * SL], psV[:], bvb[:, half * SL:(half + 1) * SL], OP.add))
                        for p in range(8):
                            ps = psKV[p // 2]
                            off = (p % 2) * 256
                            top_(lambda e, p=p, ps=ps, off=off, ti=ti: e.matmul(
                                ps[:, off:off + P], ktb[:, p * P:(p + 1) * P], vtb[:, p * P:(p + 1) * P],
                                start=(ti == 0), stop=(ti == 15), skip_group_check=True))
                            top_(lambda e, p=p, ps=ps, off=off, ti=ti: e.matmul(
                                ps[:, off + P:off + P + 1], ktb[:, p * P:(p + 1) * P], onescb[:],
                                start=(ti == 0), stop=(ti == 15), skip_group_check=True))
                        if ti == 15:
                            for p in range(8):
                                ps = psKV[p // 2]
                                off = (p % 2) * 256
                                kvt = kvb[:, (sq * 8 + p) * P:(sq * 8 + p + 1) * P]
                                vop(lambda e, kvt=kvt: e.memset(kvt, 0.0))
                                vop(lambda e, ps=ps, off=off, kvt=kvt: e.tensor_copy(kvt[0:64, 0:64], ps[0:64, off:off + 64]))
                                vop(lambda e, ps=ps, off=off, kvt=kvt: e.tensor_copy(kvt[64:128, 64:128], ps[64:128, off + 64:off + 128]))
                                kdt = kdb[:, (sq * 8 + p) * 2:(sq * 8 + p) * 2 + 2]
                                vop(lambda e, kdt=kdt: e.memset(kdt, 0.0))
                                vop(lambda e, ps=ps, off=off, kdt=kdt: e.tensor_copy(kdt[0:64, 0:1], ps[0:64, off + P:off + P + 1]))
                                vop(lambda e, ps=ps, off=off, kdt=kdt: e.tensor_copy(kdt[64:128, 1:2], ps[64:128, off + P:off + P + 1]))
                    S.end_phase(blk)

                with ExitStack() as _st:
                    wqsb = _st.enter_context(nc.sbuf_tensor(nm("wqsb"), [P, DC * D], BF16))
                    wgsb = _st.enter_context(nc.sbuf_tensor(nm("wgsb"), [P, DC * D], BF16))
                    wosb = _st.enter_context(nc.sbuf_tensor(nm("wosb"), [P, DC * D], BF16))
                    wldf2 = _st.enter_context(nc.sbuf_tensor(nm("wldf2"), [P, D], F32))
                    qrf = _st.enter_context(nc.sbuf_tensor(nm("qrf"), [P, SL], F32))
                    qef = _st.enter_context(nc.sbuf_tensor(nm("qef"), [P, SL], F32))
                    qbf = _st.enter_context(nc.sbuf_tensor(nm("qbf"), [P, DC * SL], BF16))
                    zsb = _st.enter_context(nc.sbuf_tensor(nm("zsb"), [2, SL], F32))
                    zbs = _st.enter_context(nc.sbuf_tensor(nm("zbs"), [P, SL], F32))
                    ssb = _st.enter_context(nc.sbuf_tensor(nm("ssb"), [P, SL], F32))
                    gsb = _st.enter_context(nc.sbuf_tensor(nm("gsb"), [P, DC * SL], F32))
                    abf = _st.enter_context(nc.sbuf_tensor(nm("abf"), [P, DC * SL], BF16))
                    aor = _st.enter_context(nc.sbuf_tensor(nm("aor"), [P, SL], F32))
                    axr = _st.enter_context(nc.sbuf_tensor(nm("axr"), [P, SL], F32))
                    psQ7 = _st.enter_context(nc.psum_tensor(nm("psQ7"), [P, SL], F32))
                    psQ8 = _st.enter_context(nc.psum_tensor(nm("psQ8"), [P, SL], F32))
                    psZr = _st.enter_context(nc.psum_tensor(nm("psZr"), [2, SL], F32))
                    psZb = _st.enter_context(nc.psum_tensor(nm("psZb"), [P, SL], F32))
                    psS7 = _st.enter_context(nc.psum_tensor(nm("psS7"), [P, SL], F32))
                    psO7 = _st.enter_context(nc.psum_tensor(nm("psO7"), [P, SL], F32))
                    psO8 = _st.enter_context(nc.psum_tensor(nm("psO8"), [P, SL], F32))
                    blk2 = _st.enter_context(nc.Block())
                    S.start_phase()
                    for dk in range(DC):
                        dma(wldf2[:], wq_h[dk * P:(dk + 1) * P, :])
                        vop(lambda e, dk=dk: e.tensor_copy(wqsb[:, dk * D:(dk + 1) * D], wldf2[:]))
                        dma(wldf2[:], wg_h[dk * P:(dk + 1) * P, :])
                        vop(lambda e, dk=dk: e.tensor_copy(wgsb[:, dk * D:(dk + 1) * D], wldf2[:]))
                        dma(wldf2[:], wo_h[dk * P:(dk + 1) * P, :])
                        vop(lambda e, dk=dk: e.tensor_copy(wosb[:, dk * D:(dk + 1) * D], wldf2[:]))
                    for gsl in range(NSL):
                        sq = gsl // 4
                        for qc in range(DC):
                            ps = psQ7 if qc % 2 == 0 else psQ8
                            for dk in range(DC):
                                top_(lambda e, dk=dk, qc=qc, gsl=gsl, ps=ps: e.matmul(
                                    ps[:], wqsb[:, dk * D + qc * P: dk * D + (qc + 1) * P],
                                    xn2b[:, dk * T + gsl * SL: dk * T + (gsl + 1) * SL],
                                    start=(dk == 0), stop=(dk == DC - 1)))
                            sop(lambda e, qc=qc, ps=ps: e.activation(qrf[:], ps[:], AF.Identity, bias=bq_sb[:, qc:qc + 1]))
                            vop(lambda e: e.tensor_scalar(qef[:], qrf[:], 0.0, None, OP.min))
                            sop(lambda e: e.activation(qef[:], qef[:], AF.Exp))
                            vop(lambda e, qc=qc: e.scalar_tensor_tensor(
                                qbf[:, qc * SL:(qc + 1) * SL], qrf[:], 0.0, qef[:], OP.max, OP.add))
                        for gc in range(DC):
                            ps = psQ7 if gc % 2 == 0 else psQ8
                            for dk in range(DC):
                                top_(lambda e, dk=dk, gc=gc, gsl=gsl, ps=ps: e.matmul(
                                    ps[:], wgsb[:, dk * D + gc * P: dk * D + (gc + 1) * P],
                                    xn2b[:, dk * T + gsl * SL: dk * T + (gsl + 1) * SL],
                                    start=(dk == 0), stop=(dk == DC - 1)))
                            sop(lambda e, gc=gc, ps=ps: e.activation(
                                gsb[:, gc * SL:(gc + 1) * SL], ps[:], AF.Sigmoid, bias=bg_sb[:, gc:gc + 1]))
                        for pp in range(8):
                            kdt = kdb[:, (sq * 8 + pp) * 2:(sq * 8 + pp) * 2 + 2]
                            top_(lambda e, kdt=kdt, pp=pp: e.matmul(
                                psZr[:], kdt, qbf[:, pp * SL:(pp + 1) * SL], start=True, stop=True))
                            vop(lambda e: e.tensor_scalar(zsb[:], psZr[:], 1e-6, None, OP.add))
                            vop(lambda e: e.reciprocal(zsb[:], zsb[:]))
                            top_(lambda e: e.matmul(psZb[:], patt2[:], zsb[:], start=True, stop=True))
                            vop(lambda e: e.tensor_copy(zbs[:], psZb[:]))
                            kvt = kvb[:, (sq * 8 + pp) * P:(sq * 8 + pp + 1) * P]
                            top_(lambda e, kvt=kvt, pp=pp: e.matmul(
                                psS7[:], kvt, qbf[:, pp * SL:(pp + 1) * SL], start=True, stop=True))
                            vop(lambda e: e.tensor_tensor(ssb[:], psS7[:], zbs[:], OP.mult))
                            vop(lambda e, pp=pp: e.tensor_tensor(
                                abf[:, pp * SL:(pp + 1) * SL], ssb[:], gsb[:, pp * SL:(pp + 1) * SL], OP.mult))
                        for oc in range(DC):
                            ps = psO7 if oc % 2 == 0 else psO8
                            for dk in range(DC):
                                top_(lambda e, dk=dk, oc=oc, ps=ps: e.matmul(
                                    ps[:], wosb[:, dk * D + oc * P: dk * D + (oc + 1) * P],
                                    abf[:, dk * SL:(dk + 1) * SL],
                                    start=(dk == 0), stop=(dk == DC - 1)))
                            sop(lambda e, oc=oc, ps=ps: e.activation(aor[:], ps[:], AF.Identity, bias=bo_sb[:, oc:oc + 1]))
                            dma(axr[:], src[oc * P:(oc + 1) * P, gsl * SL:(gsl + 1) * SL])
                            vop(lambda e: e.tensor_tensor(aor[:], aor[:], axr[:], OP.add))
                            dma(src[oc * P:(oc + 1) * P, gsl * SL:(gsl + 1) * SL], aor[:])
                    S.end_phase(blk2)

        ffn(0, xfm)
        if debug:
            with nc.Block() as blk:
                S.start_phase()
                with nc.sbuf_tensor(nm("dbt"), [P, T], F32) as dbt:
                    for dc in range(DC):
                        dma(dbt[:], xfm[dc * P:(dc + 1) * P, :])
                        dma(dbg[0][dc * P:(dc + 1) * P, :], dbt[:])
                    dma(dbgs[0:1, 0:32], scrow[0:1, 0:32])
                S.end_phase(blk)
        attn(xfm)
        if debug:
            with nc.Block() as blk:
                S.start_phase()
                with nc.sbuf_tensor(nm("dbt2"), [P, T], F32) as dbt:
                    for dc in range(DC):
                        dma(dbt[:], xfm[dc * P:(dc + 1) * P, :])
                        dma(dbg[1][dc * P:(dc + 1) * P, :], dbt[:])
                S.end_phase(blk)
        ffn(1, xfm)
        if debug:
            with nc.Block() as blk:
                S.start_phase()
                with nc.sbuf_tensor(nm("dbt3"), [P, T], F32) as dbt:
                    for dc in range(DC):
                        dma(dbt[:], xfm[dc * P:(dc + 1) * P, :])
                        dma(dbg[2][dc * P:(dc + 1) * P, :], dbt[:])
                S.end_phase(blk)

        with ExitStack() as _st:
            ost = _st.enter_context(nc.sbuf_tensor(nm("ost"), [P, DC * 1024], F32))
            oyt = _st.enter_context(nc.sbuf_tensor(nm("oyt"), [P, D], F32))
            pst_o = _st.enter_context(nc.psum_tensor(nm("pst_o"), [P, P], F32))
            blk = _st.enter_context(nc.Block())
            S.start_phase()
            for tg in range(TCN // 8):
                base = tg * 1024
                for dc in range(DC):
                    dma(ost[:, dc * 1024:(dc + 1) * 1024], xfm[dc * P:(dc + 1) * P, base:base + 1024])
                for g in range(8):
                    tc = tg * 8 + g
                    for dc in range(DC):
                        top_(lambda e, dc=dc, g=g: e.transpose(
                            pst_o[:], ost[:, dc * 1024 + g * P: dc * 1024 + (g + 1) * P], id32[:]))
                        sop(lambda e, dc=dc: e.copy(oyt[:, dc * P:(dc + 1) * P], pst_o[:]))
                    dma(y_out[tc * P:(tc + 1) * P, :], oyt[:])
            S.wait_all("sync")
            S.end_phase(blk)

    return nc


_NC_CACHE = {}


def _get_nc(debug=False):
    key = bool(debug)
    if key not in _NC_CACHE:
        _NC_CACHE[key] = build(debug=debug)
    return _NC_CACHE[key]


def _in_maps(inputs, ncores=8):
    f32 = np.float32
    x = np.ascontiguousarray(np.asarray(inputs["L0_attn_x"], dtype=f32))
    base = {
        "wq": np.ascontiguousarray(np.asarray(inputs["L0_attn_wq_w"], f32)),
        "wk": np.ascontiguousarray(np.asarray(inputs["L0_attn_wk_w"], f32)),
        "wv": np.ascontiguousarray(np.asarray(inputs["L0_attn_wv_w"], f32)),
        "wg": np.ascontiguousarray(np.asarray(inputs["L0_attn_wg_w"], f32)),
        "wo": np.ascontiguousarray(np.asarray(inputs["L0_attn_out_w"], f32)),
        "bq": np.ascontiguousarray(np.asarray(inputs["L0_attn_wq_b"], f32).reshape(D, 1)),
        "bk": np.ascontiguousarray(np.asarray(inputs["L0_attn_wk_b"], f32).reshape(1, D)),
        "bv": np.ascontiguousarray(np.asarray(inputs["L0_attn_wv_b"], f32).reshape(1, D)),
        "bg": np.ascontiguousarray(np.asarray(inputs["L0_attn_wg_b"], f32).reshape(D, 1)),
        "bo": np.ascontiguousarray(np.asarray(inputs["L0_attn_out_b"], f32).reshape(D, 1)),
        "wgate": np.ascontiguousarray(np.asarray(inputs["L0_ffn_w_gate"], f32)),
        "wup": np.ascontiguousarray(np.asarray(inputs["L0_ffn_w_up"], f32)),
        "wdown": np.ascontiguousarray(np.asarray(inputs["L0_ffn_w_down"], f32)),
        "id32": np.eye(P, dtype=f32),
        "idbf": np.eye(P).astype(ml_dtypes.bfloat16),
        "onesc": np.ones((P, 1), f32),
        "onesr": np.ones((1, P), f32),
        "onescb": np.ones((P, 1), ml_dtypes.bfloat16),
        "patt2": np.concatenate(
            [np.concatenate([np.ones((1, 64), f32), np.zeros((1, 64), f32)], axis=1),
             np.concatenate([np.zeros((1, 64), f32), np.ones((1, 64), f32)], axis=1)], axis=0),
    }
    maps = []
    for c in range(ncores):
        m = dict(base)
        m["x"] = np.ascontiguousarray(x[2 * c:2 * c + 2].reshape(T, D))
        maps.append(m)
    return maps


def kernel(**inputs):
    nc = _get_nc(debug=False)
    maps = _in_maps(inputs)
    res = run_bass_kernel_spmd(nc, maps, core_ids=list(range(8)))
    out = np.concatenate(
        [np.asarray(res.results[c]["y"]).reshape(2, 2048, D) for c in range(8)], axis=0
    )
    return out


# version 04
# speedup vs baseline: 1.3251x; 1.3251x over previous
import numpy as np
import ml_dtypes
from contextlib import ExitStack

import concourse.bass as bass
import concourse.mybir as mybir
from concourse.alu_op_type import AluOpType as OP
from concourse.bass_utils import run_bass_kernel_spmd

F32 = mybir.dt.float32
BF16 = mybir.dt.bfloat16
I16 = mybir.dt.int16
AF = mybir.ActivationFunctionType
AX = mybir.AxisListType.X

P = 128
T = 4096
D = 1024
FF = 3072
DC = D // P
FCN = FF // P
SL = 512
NSL = T // SL
TCN = T // P
RC = 1.5 * 2.0 ** 23
QA = 32767.0
QW = 7.0
ENGS = ("sync", "vector", "scalar", "tensor", "gpsimd")


class Ser:

    def __init__(self, nc, chain):
        self.nc = nc
        self.chain = chain
        self.c = 0
        self.gap_c = 0
        self.auto = True
        self.last_eng = None
        self.q = None

    def start_phase(self):
        self.q = {e: [] for e in ENGS}
        self.gap()

    def do(self, ekey, fn, inc=1):
        import os
        mode = os.environ.get("KSER", "runs")
        allow = {"full": (), "novec": ("sync", "tensor"), "notens": ("sync", "vector", "scalar", "gpsimd"),
                 "nosync": ("tensor", "vector", "scalar", "gpsimd"),
                 "vec": ("sync", "tensor", "vector"), "sca": ("sync", "tensor", "scalar"),
                 "runs": ("sync", "tensor", "scalar")}[mode]
        if self.auto and (ekey != self.last_eng or ekey not in allow):
            self.gap_c = self.c
        self.last_eng = ekey
        tgt = self.gap_c

        def emit(eng, fn=fn, tgt=tgt, inc=inc):
            if tgt > 0:
                eng.wait_ge(self.chain, tgt)
            fn(eng).then_inc(self.chain, inc)

        self.q[ekey].append(emit)
        self.c += inc

    def gap(self):
        self.gap_c = self.c
        self.last_eng = None

    def wait_all(self, ekey):
        tgt = self.c

        def emit(eng, tgt=tgt):
            eng.wait_ge(self.chain, tgt)

        self.q[ekey].append(emit)

    def end_phase(self, blk):
        for e in ENGS:
            items = self.q[e]
            if items:
                getattr(blk, e)(lambda eng, items=items: [f(eng) for f in items])
        self.q = None


def build(debug=False):
    nc = bass.Bass("TRN2", target_bir_lowering=False, debug=False)
    import itertools
    _uid = itertools.count()

    def nm(s):
        return f"{s}_{next(_uid)}"


    x_in = nc.dram_tensor("x", [T, D], F32, kind="ExternalInput").ap()
    wq_h = nc.dram_tensor("wq", [D, D], F32, kind="ExternalInput").ap()
    wk_h = nc.dram_tensor("wk", [D, D], F32, kind="ExternalInput").ap()
    wv_h = nc.dram_tensor("wv", [D, D], F32, kind="ExternalInput").ap()
    wg_h = nc.dram_tensor("wg", [D, D], F32, kind="ExternalInput").ap()
    wo_h = nc.dram_tensor("wo", [D, D], F32, kind="ExternalInput").ap()
    bq_h = nc.dram_tensor("bq", [D, 1], F32, kind="ExternalInput").ap()
    bk_h = nc.dram_tensor("bk", [1, D], F32, kind="ExternalInput").ap()
    bv_h = nc.dram_tensor("bv", [1, D], F32, kind="ExternalInput").ap()
    bg_h = nc.dram_tensor("bg", [D, 1], F32, kind="ExternalInput").ap()
    bo_h = nc.dram_tensor("bo", [D, 1], F32, kind="ExternalInput").ap()
    wgate_h = nc.dram_tensor("wgate", [FF, D], F32, kind="ExternalInput").ap()
    wup_h = nc.dram_tensor("wup", [FF, D], F32, kind="ExternalInput").ap()
    wdown_h = nc.dram_tensor("wdown", [D, FF], F32, kind="ExternalInput").ap()
    id32_h = nc.dram_tensor("id32", [P, P], F32, kind="ExternalInput").ap()
    idbf_h = nc.dram_tensor("idbf", [P, P], BF16, kind="ExternalInput").ap()
    onesc_h = nc.dram_tensor("onesc", [P, 1], F32, kind="ExternalInput").ap()
    onesr_h = nc.dram_tensor("onesr", [1, P], F32, kind="ExternalInput").ap()
    onescb_h = nc.dram_tensor("onescb", [P, 1], BF16, kind="ExternalInput").ap()
    patt2_h = nc.dram_tensor("patt2", [2, P], F32, kind="ExternalInput").ap()

    y_out = nc.dram_tensor("y", [T, D], F32, kind="ExternalOutput").ap()

    xfm = nc.dram_tensor("xfm", [DC * P, T], F32).ap()
    mg = nc.dram_tensor("mg", [FCN * P, T], F32).ap()
    mup = nc.dram_tensor("mup", [FCN * P, T], F32).ap()
    ci = nc.dram_tensor("ci", [FCN * P, T], F32).ap()
    g2s = nc.dram_tensor("g2s", [FCN * P, T], I16).ap()
    ups = nc.dram_tensor("ups", [FCN * P, T], I16).ap()
    wgT_c = nc.dram_tensor("wgT_c", [DC * P, FF], BF16).ap()
    wupT_c = nc.dram_tensor("wupT_c", [DC * P, FF], BF16).ap()
    wdT_c = nc.dram_tensor("wdT_c", [FCN * P, D], BF16).ap()
    cc_in = nc.dram_tensor("cc_in", [1, 8], F32)
    cc_out = nc.dram_tensor("cc_out", [1, 8], F32)
    if debug:
        dbg = [nc.dram_tensor(f"dbg{i}", [DC * P, T], F32, kind="ExternalOutput").ap() for i in range(3)]
        dbgs = nc.dram_tensor("dbgs", [1, 64], F32, kind="ExternalOutput").ap()

    with ExitStack() as top:
        chain = top.enter_context(nc.semaphore("chain"))
        S = Ser(nc, chain)

        id32 = top.enter_context(nc.sbuf_tensor(nm("id32s"), [P, P], F32))
        idbf = top.enter_context(nc.sbuf_tensor(nm("idbfs"), [P, P], BF16))
        onesc = top.enter_context(nc.sbuf_tensor(nm("onescs"), [P, 1], F32))
        onesr = top.enter_context(nc.sbuf_tensor(nm("onesrs"), [1, P], F32))
        onescb = top.enter_context(nc.sbuf_tensor(nm("onescbs"), [P, 1], BF16))
        patt2 = top.enter_context(nc.sbuf_tensor(nm("patt2s"), [2, P], F32))
        crnd = top.enter_context(nc.sbuf_tensor(nm("crnd"), [P, 1], F32))
        c65k = top.enter_context(nc.sbuf_tensor(nm("c65k"), [P, 1], F32))
        ceps = top.enter_context(nc.sbuf_tensor(nm("ceps"), [1, 1], F32))
        crnd_r = top.enter_context(nc.sbuf_tensor(nm("crnd_r"), [1, 1], F32))
        czero = top.enter_context(nc.sbuf_tensor(nm("czero"), [P, 1], F32))
        scrow = top.enter_context(nc.sbuf_tensor(nm("scrow"), [1, 32], F32))
        sc = top.enter_context(nc.sbuf_tensor(nm("sc"), [P, 16], F32))
        ccl = top.enter_context(nc.sbuf_tensor(nm("ccl"), [1, 8], F32))
        ccr = top.enter_context(nc.sbuf_tensor(nm("ccr"), [1, 8], F32))
        bq_sb = top.enter_context(nc.sbuf_tensor(nm("bq_sb"), [P, DC], F32))
        bg_sb = top.enter_context(nc.sbuf_tensor(nm("bg_sb"), [P, DC], F32))
        bo_sb = top.enter_context(nc.sbuf_tensor(nm("bo_sb"), [P, DC], F32))
        bkr = top.enter_context(nc.sbuf_tensor(nm("bkr"), [1, D], F32))
        bvr = top.enter_context(nc.sbuf_tensor(nm("bvr"), [1, D], F32))
        lmax = top.enter_context(nc.sbuf_tensor(nm("lmax"), [P, 8], F32))
        t128 = top.enter_context(nc.sbuf_tensor(nm("t128"), [P, 2], F32))

        (IV_S1, IS_S1, IS_S3, IS_S2, IS_S4, IS_S5, IS_T0, IS_T1, IS_T2) = range(9)
        (SC_K1, SC_S3, SC_K2, SC_K3, SC_K4, SC_BETA) = range(6)

        def dma(dst, src):
            S.do("sync", lambda e, d=dst, s=src: e.dma_start(out=d, in_=s), 16)

        def vop(fn):
            S.do("vector", fn)

        def sop(fn):
            S.do("scalar", fn)

        def top_(fn):
            S.do("tensor", fn)

        def gop(fn):
            S.do("gpsimd", fn)

        def sb_row(ap):
            return ap

        H = {}

        def cross_part_absmax(src_col, dst_row):
            top_(lambda e: e.transpose(H["psx"][0:1, 0:P], src_col, id32[:]))
            vop(lambda e: e.reduce_max(dst_row, H["psx"][0:1, 0:P], axis=AX, apply_absolute_value=True))

        def bcast128(src11, slot):
            top_(lambda e: e.matmul(H["pssc"][:, slot:slot + 1], onesr[0:1, :], src11, start=True, stop=True))
            sop(lambda e: e.copy(sc[:, slot:slot + 1], H["pssc"][:, slot:slot + 1]))

        def allreduce_max(n):
            S.gap()
            S.do("gpsimd", lambda e: e.dma_start(out=cc_in.ap(), in_=ccl[:]), 16)
            S.gap()
            S.do("gpsimd", lambda e: e.collective_compute(
                "AllReduce", OP.max, replica_groups=[list(range(8))],
                ins=[cc_in.ap().opt()], outs=[cc_out.ap().opt()]), 1)
            S.gap()
            S.do("gpsimd", lambda e: e.dma_start(out=ccr[:], in_=cc_out.ap()), 16)
            S.gap()

        with nc.Block() as blk:
            S.start_phase()
            dma(id32[:], id32_h)
            dma(idbf[:], idbf_h)
            dma(onesc[:], onesc_h)
            dma(onesr[:], onesr_h)
            dma(onescb[:], onescb_h)
            dma(patt2[:], patt2_h)
            vop(lambda e: e.memset(crnd[:], RC))
            vop(lambda e: e.memset(c65k[:], 65536.0))
            vop(lambda e: e.memset(ceps[:], 1e-5))
            vop(lambda e: e.memset(crnd_r[:], RC))
            vop(lambda e: e.memset(czero[:], 0.0))
            vop(lambda e: e.memset(ccl[:], 0.0))
            for dc in range(DC):
                dma(bq_sb[:, dc:dc + 1], bq_h[dc * P:(dc + 1) * P, 0:1])
                dma(bg_sb[:, dc:dc + 1], bg_h[dc * P:(dc + 1) * P, 0:1])
                dma(bo_sb[:, dc:dc + 1], bo_h[dc * P:(dc + 1) * P, 0:1])
            dma(bkr[:], bk_h)
            dma(bvr[:], bv_h)
            S.end_phase(blk)

        with ExitStack() as _st:
            xtm = _st.enter_context(nc.sbuf_tensor(nm("xtm"), [P, D], F32))
            stf = _st.enter_context(nc.sbuf_tensor(nm("stf"), [P, DC * 1024], F32))
            pst_in = _st.enter_context(nc.psum_tensor(nm("pst_in"), [P, P], F32))
            blk = _st.enter_context(nc.Block())
            S.start_phase()
            for tc in range(TCN):
                dma(xtm[:], x_in[tc * P:(tc + 1) * P, :])
                g = tc % 8
                for dc in range(DC):
                    top_(lambda e, dc=dc: e.transpose(pst_in[:], xtm[:, dc * P:(dc + 1) * P], id32[:]))
                    sop(lambda e, dc=dc, g=g: e.copy(stf[:, dc * 1024 + g * P: dc * 1024 + (g + 1) * P], pst_in[:]))
                if g == 7:
                    base = (tc - 7) * P
                    for dc in range(DC):
                        dma(xfm[dc * P:(dc + 1) * P, base:base + 1024], stf[:, dc * 1024:(dc + 1) * 1024])
            S.end_phase(blk)

        def ln_stats(src, rows_a, rows_b):
            with ExitStack() as _st:
                lnx = _st.enter_context(nc.sbuf_tensor(nm("lnx"), [P, SL * 3], F32))
                lnq = _st.enter_context(nc.sbuf_tensor(nm("lnq"), [P, SL], F32))
                lnr = _st.enter_context(nc.sbuf_tensor(nm("lnr"), [1, SL * 3], F32))
                psS = _st.enter_context(nc.psum_tensor(nm("psS"), [1, SL], F32))
                psQ = _st.enter_context(nc.psum_tensor(nm("psQ"), [1, SL], F32))
                blk = _st.enter_context(nc.Block())
                S.start_phase()
                for sl in range(NSL):
                    for dc in range(DC):
                        xb = lnx[:, (dc % 3) * SL:(dc % 3 + 1) * SL]
                        dma(xb, src[dc * P:(dc + 1) * P, sl * SL:(sl + 1) * SL])
                        sop(lambda e, xb=xb: e.activation(lnq[:], xb, AF.Square))
                        top_(lambda e, xb=xb, dc=dc: e.matmul(
                            psS[0:1, :], onesc[:], xb, start=(dc == 0), stop=(dc == DC - 1)))
                        top_(lambda e, dc=dc: e.matmul(
                            psQ[0:1, :], onesc[:], lnq[:], start=(dc == 0), stop=(dc == DC - 1)))
                    mur = lnr[0:1, 0:SL]
                    tmp = lnr[0:1, SL:2 * SL]
                    sdr = lnr[0:1, 2 * SL:3 * SL]
                    vop(lambda e, mur=mur: e.tensor_scalar(mur, psS[0:1, :], 1.0 / D, None, OP.mult))
                    vop(lambda e, tmp=tmp: e.tensor_scalar(tmp, psQ[0:1, :], 1.0 / D, None, OP.mult))
                    vop(lambda e, mur=mur, sdr=sdr: e.tensor_tensor(sdr, mur, mur, OP.mult))
                    vop(lambda e, tmp=tmp, sdr=sdr: e.tensor_tensor(tmp, tmp, sdr, OP.subtract))
                    sop(lambda e, sdr=sdr, tmp=tmp: e.activation(sdr, tmp, AF.Sqrt, bias=ceps[:]))
                    vop(lambda e, sdr=sdr: e.reciprocal(sdr, sdr))
                    vop(lambda e, sdr=sdr, sl=sl: e.tensor_copy(rows_a[0:1, sl * SL:(sl + 1) * SL], sdr))
                    vop(lambda e, mur=mur, sdr=sdr, sl=sl: e.tensor_tensor(
                        rows_b[0:1, sl * SL:(sl + 1) * SL], mur, sdr, OP.mult))
                S.end_phase(blk)

        def ln_absmax_and_ar1(src, rows_a, rows_b):
            with ExitStack() as _st:
                amx = _st.enter_context(nc.sbuf_tensor(nm("amx"), [P, SL * 3], F32))
                amu = _st.enter_context(nc.sbuf_tensor(nm("amu"), [P, SL], F32))
                amt = _st.enter_context(nc.sbuf_tensor(nm("amt"), [P, 2], F32))
                psA = _st.enter_context(nc.psum_tensor(nm("psA"), [P, SL], F32))
                psB = _st.enter_context(nc.psum_tensor(nm("psB"), [P, SL], F32))
                psx_l = _st.enter_context(nc.psum_tensor(nm("psx_"), [1, P], F32))
                pssc_l = _st.enter_context(nc.psum_tensor(nm("pssc_"), [P, 8], F32))
                blk = _st.enter_context(nc.Block())
                H["psx"], H["pssc"] = psx_l, pssc_l
                S.start_phase()
                vop(lambda e: e.memset(lmax[:, 0:1], -3.0e38))
                vop(lambda e: e.memset(lmax[:, 1:2], 3.0e38))
                for sl in range(NSL):
                    top_(lambda e, sl=sl: e.matmul(psA[:], onesr[0:1, :], rows_a[0:1, sl * SL:(sl + 1) * SL], start=True, stop=True))
                    top_(lambda e, sl=sl: e.matmul(psB[:], onesr[0:1, :], rows_b[0:1, sl * SL:(sl + 1) * SL], start=True, stop=True))
                    for dc in range(DC):
                        xb = amx[:, (dc % 3) * SL:(dc % 3 + 1) * SL]
                        dma(xb, src[dc * P:(dc + 1) * P, sl * SL:(sl + 1) * SL])
                        vop(lambda e, xb=xb: e.tensor_tensor(amu[:], xb, psA[:], OP.mult))
                        vop(lambda e: e.tensor_tensor(amu[:], amu[:], psB[:], OP.subtract))
                        vop(lambda e: e.tensor_scalar(amu[:], amu[:], 1.0, -3.0e38, OP.mult, OP.max, accum_out=amt[:, 0:1]))
                        vop(lambda e: e.tensor_scalar(amu[:], amu[:], 1.0, 3.0e38, OP.mult, OP.min, accum_out=amt[:, 1:2]))
                        vop(lambda e: e.tensor_tensor(lmax[:, 0:1], lmax[:, 0:1], amt[:, 0:1], OP.max))
                        vop(lambda e: e.tensor_tensor(lmax[:, 1:2], lmax[:, 1:2], amt[:, 1:2], OP.min))
                vop(lambda e: e.tensor_scalar(lmax[:, 1:2], lmax[:, 1:2], -1.0, None, OP.mult))
                vop(lambda e: e.tensor_tensor(lmax[:, 0:1], lmax[:, 0:1], lmax[:, 1:2], OP.max))
                cross_part_absmax(lmax[:, 0:1], scrow[0:1, IS_T0:IS_T0 + 1])
                vop(lambda e: e.tensor_copy(ccl[0:1, 0:1], scrow[0:1, IS_T0:IS_T0 + 1]))
                allreduce_max(1)
                vop(lambda e: e.tensor_scalar(scrow[0:1, IS_S1:IS_S1 + 1], ccr[0:1, 0:1], 1.0 / QA, 1e-12, OP.mult, OP.max))
                vop(lambda e: e.reciprocal(scrow[0:1, IV_S1:IV_S1 + 1], scrow[0:1, IS_S1:IS_S1 + 1]))
                vop(lambda e: e.tensor_scalar(rows_a[:], rows_a[:], scrow[0:1, IV_S1:IV_S1 + 1], None, OP.mult))
                vop(lambda e: e.tensor_scalar(rows_b[:], rows_b[:], scrow[0:1, IV_S1:IV_S1 + 1], crnd_r[:], OP.mult, OP.subtract))
                S.end_phase(blk)

        def xq_pass(src, dst_bf, rows_a, rows_b):
            with ExitStack() as _st:
                xqx = _st.enter_context(nc.sbuf_tensor(nm("xqx"), [P, SL * 3], F32))
                xqu = _st.enter_context(nc.sbuf_tensor(nm("xqu"), [P, SL], F32))
                psA = _st.enter_context(nc.psum_tensor(nm("psA2"), [P, SL], F32))
                psB = _st.enter_context(nc.psum_tensor(nm("psB2"), [P, SL], F32))
                blk = _st.enter_context(nc.Block())
                S.start_phase()
                for sl in range(NSL):
                    top_(lambda e, sl=sl: e.matmul(psA[:], onesr[0:1, :], rows_a[0:1, sl * SL:(sl + 1) * SL], start=True, stop=True))
                    top_(lambda e, sl=sl: e.matmul(psB[:], onesr[0:1, :], rows_b[0:1, sl * SL:(sl + 1) * SL], start=True, stop=True))
                    for dc in range(DC):
                        xb = xqx[:, (dc % 3) * SL:(dc % 3 + 1) * SL]
                        dma(xb, src[dc * P:(dc + 1) * P, sl * SL:(sl + 1) * SL])
                        vop(lambda e, xb=xb: e.tensor_tensor(xqu[:], xb, psA[:], OP.mult))
                        vop(lambda e: e.tensor_tensor(xqu[:], xqu[:], psB[:], OP.subtract))
                        vop(lambda e, dc=dc, sl=sl: e.tensor_scalar(
                            dst_bf[:, dc * T + sl * SL: dc * T + (sl + 1) * SL], xqu[:], RC, None, OP.subtract))
                S.end_phase(blk)

        def xn_pass(src, dst_bf, rows_a, rows_b):
            with ExitStack() as _st:
                xnx = _st.enter_context(nc.sbuf_tensor(nm("xnx"), [P, SL * 3], F32))
                xnu = _st.enter_context(nc.sbuf_tensor(nm("xnu"), [P, SL], F32))
                psA = _st.enter_context(nc.psum_tensor(nm("psA3"), [P, SL], F32))
                psB = _st.enter_context(nc.psum_tensor(nm("psB3"), [P, SL], F32))
                blk = _st.enter_context(nc.Block())
                S.start_phase()
                for sl in range(NSL):
                    top_(lambda e, sl=sl: e.matmul(psA[:], onesr[0:1, :], rows_a[0:1, sl * SL:(sl + 1) * SL], start=True, stop=True))
                    top_(lambda e, sl=sl: e.matmul(psB[:], onesr[0:1, :], rows_b[0:1, sl * SL:(sl + 1) * SL], start=True, stop=True))
                    for dc in range(DC):
                        xb = xnx[:, (dc % 3) * SL:(dc % 3 + 1) * SL]
                        dma(xb, src[dc * P:(dc + 1) * P, sl * SL:(sl + 1) * SL])
                        vop(lambda e, xb=xb: e.tensor_tensor(xnu[:], xb, psA[:], OP.mult))
                        vop(lambda e, dc=dc, sl=sl: e.tensor_tensor(
                            dst_bf[:, dc * T + sl * SL: dc * T + (sl + 1) * SL], xnu[:], psB[:], OP.subtract))
                S.end_phase(blk)

        def ffn(inv, src):
            with nc.sbuf_tensor(f"xqib{inv}", [P, DC * T], BF16) as xqib:
                with nc.sbuf_tensor(nm("rows_a"), [1, T], F32) as rows_a, nc.sbuf_tensor(nm("rows_b"), [1, T], F32) as rows_b:
                    ln_stats(src, rows_a, rows_b)
                    ln_absmax_and_ar1(src, rows_a, rows_b)
                    xq_pass(src, xqib, rows_a, rows_b)

                with ExitStack() as _st:
                    wnat = _st.enter_context(nc.sbuf_tensor(nm("wnat"), [P, D * 2], F32))
                    wqb = _st.enter_context(nc.sbuf_tensor(nm("wqb"), [P, D], F32))
                    wpb = _st.enter_context(nc.sbuf_tensor(nm("wpb"), [P, D], BF16))
                    bm = _st.enter_context(nc.sbuf_tensor(nm("bm"), [P, 64], F32))
                    wT = _st.enter_context(nc.sbuf_tensor(nm("wT"), [P, 2 * DC * P], BF16))
                    mst = _st.enter_context(nc.sbuf_tensor(nm("mst"), [P, T], F32))
                    mtmp = _st.enter_context(nc.sbuf_tensor(nm("mtmp"), [P, 2], F32))
                    psM = _st.enter_context(nc.psum_tensor(nm("psM"), [P, SL], F32))
                    psM2 = _st.enter_context(nc.psum_tensor(nm("psM2"), [P, SL], F32))
                    psW = _st.enter_context(nc.psum_tensor(nm("psW"), [P, P], BF16))
                    psx_l = _st.enter_context(nc.psum_tensor(nm("psx4"), [1, P], F32))
                    pssc_l = _st.enter_context(nc.psum_tensor(nm("pssc4"), [P, 8], F32))
                    blk = _st.enter_context(nc.Block())
                    H["psx"], H["pssc"] = psx_l, pssc_l
                    S.start_phase()
                    for wi, (wsrc, spill, wtc) in enumerate(
                        [(wgate_h, mg, wgT_c), (wup_h, mup, wupT_c)]
                    ):
                        vop(lambda e, wi=wi: e.memset(lmax[:, 2 + 2 * wi:3 + 2 * wi], -3.0e38))
                        vop(lambda e, wi=wi: e.memset(lmax[:, 3 + 2 * wi:4 + 2 * wi], 3.0e38))
                        for fc in range(FCN):
                            wr = wT[:, (fc % 2) * DC * P:(fc % 2 + 1) * DC * P]
                            if inv == 0:
                                wn = wnat[:, (fc % 2) * D:(fc % 2 + 1) * D]
                                dma(wn, wsrc[fc * P:(fc + 1) * P, :])
                                vop(lambda e, wn=wn: e.reduce_max(
                                    bm[:, 0:32], wn.rearrange("p (b e) -> p b e", e=32),
                                    axis=AX, apply_absolute_value=True))
                                vop(lambda e: e.tensor_scalar(bm[:, 0:32], bm[:, 0:32], 1.0 / QW, 1e-12, OP.mult, OP.max))
                                vop(lambda e: e.reciprocal(bm[:, 32:64], bm[:, 0:32]))
                                vop(lambda e, wn=wn: e.tensor_tensor(
                                    wqb[:].rearrange("p (b e) -> p b e", e=32),
                                    wn.rearrange("p (b e) -> p b e", e=32),
                                    bm[:, 32:64].unsqueeze(2).broadcast_to([P, 32, 32]), OP.mult))
                                vop(lambda e: e.tensor_scalar(wqb[:], wqb[:], RC, RC, OP.add, OP.subtract))
                                vop(lambda e: e.tensor_tensor(
                                    wpb[:].rearrange("p (b e) -> p b e", e=32),
                                    wqb[:].rearrange("p (b e) -> p b e", e=32),
                                    bm[:, 0:32].unsqueeze(2).broadcast_to([P, 32, 32]), OP.mult))
                                for dc in range(DC):
                                    top_(lambda e, dc=dc: e.transpose(psW[:], wpb[:, dc * P:(dc + 1) * P], idbf[:]))
                                    sop(lambda e, dc=dc, wr=wr: e.copy(wr[:, dc * P:(dc + 1) * P], psW[:]))
                                    dma(wtc[dc * P:(dc + 1) * P, fc * P:(fc + 1) * P], wr[:, dc * P:(dc + 1) * P])
                            else:
                                for dc in range(DC):
                                    dma(wr[:, dc * P:(dc + 1) * P], wtc[dc * P:(dc + 1) * P, fc * P:(fc + 1) * P])
                            for sl in range(NSL):
                                ps = psM if sl % 2 == 0 else psM2
                                for dc in range(DC):
                                    top_(lambda e, dc=dc, sl=sl, wr=wr, ps=ps: e.matmul(
                                        ps[:], wr[:, dc * P:(dc + 1) * P],
                                        xqib[:, dc * T + sl * SL: dc * T + (sl + 1) * SL],
                                        start=(dc == 0), stop=(dc == DC - 1)))
                                vop(lambda e, sl=sl, ps=ps: e.tensor_scalar(
                                    mst[:, sl * SL:(sl + 1) * SL], ps[:], 1.0, -3.0e38, OP.mult, OP.max, accum_out=mtmp[:, 0:1]))
                                vop(lambda e, sl=sl, ps=ps: e.tensor_scalar(
                                    mst[:, sl * SL:(sl + 1) * SL], ps[:], 1.0, 3.0e38, OP.mult, OP.min, accum_out=mtmp[:, 1:2]))
                                vop(lambda e, wi=wi: e.tensor_tensor(lmax[:, 2 + 2 * wi:3 + 2 * wi], lmax[:, 2 + 2 * wi:3 + 2 * wi], mtmp[:, 0:1], OP.max))
                                vop(lambda e, wi=wi: e.tensor_tensor(lmax[:, 3 + 2 * wi:4 + 2 * wi], lmax[:, 3 + 2 * wi:4 + 2 * wi], mtmp[:, 1:2], OP.min))
                            dma(spill[fc * P:(fc + 1) * P, :], mst[:])
                        vop(lambda e, wi=wi: e.tensor_scalar(lmax[:, 3 + 2 * wi:4 + 2 * wi], lmax[:, 3 + 2 * wi:4 + 2 * wi], -1.0, None, OP.mult))
                        vop(lambda e, wi=wi: e.tensor_tensor(lmax[:, 2 + 2 * wi:3 + 2 * wi], lmax[:, 2 + 2 * wi:3 + 2 * wi], lmax[:, 3 + 2 * wi:4 + 2 * wi], OP.max))
                        cross_part_absmax(lmax[:, 2 + 2 * wi:3 + 2 * wi], scrow[0:1, IS_T0 + wi:IS_T0 + wi + 1])
                        vop(lambda e, wi=wi: e.tensor_copy(ccl[0:1, wi:wi + 1], scrow[0:1, IS_T0 + wi:IS_T0 + wi + 1]))

                    if inv == 0:
                        with ExitStack() as _st:
                            wd3 = _st.enter_context(nc.sbuf_tensor(nm("wd3"), [P, FF], F32))
                            wdq = _st.enter_context(nc.sbuf_tensor(nm("wdq"), [P, FF], F32))
                            wdb = _st.enter_context(nc.sbuf_tensor(nm("wdb"), [P, FF], BF16))
                            bm3 = _st.enter_context(nc.sbuf_tensor(nm("bm3"), [P, 192], F32))
                            wds = _st.enter_context(nc.sbuf_tensor(nm("wds"), [P, P], BF16))
                            for r in range(DC):
                                dma(wd3[:], wdown_h[r * P:(r + 1) * P, :])
                                vop(lambda e: e.reduce_max(
                                    bm3[:, 0:96], wd3[:].rearrange("p (b e) -> p b e", e=32),
                                    axis=AX, apply_absolute_value=True))
                                vop(lambda e: e.tensor_scalar(bm3[:, 0:96], bm3[:, 0:96], 1.0 / QW, 1e-12, OP.mult, OP.max))
                                vop(lambda e: e.reciprocal(bm3[:, 96:192], bm3[:, 0:96]))
                                vop(lambda e: e.tensor_tensor(
                                    wdq[:].rearrange("p (b e) -> p b e", e=32),
                                    wd3[:].rearrange("p (b e) -> p b e", e=32),
                                    bm3[:, 96:192].unsqueeze(2).broadcast_to([P, 96, 32]), OP.mult))
                                vop(lambda e: e.tensor_scalar(wdq[:], wdq[:], RC, RC, OP.add, OP.subtract))
                                vop(lambda e: e.tensor_tensor(
                                    wdb[:].rearrange("p (b e) -> p b e", e=32),
                                    wdq[:].rearrange("p (b e) -> p b e", e=32),
                                    bm3[:, 0:96].unsqueeze(2).broadcast_to([P, 96, 32]), OP.mult))
                                for fc in range(FCN):
                                    top_(lambda e, fc=fc: e.transpose(psW[:], wdb[:, fc * P:(fc + 1) * P], idbf[:]))
                                    sop(lambda e: e.copy(wds[:], psW[:]))
                                    dma(wdT_c[fc * P:(fc + 1) * P, r * P:(r + 1) * P], wds[:])

                    allreduce_max(2)
                    vop(lambda e: e.tensor_scalar(ccr[0:1, 0:2], ccr[0:1, 0:2], scrow[0:1, IS_S1:IS_S1 + 1], None, OP.mult))
                    vop(lambda e: e.tensor_scalar(scrow[0:1, IS_S3:IS_S3 + 1], ccr[0:1, 0:1], 1.0 / QA, 1e-12, OP.mult, OP.max))
                    vop(lambda e: e.tensor_scalar(scrow[0:1, IS_S2:IS_S2 + 1], ccr[0:1, 1:2], 1.0 / QA, 1e-12, OP.mult, OP.max))
                    vop(lambda e: e.reciprocal(scrow[0:1, IS_T1:IS_T1 + 1], scrow[0:1, IS_S3:IS_S3 + 1]))
                    vop(lambda e: e.tensor_scalar(scrow[0:1, IS_T1:IS_T1 + 1], scrow[0:1, IS_T1:IS_T1 + 1], scrow[0:1, IS_S1:IS_S1 + 1], None, OP.mult))
                    bcast128(scrow[0:1, IS_T1:IS_T1 + 1], SC_K1)
                    vop(lambda e: e.reciprocal(scrow[0:1, IS_T2:IS_T2 + 1], scrow[0:1, IS_S2:IS_S2 + 1]))
                    vop(lambda e: e.tensor_scalar(scrow[0:1, IS_T2:IS_T2 + 1], scrow[0:1, IS_T2:IS_T2 + 1], scrow[0:1, IS_S1:IS_S1 + 1], None, OP.mult))
                    bcast128(scrow[0:1, IS_T2:IS_T2 + 1], SC_K2)
                    bcast128(scrow[0:1, IS_S3:IS_S3 + 1], SC_S3)
                    S.end_phase(blk)

            with ExitStack() as _st:
                e1m = _st.enter_context(nc.sbuf_tensor(nm("e1m"), [P, T], F32))
                e1a = _st.enter_context(nc.sbuf_tensor(nm("e1a"), [P, T], F32))
                e1b = _st.enter_context(nc.sbuf_tensor(nm("e1b"), [P, T], F32))
                e1c = _st.enter_context(nc.sbuf_tensor(nm("e1c"), [P, T], F32))
                psx_l = _st.enter_context(nc.psum_tensor(nm("psx5"), [1, P], F32))
                pssc_l = _st.enter_context(nc.psum_tensor(nm("pssc5"), [P, 8], F32))
                blk = _st.enter_context(nc.Block())
                H["psx"], H["pssc"] = psx_l, pssc_l
                S.start_phase()
                vop(lambda e: e.memset(lmax[:, 6:7], 0.0))
                for fc in range(FCN):
                    dma(e1m[:], mg[fc * P:(fc + 1) * P, :])
                    sop(lambda e: e.activation(e1a[:], e1m[:], AF.Identity, bias=crnd[:], scale=sc[:, SC_K1:SC_K1 + 1]))
                    vop(lambda e: e.tensor_scalar(e1a[:], e1a[:], RC, None, OP.subtract))
                    sop(lambda e: e.activation(e1b[:], e1a[:], AF.Sigmoid, bias=czero[:], scale=sc[:, SC_S3:SC_S3 + 1]))
                    sop(lambda e: e.activation(e1b[:], e1b[:], AF.Identity, bias=crnd[:], scale=c65k[:]))
                    vop(lambda e: e.tensor_scalar(e1b[:], e1b[:], RC, 65535.0, OP.subtract, OP.min))
                    vop(lambda e: e.tensor_tensor(e1c[:], e1a[:], e1b[:], OP.mult))
                    vop(lambda e: e.reduce_max(lmax[:, 7:8], e1c[:], axis=AX, apply_absolute_value=True))
                    vop(lambda e: e.tensor_tensor(lmax[:, 6:7], lmax[:, 6:7], lmax[:, 7:8], OP.max))
                    dma(ci[fc * P:(fc + 1) * P, :], e1c[:])
                cross_part_absmax(lmax[:, 6:7], scrow[0:1, IS_T0:IS_T0 + 1])
                vop(lambda e: e.tensor_copy(ccl[0:1, 0:1], scrow[0:1, IS_T0:IS_T0 + 1]))
                allreduce_max(1)
                vop(lambda e: e.tensor_scalar(scrow[0:1, IS_T1:IS_T1 + 1], scrow[0:1, IS_S3:IS_S3 + 1], 1.0 / 65536.0, None, OP.mult))
                vop(lambda e: e.tensor_scalar(scrow[0:1, IS_T2:IS_T2 + 1], ccr[0:1, 0:1], scrow[0:1, IS_T1:IS_T1 + 1], None, OP.mult))
                vop(lambda e: e.tensor_scalar(scrow[0:1, IS_S4:IS_S4 + 1], scrow[0:1, IS_T2:IS_T2 + 1], 1.0 / QA, 1e-12, OP.mult, OP.max))
                vop(lambda e: e.reciprocal(scrow[0:1, IS_T2:IS_T2 + 1], scrow[0:1, IS_S4:IS_S4 + 1]))
                vop(lambda e: e.tensor_scalar(scrow[0:1, IS_T2:IS_T2 + 1], scrow[0:1, IS_T2:IS_T2 + 1], scrow[0:1, IS_T1:IS_T1 + 1], None, OP.mult))
                bcast128(scrow[0:1, IS_T2:IS_T2 + 1], SC_K3)
                S.end_phase(blk)

            with ExitStack() as _st:
                e2a = _st.enter_context(nc.sbuf_tensor(nm("e2a"), [P, T], F32))
                e2b = _st.enter_context(nc.sbuf_tensor(nm("e2b"), [P, T], F32))
                e2c = _st.enter_context(nc.sbuf_tensor(nm("e2c"), [P, T], F32))
                e2i = _st.enter_context(nc.sbuf_tensor(nm("e2i"), [P, T], I16))
                e2j = _st.enter_context(nc.sbuf_tensor(nm("e2j"), [P, T], I16))
                psx_l = _st.enter_context(nc.psum_tensor(nm("psx6"), [1, P], F32))
                pssc_l = _st.enter_context(nc.psum_tensor(nm("pssc6"), [P, 8], F32))
                blk = _st.enter_context(nc.Block())
                H["psx"], H["pssc"] = psx_l, pssc_l
                S.start_phase()
                vop(lambda e: e.memset(lmax[:, 6:7], 0.0))
                for fc in range(FCN):
                    dma(e2a[:], ci[fc * P:(fc + 1) * P, :])
                    sop(lambda e: e.activation(e2a[:], e2a[:], AF.Identity, bias=crnd[:], scale=sc[:, SC_K3:SC_K3 + 1]))
                    vop(lambda e: e.tensor_scalar(e2a[:], e2a[:], RC, None, OP.subtract))
                    vop(lambda e: e.tensor_copy(e2i[:], e2a[:]))
                    dma(g2s[fc * P:(fc + 1) * P, :], e2i[:])
                    dma(e2b[:], mup[fc * P:(fc + 1) * P, :])
                    sop(lambda e: e.activation(e2b[:], e2b[:], AF.Identity, bias=crnd[:], scale=sc[:, SC_K2:SC_K2 + 1]))
                    vop(lambda e: e.tensor_scalar(e2b[:], e2b[:], RC, None, OP.subtract))
                    vop(lambda e: e.tensor_copy(e2j[:], e2b[:]))
                    dma(ups[fc * P:(fc + 1) * P, :], e2j[:])
                    vop(lambda e: e.tensor_tensor(e2c[:], e2a[:], e2b[:], OP.mult))
                    vop(lambda e: e.reduce_max(lmax[:, 7:8], e2c[:], axis=AX, apply_absolute_value=True))
                    vop(lambda e: e.tensor_tensor(lmax[:, 6:7], lmax[:, 6:7], lmax[:, 7:8], OP.max))
                cross_part_absmax(lmax[:, 6:7], scrow[0:1, IS_T0:IS_T0 + 1])
                vop(lambda e: e.tensor_copy(ccl[0:1, 0:1], scrow[0:1, IS_T0:IS_T0 + 1]))
                allreduce_max(1)
                vop(lambda e: e.tensor_scalar(scrow[0:1, IS_T1:IS_T1 + 1], scrow[0:1, IS_S4:IS_S4 + 1], scrow[0:1, IS_S2:IS_S2 + 1], None, OP.mult))
                vop(lambda e: e.tensor_scalar(scrow[0:1, IS_T2:IS_T2 + 1], ccr[0:1, 0:1], scrow[0:1, IS_T1:IS_T1 + 1], None, OP.mult))
                vop(lambda e: e.tensor_scalar(scrow[0:1, IS_S5:IS_S5 + 1], scrow[0:1, IS_T2:IS_T2 + 1], 1.0 / QA, 1e-12, OP.mult, OP.max))
                vop(lambda e: e.reciprocal(scrow[0:1, IS_T2:IS_T2 + 1], scrow[0:1, IS_S5:IS_S5 + 1]))
                vop(lambda e: e.tensor_scalar(scrow[0:1, IS_T2:IS_T2 + 1], scrow[0:1, IS_T2:IS_T2 + 1], scrow[0:1, IS_T1:IS_T1 + 1], None, OP.mult))
                bcast128(scrow[0:1, IS_T2:IS_T2 + 1], SC_K4)
                vop(lambda e: e.tensor_scalar(scrow[0:1, IS_T1:IS_T1 + 1], scrow[0:1, IS_S5:IS_S5 + 1], 0.5, None, OP.mult))
                bcast128(scrow[0:1, IS_T1:IS_T1 + 1], SC_BETA)
                S.end_phase(blk)

            with ExitStack() as _st:
                wdTs = _st.enter_context(nc.sbuf_tensor(nm("wdTs"), [P, FCN * D], BF16))
                oib = _st.enter_context(nc.sbuf_tensor(nm("oib"), [P, FCN * SL], BF16))
                e3g = _st.enter_context(nc.sbuf_tensor(nm("e3g"), [P, FCN * SL], I16))
                e3u = _st.enter_context(nc.sbuf_tensor(nm("e3u"), [P, FCN * SL], I16))
                e3d = _st.enter_context(nc.sbuf_tensor(nm("e3d"), [P, FCN * SL], F32))
                e3x = _st.enter_context(nc.sbuf_tensor(nm("e3x"), [P, SL], F32))
                e3s = _st.enter_context(nc.sbuf_tensor(nm("e3s"), [P, SL], F32))
                psD = _st.enter_context(nc.psum_tensor(nm("psD"), [P, SL], F32))
                psD2 = _st.enter_context(nc.psum_tensor(nm("psD2"), [P, SL], F32))
                blk = _st.enter_context(nc.Block())
                S.start_phase()
                for fc in range(FCN):
                    dma(wdTs[:, fc * D:(fc + 1) * D], wdT_c[fc * P:(fc + 1) * P, :])
                for sl in range(NSL):
                    for fc in range(FCN):
                        dma(e3g[:, fc * SL:(fc + 1) * SL], g2s[fc * P:(fc + 1) * P, sl * SL:(sl + 1) * SL])
                        dma(e3u[:, fc * SL:(fc + 1) * SL], ups[fc * P:(fc + 1) * P, sl * SL:(sl + 1) * SL])
                    vop(lambda e: e.tensor_tensor(e3d[:], e3g[:], e3u[:], OP.mult))
                    sop(lambda e: e.activation(e3d[:], e3d[:], AF.Identity, bias=crnd[:], scale=sc[:, SC_K4:SC_K4 + 1]))
                    vop(lambda e: e.tensor_scalar(oib[:], e3d[:], RC, None, OP.subtract))
                    for dc in range(DC):
                        ps = psD if dc % 2 == 0 else psD2
                        for fc in range(FCN):
                            top_(lambda e, fc=fc, dc=dc, ps=ps: e.matmul(
                                ps[:], wdTs[:, fc * D + dc * P: fc * D + (dc + 1) * P],
                                oib[:, fc * SL:(fc + 1) * SL],
                                start=(fc == 0), stop=(fc == FCN - 1)))
                        dma(e3x[:], src[dc * P:(dc + 1) * P, sl * SL:(sl + 1) * SL])
                        sop(lambda e, ps=ps: e.activation(e3s[:], ps[:], AF.Identity, bias=czero[:], scale=sc[:, SC_BETA:SC_BETA + 1]))
                        vop(lambda e: e.tensor_tensor(e3s[:], e3s[:], e3x[:], OP.add))
                        dma(src[dc * P:(dc + 1) * P, sl * SL:(sl + 1) * SL], e3s[:])
                S.end_phase(blk)

        def attn(src):
            with ExitStack() as _st:
                xn2b = _st.enter_context(nc.sbuf_tensor(nm("xn2b"), [P, DC * T], BF16))
                kvb = _st.enter_context(nc.sbuf_tensor(nm("kvb"), [P, 16 * P], BF16))
                kdb = _st.enter_context(nc.sbuf_tensor(nm("kdb"), [P, 32], BF16))
                with nc.sbuf_tensor(nm("rows_a"), [1, T], F32) as rows_a, nc.sbuf_tensor(nm("rows_b"), [1, T], F32) as rows_b:
                    ln_stats(src, rows_a, rows_b)
                    xn_pass(src, xn2b, rows_a, rows_b)

                with ExitStack() as _st:
                    wkb = _st.enter_context(nc.sbuf_tensor(nm("wkb"), [P, DC * D], BF16))
                    wvb = _st.enter_context(nc.sbuf_tensor(nm("wvb"), [P, DC * D], BF16))
                    wldf = _st.enter_context(nc.sbuf_tensor(nm("wldf"), [P, D], F32))
                    bkb = _st.enter_context(nc.sbuf_tensor(nm("bkb"), [P, D], F32))
                    bvb = _st.enter_context(nc.sbuf_tensor(nm("bvb"), [P, D], F32))
                    ktf = _st.enter_context(nc.sbuf_tensor(nm("ktf"), [P, SL], F32))
                    kte = _st.enter_context(nc.sbuf_tensor(nm("kte"), [P, SL], F32))
                    ktb = _st.enter_context(nc.sbuf_tensor(nm("ktb"), [P, D], BF16))
                    vtb = _st.enter_context(nc.sbuf_tensor(nm("vtb"), [P, D], BF16))
                    psK = _st.enter_context(nc.psum_tensor(nm("psK"), [P, SL], F32))
                    psV = _st.enter_context(nc.psum_tensor(nm("psV"), [P, SL], F32))
                    psKV0 = _st.enter_context(nc.psum_tensor(nm("psKV0"), [P, SL], F32))
                    psKV1 = _st.enter_context(nc.psum_tensor(nm("psKV1"), [P, SL], F32))
                    psKV2 = _st.enter_context(nc.psum_tensor(nm("psKV2"), [P, SL], F32))
                    psKV3 = _st.enter_context(nc.psum_tensor(nm("psKV3"), [P, SL], F32))
                    psBC = _st.enter_context(nc.psum_tensor(nm("psBC"), [P, SL], F32))
                    blk = _st.enter_context(nc.Block())
                    psKV = [psKV0, psKV1, psKV2, psKV3]
                    S.start_phase()
                    for dk in range(DC):
                        dma(wldf[:], wk_h[dk * P:(dk + 1) * P, :])
                        vop(lambda e, dk=dk: e.tensor_copy(wkb[:, dk * D:(dk + 1) * D], wldf[:]))
                        dma(wldf[:], wv_h[dk * P:(dk + 1) * P, :])
                        vop(lambda e, dk=dk: e.tensor_copy(wvb[:, dk * D:(dk + 1) * D], wldf[:]))
                    for h in range(2):
                        top_(lambda e, h=h: e.matmul(psBC[:], onesr[0:1, :], bkr[0:1, h * SL:(h + 1) * SL], start=True, stop=True))
                        sop(lambda e, h=h: e.copy(bkb[:, h * SL:(h + 1) * SL], psBC[:]))
                        top_(lambda e, h=h: e.matmul(psBC[:], onesr[0:1, :], bvr[0:1, h * SL:(h + 1) * SL], start=True, stop=True))
                        sop(lambda e, h=h: e.copy(bvb[:, h * SL:(h + 1) * SL], psBC[:]))
                    for tc in range(TCN):
                        sq, ti = tc // 16, tc % 16
                        for half in range(2):
                            for dk in range(DC):
                                top_(lambda e, dk=dk, tc=tc, half=half: e.matmul(
                                    psK[:], xn2b[:, dk * T + tc * P:dk * T + (tc + 1) * P],
                                    wkb[:, dk * D + half * SL: dk * D + (half + 1) * SL],
                                    start=(dk == 0), stop=(dk == DC - 1)))
                            vop(lambda e, half=half: e.tensor_tensor(ktf[:], psK[:], bkb[:, half * SL:(half + 1) * SL], OP.add))
                            vop(lambda e: e.tensor_scalar(kte[:], ktf[:], 0.0, None, OP.min))
                            sop(lambda e: e.activation(kte[:], kte[:], AF.Exp))
                            vop(lambda e, half=half: e.scalar_tensor_tensor(
                                ktb[:, half * SL:(half + 1) * SL], ktf[:], 0.0, kte[:], OP.max, OP.add))
                            for dk in range(DC):
                                top_(lambda e, dk=dk, tc=tc, half=half: e.matmul(
                                    psV[:], xn2b[:, dk * T + tc * P:dk * T + (tc + 1) * P],
                                    wvb[:, dk * D + half * SL: dk * D + (half + 1) * SL],
                                    start=(dk == 0), stop=(dk == DC - 1)))
                            vop(lambda e, half=half: e.tensor_tensor(
                                vtb[:, half * SL:(half + 1) * SL], psV[:], bvb[:, half * SL:(half + 1) * SL], OP.add))
                        for p in range(8):
                            ps = psKV[p // 2]
                            off = (p % 2) * 256
                            top_(lambda e, p=p, ps=ps, off=off, ti=ti: e.matmul(
                                ps[:, off:off + P], ktb[:, p * P:(p + 1) * P], vtb[:, p * P:(p + 1) * P],
                                start=(ti == 0), stop=(ti == 15), skip_group_check=True))
                            top_(lambda e, p=p, ps=ps, off=off, ti=ti: e.matmul(
                                ps[:, off + P:off + P + 1], ktb[:, p * P:(p + 1) * P], onescb[:],
                                start=(ti == 0), stop=(ti == 15), skip_group_check=True))
                        if ti == 15:
                            for p in range(8):
                                ps = psKV[p // 2]
                                off = (p % 2) * 256
                                kvt = kvb[:, (sq * 8 + p) * P:(sq * 8 + p + 1) * P]
                                vop(lambda e, kvt=kvt: e.memset(kvt, 0.0))
                                vop(lambda e, ps=ps, off=off, kvt=kvt: e.tensor_copy(kvt[0:64, 0:64], ps[0:64, off:off + 64]))
                                vop(lambda e, ps=ps, off=off, kvt=kvt: e.tensor_copy(kvt[64:128, 64:128], ps[64:128, off + 64:off + 128]))
                                kdt = kdb[:, (sq * 8 + p) * 2:(sq * 8 + p) * 2 + 2]
                                vop(lambda e, kdt=kdt: e.memset(kdt, 0.0))
                                vop(lambda e, ps=ps, off=off, kdt=kdt: e.tensor_copy(kdt[0:64, 0:1], ps[0:64, off + P:off + P + 1]))
                                vop(lambda e, ps=ps, off=off, kdt=kdt: e.tensor_copy(kdt[64:128, 1:2], ps[64:128, off + P:off + P + 1]))
                    S.end_phase(blk)

                with ExitStack() as _st:
                    wqsb = _st.enter_context(nc.sbuf_tensor(nm("wqsb"), [P, DC * D], BF16))
                    wgsb = _st.enter_context(nc.sbuf_tensor(nm("wgsb"), [P, DC * D], BF16))
                    wosb = _st.enter_context(nc.sbuf_tensor(nm("wosb"), [P, DC * D], BF16))
                    wldf2 = _st.enter_context(nc.sbuf_tensor(nm("wldf2"), [P, D], F32))
                    qrf = _st.enter_context(nc.sbuf_tensor(nm("qrf"), [P, SL], F32))
                    qef = _st.enter_context(nc.sbuf_tensor(nm("qef"), [P, SL], F32))
                    qbf = _st.enter_context(nc.sbuf_tensor(nm("qbf"), [P, DC * SL], BF16))
                    zsb = _st.enter_context(nc.sbuf_tensor(nm("zsb"), [2, SL], F32))
                    zbs = _st.enter_context(nc.sbuf_tensor(nm("zbs"), [P, SL], F32))
                    ssb = _st.enter_context(nc.sbuf_tensor(nm("ssb"), [P, SL], F32))
                    gsb = _st.enter_context(nc.sbuf_tensor(nm("gsb"), [P, DC * SL], F32))
                    abf = _st.enter_context(nc.sbuf_tensor(nm("abf"), [P, DC * SL], BF16))
                    aor = _st.enter_context(nc.sbuf_tensor(nm("aor"), [P, SL], F32))
                    axr = _st.enter_context(nc.sbuf_tensor(nm("axr"), [P, SL], F32))
                    psQ7 = _st.enter_context(nc.psum_tensor(nm("psQ7"), [P, SL], F32))
                    psQ8 = _st.enter_context(nc.psum_tensor(nm("psQ8"), [P, SL], F32))
                    psZr = _st.enter_context(nc.psum_tensor(nm("psZr"), [2, SL], F32))
                    psZb = _st.enter_context(nc.psum_tensor(nm("psZb"), [P, SL], F32))
                    psS7 = _st.enter_context(nc.psum_tensor(nm("psS7"), [P, SL], F32))
                    psO7 = _st.enter_context(nc.psum_tensor(nm("psO7"), [P, SL], F32))
                    psO8 = _st.enter_context(nc.psum_tensor(nm("psO8"), [P, SL], F32))
                    blk2 = _st.enter_context(nc.Block())
                    S.start_phase()
                    for dk in range(DC):
                        dma(wldf2[:], wq_h[dk * P:(dk + 1) * P, :])
                        vop(lambda e, dk=dk: e.tensor_copy(wqsb[:, dk * D:(dk + 1) * D], wldf2[:]))
                        dma(wldf2[:], wg_h[dk * P:(dk + 1) * P, :])
                        vop(lambda e, dk=dk: e.tensor_copy(wgsb[:, dk * D:(dk + 1) * D], wldf2[:]))
                        dma(wldf2[:], wo_h[dk * P:(dk + 1) * P, :])
                        vop(lambda e, dk=dk: e.tensor_copy(wosb[:, dk * D:(dk + 1) * D], wldf2[:]))
                    for gsl in range(NSL):
                        sq = gsl // 4
                        for qc in range(DC):
                            ps = psQ7 if qc % 2 == 0 else psQ8
                            for dk in range(DC):
                                top_(lambda e, dk=dk, qc=qc, gsl=gsl, ps=ps: e.matmul(
                                    ps[:], wqsb[:, dk * D + qc * P: dk * D + (qc + 1) * P],
                                    xn2b[:, dk * T + gsl * SL: dk * T + (gsl + 1) * SL],
                                    start=(dk == 0), stop=(dk == DC - 1)))
                            sop(lambda e, qc=qc, ps=ps: e.activation(qrf[:], ps[:], AF.Identity, bias=bq_sb[:, qc:qc + 1]))
                            vop(lambda e: e.tensor_scalar(qef[:], qrf[:], 0.0, None, OP.min))
                            sop(lambda e: e.activation(qef[:], qef[:], AF.Exp))
                            vop(lambda e, qc=qc: e.scalar_tensor_tensor(
                                qbf[:, qc * SL:(qc + 1) * SL], qrf[:], 0.0, qef[:], OP.max, OP.add))
                        for gc in range(DC):
                            ps = psQ7 if gc % 2 == 0 else psQ8
                            for dk in range(DC):
                                top_(lambda e, dk=dk, gc=gc, gsl=gsl, ps=ps: e.matmul(
                                    ps[:], wgsb[:, dk * D + gc * P: dk * D + (gc + 1) * P],
                                    xn2b[:, dk * T + gsl * SL: dk * T + (gsl + 1) * SL],
                                    start=(dk == 0), stop=(dk == DC - 1)))
                            sop(lambda e, gc=gc, ps=ps: e.activation(
                                gsb[:, gc * SL:(gc + 1) * SL], ps[:], AF.Sigmoid, bias=bg_sb[:, gc:gc + 1]))
                        for pp in range(8):
                            kdt = kdb[:, (sq * 8 + pp) * 2:(sq * 8 + pp) * 2 + 2]
                            top_(lambda e, kdt=kdt, pp=pp: e.matmul(
                                psZr[:], kdt, qbf[:, pp * SL:(pp + 1) * SL], start=True, stop=True))
                            vop(lambda e: e.tensor_scalar(zsb[:], psZr[:], 1e-6, None, OP.add))
                            vop(lambda e: e.reciprocal(zsb[:], zsb[:]))
                            top_(lambda e: e.matmul(psZb[:], patt2[:], zsb[:], start=True, stop=True))
                            vop(lambda e: e.tensor_copy(zbs[:], psZb[:]))
                            kvt = kvb[:, (sq * 8 + pp) * P:(sq * 8 + pp + 1) * P]
                            top_(lambda e, kvt=kvt, pp=pp: e.matmul(
                                psS7[:], kvt, qbf[:, pp * SL:(pp + 1) * SL], start=True, stop=True))
                            vop(lambda e: e.tensor_tensor(ssb[:], psS7[:], zbs[:], OP.mult))
                            vop(lambda e, pp=pp: e.tensor_tensor(
                                abf[:, pp * SL:(pp + 1) * SL], ssb[:], gsb[:, pp * SL:(pp + 1) * SL], OP.mult))
                        for oc in range(DC):
                            ps = psO7 if oc % 2 == 0 else psO8
                            for dk in range(DC):
                                top_(lambda e, dk=dk, oc=oc, ps=ps: e.matmul(
                                    ps[:], wosb[:, dk * D + oc * P: dk * D + (oc + 1) * P],
                                    abf[:, dk * SL:(dk + 1) * SL],
                                    start=(dk == 0), stop=(dk == DC - 1)))
                            sop(lambda e, oc=oc, ps=ps: e.activation(aor[:], ps[:], AF.Identity, bias=bo_sb[:, oc:oc + 1]))
                            dma(axr[:], src[oc * P:(oc + 1) * P, gsl * SL:(gsl + 1) * SL])
                            vop(lambda e: e.tensor_tensor(aor[:], aor[:], axr[:], OP.add))
                            dma(src[oc * P:(oc + 1) * P, gsl * SL:(gsl + 1) * SL], aor[:])
                    S.end_phase(blk2)

        ffn(0, xfm)
        if debug:
            with nc.Block() as blk:
                S.start_phase()
                with nc.sbuf_tensor(nm("dbt"), [P, T], F32) as dbt:
                    for dc in range(DC):
                        dma(dbt[:], xfm[dc * P:(dc + 1) * P, :])
                        S.gap()
                        dma(dbg[0][dc * P:(dc + 1) * P, :], dbt[:])
                        S.gap()
                    dma(dbgs[0:1, 0:32], scrow[0:1, 0:32])
                S.end_phase(blk)
        attn(xfm)
        if debug:
            with nc.Block() as blk:
                S.start_phase()
                with nc.sbuf_tensor(nm("dbt2"), [P, T], F32) as dbt:
                    for dc in range(DC):
                        dma(dbt[:], xfm[dc * P:(dc + 1) * P, :])
                        S.gap()
                        dma(dbg[1][dc * P:(dc + 1) * P, :], dbt[:])
                        S.gap()
                S.end_phase(blk)
        ffn(1, xfm)
        if debug:
            with nc.Block() as blk:
                S.start_phase()
                with nc.sbuf_tensor(nm("dbt3"), [P, T], F32) as dbt:
                    for dc in range(DC):
                        dma(dbt[:], xfm[dc * P:(dc + 1) * P, :])
                        S.gap()
                        dma(dbg[2][dc * P:(dc + 1) * P, :], dbt[:])
                        S.gap()
                S.end_phase(blk)

        with ExitStack() as _st:
            ost = _st.enter_context(nc.sbuf_tensor(nm("ost"), [P, DC * 1024], F32))
            oyt = _st.enter_context(nc.sbuf_tensor(nm("oyt"), [P, D], F32))
            pst_o = _st.enter_context(nc.psum_tensor(nm("pst_o"), [P, P], F32))
            blk = _st.enter_context(nc.Block())
            S.start_phase()
            for tg in range(TCN // 8):
                base = tg * 1024
                for dc in range(DC):
                    dma(ost[:, dc * 1024:(dc + 1) * 1024], xfm[dc * P:(dc + 1) * P, base:base + 1024])
                for g in range(8):
                    tc = tg * 8 + g
                    for dc in range(DC):
                        top_(lambda e, dc=dc, g=g: e.transpose(
                            pst_o[:], ost[:, dc * 1024 + g * P: dc * 1024 + (g + 1) * P], id32[:]))
                        sop(lambda e, dc=dc: e.copy(oyt[:, dc * P:(dc + 1) * P], pst_o[:]))
                    dma(y_out[tc * P:(tc + 1) * P, :], oyt[:])
            S.wait_all("sync")
            S.end_phase(blk)

    return nc


_NC_CACHE = {}


def _get_nc(debug=False):
    key = bool(debug)
    if key not in _NC_CACHE:
        _NC_CACHE[key] = build(debug=debug)
    return _NC_CACHE[key]


def _in_maps(inputs, ncores=8):
    f32 = np.float32
    x = np.ascontiguousarray(np.asarray(inputs["L0_attn_x"], dtype=f32))
    base = {
        "wq": np.ascontiguousarray(np.asarray(inputs["L0_attn_wq_w"], f32)),
        "wk": np.ascontiguousarray(np.asarray(inputs["L0_attn_wk_w"], f32)),
        "wv": np.ascontiguousarray(np.asarray(inputs["L0_attn_wv_w"], f32)),
        "wg": np.ascontiguousarray(np.asarray(inputs["L0_attn_wg_w"], f32)),
        "wo": np.ascontiguousarray(np.asarray(inputs["L0_attn_out_w"], f32)),
        "bq": np.ascontiguousarray(np.asarray(inputs["L0_attn_wq_b"], f32).reshape(D, 1)),
        "bk": np.ascontiguousarray(np.asarray(inputs["L0_attn_wk_b"], f32).reshape(1, D)),
        "bv": np.ascontiguousarray(np.asarray(inputs["L0_attn_wv_b"], f32).reshape(1, D)),
        "bg": np.ascontiguousarray(np.asarray(inputs["L0_attn_wg_b"], f32).reshape(D, 1)),
        "bo": np.ascontiguousarray(np.asarray(inputs["L0_attn_out_b"], f32).reshape(D, 1)),
        "wgate": np.ascontiguousarray(np.asarray(inputs["L0_ffn_w_gate"], f32)),
        "wup": np.ascontiguousarray(np.asarray(inputs["L0_ffn_w_up"], f32)),
        "wdown": np.ascontiguousarray(np.asarray(inputs["L0_ffn_w_down"], f32)),
        "id32": np.eye(P, dtype=f32),
        "idbf": np.eye(P).astype(ml_dtypes.bfloat16),
        "onesc": np.ones((P, 1), f32),
        "onesr": np.ones((1, P), f32),
        "onescb": np.ones((P, 1), ml_dtypes.bfloat16),
        "patt2": np.concatenate(
            [np.concatenate([np.ones((1, 64), f32), np.zeros((1, 64), f32)], axis=1),
             np.concatenate([np.zeros((1, 64), f32), np.ones((1, 64), f32)], axis=1)], axis=0),
    }
    maps = []
    for c in range(ncores):
        m = dict(base)
        m["x"] = np.ascontiguousarray(x[2 * c:2 * c + 2].reshape(T, D))
        maps.append(m)
    return maps


def kernel(**inputs):
    nc = _get_nc(debug=False)
    maps = _in_maps(inputs)
    res = run_bass_kernel_spmd(nc, maps, core_ids=list(range(8)))
    out = np.concatenate(
        [np.asarray(res.results[c]["y"]).reshape(2, 2048, D) for c in range(8)], axis=0
    )
    return out


# version 05
# speedup vs baseline: 1.4269x; 1.0768x over previous
import numpy as np
import ml_dtypes
from contextlib import ExitStack

import concourse.bass as bass
import concourse.mybir as mybir
from concourse.alu_op_type import AluOpType as OP
from concourse.bass_utils import run_bass_kernel_spmd

F32 = mybir.dt.float32
BF16 = mybir.dt.bfloat16
I16 = mybir.dt.int16
AF = mybir.ActivationFunctionType
AX = mybir.AxisListType.X

P = 128
T = 4096
D = 1024
FF = 3072
DC = D // P
FCN = FF // P
SL = 512
NSL = T // SL
TCN = T // P
RC = 1.5 * 2.0 ** 23
QA = 32767.0
QW = 7.0
ENGS = ("sync", "vector", "scalar", "tensor", "gpsimd")


class Ser:

    def __init__(self, nc, chain):
        self.nc = nc
        self.chain = chain
        self.c = 0
        self.gap_c = 0
        self.auto = True
        self.last_eng = None
        self._open = None
        self.q = None

    def _close(self):
        if self._open is not None:
            self._open["inc"] = 1
            self.c += 1
            self._open = None

    def start_phase(self):
        import os
        if os.environ.get("KDBG"):
            print("PHASE vec_i=", getattr(self, "vec_i", 0))
        self.q = {e: [] for e in ENGS}
        self.gap()

    def do(self, ekey, fn, inc=1, imm=False):
        import os
        allow = os.environ.get("KENGS", "sync,tensor,scalar").split(",")
        ok = ekey in allow
        if ekey == "vector":
            self.vec_i = getattr(self, "vec_i", 0) + 1
            if os.environ.get("KDBG") == "2" and 2440 <= self.vec_i <= 2500:
                print("vec_i", self.vec_i, "line", fn.__code__.co_firstlineno)
            lo = int(os.environ.get("KVLO", "0"))
            hi = int(os.environ.get("KVHI", "10**9").replace("10**9", "1000000000"))
            if not (lo <= self.vec_i < hi):
                ok = False
        if self.auto and (ekey != self.last_eng or not ok):
            self._close()
            self.gap_c = self.c
        self.last_eng = ekey
        rec = {"fn": fn, "tgt": self.gap_c, "inc": 0}
        if imm:
            rec["inc"] = inc
            self.c += inc
        else:
            self._open = rec
        self.q[ekey].append(rec)

    def gap(self):
        self._close()
        self.gap_c = self.c
        self.last_eng = None

    def wait_all(self, ekey):
        self._close()
        tgt = self.c
        self.q[ekey].append({"fn": None, "tgt": tgt, "inc": 0})

    def end_phase(self, blk):
        self._close()
        for e in ENGS:
            items = self.q[e]
            if not items:
                continue

            def emit(eng, items=items):
                for rec in items:
                    if rec["tgt"] > 0:
                        eng.wait_ge(self.chain, rec["tgt"])
                    if rec["fn"] is None:
                        continue
                    inst = rec["fn"](eng)
                    if rec["inc"] > 0:
                        inst.then_inc(self.chain, rec["inc"])

            getattr(blk, e)(emit)
        self.q = None


def build(debug=False):
    nc = bass.Bass("TRN2", target_bir_lowering=False, debug=False)
    import itertools
    _uid = itertools.count()

    def nm(s):
        return f"{s}_{next(_uid)}"


    x_in = nc.dram_tensor("x", [T, D], F32, kind="ExternalInput").ap()
    wq_h = nc.dram_tensor("wq", [D, D], F32, kind="ExternalInput").ap()
    wk_h = nc.dram_tensor("wk", [D, D], F32, kind="ExternalInput").ap()
    wv_h = nc.dram_tensor("wv", [D, D], F32, kind="ExternalInput").ap()
    wg_h = nc.dram_tensor("wg", [D, D], F32, kind="ExternalInput").ap()
    wo_h = nc.dram_tensor("wo", [D, D], F32, kind="ExternalInput").ap()
    bq_h = nc.dram_tensor("bq", [D, 1], F32, kind="ExternalInput").ap()
    bk_h = nc.dram_tensor("bk", [1, D], F32, kind="ExternalInput").ap()
    bv_h = nc.dram_tensor("bv", [1, D], F32, kind="ExternalInput").ap()
    bg_h = nc.dram_tensor("bg", [D, 1], F32, kind="ExternalInput").ap()
    bo_h = nc.dram_tensor("bo", [D, 1], F32, kind="ExternalInput").ap()
    wgate_h = nc.dram_tensor("wgate", [FF, D], F32, kind="ExternalInput").ap()
    wup_h = nc.dram_tensor("wup", [FF, D], F32, kind="ExternalInput").ap()
    wdown_h = nc.dram_tensor("wdown", [D, FF], F32, kind="ExternalInput").ap()
    id32_h = nc.dram_tensor("id32", [P, P], F32, kind="ExternalInput").ap()
    idbf_h = nc.dram_tensor("idbf", [P, P], BF16, kind="ExternalInput").ap()
    onesc_h = nc.dram_tensor("onesc", [P, 1], F32, kind="ExternalInput").ap()
    onesr_h = nc.dram_tensor("onesr", [1, P], F32, kind="ExternalInput").ap()
    onescb_h = nc.dram_tensor("onescb", [P, 1], BF16, kind="ExternalInput").ap()
    patt2_h = nc.dram_tensor("patt2", [2, P], F32, kind="ExternalInput").ap()

    y_out = nc.dram_tensor("y", [T, D], F32, kind="ExternalOutput").ap()

    xfm = nc.dram_tensor("xfm", [DC * P, T], F32).ap()
    mg = nc.dram_tensor("mg", [FCN * P, T], F32).ap()
    mup = nc.dram_tensor("mup", [FCN * P, T], F32).ap()
    ci = nc.dram_tensor("ci", [FCN * P, T], F32).ap()
    g2s = nc.dram_tensor("g2s", [FCN * P, T], I16).ap()
    ups = nc.dram_tensor("ups", [FCN * P, T], I16).ap()
    wgT_c = nc.dram_tensor("wgT_c", [DC * P, FF], BF16).ap()
    wupT_c = nc.dram_tensor("wupT_c", [DC * P, FF], BF16).ap()
    wdT_c = nc.dram_tensor("wdT_c", [FCN * P, D], BF16).ap()
    cc_in = nc.dram_tensor("cc_in", [1, 8], F32)
    cc_out = nc.dram_tensor("cc_out", [1, 8], F32)
    if debug:
        dbg = [nc.dram_tensor(f"dbg{i}", [DC * P, T], F32, kind="ExternalOutput").ap() for i in range(3)]
        dbgs = nc.dram_tensor("dbgs", [1, 64], F32, kind="ExternalOutput").ap()

    with ExitStack() as top:
        chain = top.enter_context(nc.semaphore("chain"))
        S = Ser(nc, chain)

        id32 = top.enter_context(nc.sbuf_tensor(nm("id32s"), [P, P], F32))
        idbf = top.enter_context(nc.sbuf_tensor(nm("idbfs"), [P, P], BF16))
        onesc = top.enter_context(nc.sbuf_tensor(nm("onescs"), [P, 1], F32))
        onesr = top.enter_context(nc.sbuf_tensor(nm("onesrs"), [1, P], F32))
        onescb = top.enter_context(nc.sbuf_tensor(nm("onescbs"), [P, 1], BF16))
        patt2 = top.enter_context(nc.sbuf_tensor(nm("patt2s"), [2, P], F32))
        crnd = top.enter_context(nc.sbuf_tensor(nm("crnd"), [P, 1], F32))
        c65k = top.enter_context(nc.sbuf_tensor(nm("c65k"), [P, 1], F32))
        ceps = top.enter_context(nc.sbuf_tensor(nm("ceps"), [1, 1], F32))
        crnd_r = top.enter_context(nc.sbuf_tensor(nm("crnd_r"), [1, 1], F32))
        czero = top.enter_context(nc.sbuf_tensor(nm("czero"), [P, 1], F32))
        scrow = top.enter_context(nc.sbuf_tensor(nm("scrow"), [1, 32], F32))
        sc = top.enter_context(nc.sbuf_tensor(nm("sc"), [P, 16], F32))
        ccl = top.enter_context(nc.sbuf_tensor(nm("ccl"), [1, 8], F32))
        ccr = top.enter_context(nc.sbuf_tensor(nm("ccr"), [1, 8], F32))
        bq_sb = top.enter_context(nc.sbuf_tensor(nm("bq_sb"), [P, DC], F32))
        bg_sb = top.enter_context(nc.sbuf_tensor(nm("bg_sb"), [P, DC], F32))
        bo_sb = top.enter_context(nc.sbuf_tensor(nm("bo_sb"), [P, DC], F32))
        bkr = top.enter_context(nc.sbuf_tensor(nm("bkr"), [1, D], F32))
        bvr = top.enter_context(nc.sbuf_tensor(nm("bvr"), [1, D], F32))
        lmax = top.enter_context(nc.sbuf_tensor(nm("lmax"), [P, 8], F32))
        t128 = top.enter_context(nc.sbuf_tensor(nm("t128"), [P, 2], F32))

        (IV_S1, IS_S1, IS_S3, IS_S2, IS_S4, IS_S5, IS_T0, IS_T1, IS_T2) = range(9)
        (SC_K1, SC_S3, SC_K2, SC_K3, SC_K4, SC_BETA) = range(6)

        def dma(dst, src):
            S.do("sync", lambda e, d=dst, s=src: e.dma_start(out=d, in_=s), 16, imm=True)

        def vop(fn):
            S.do("vector", fn)


        def vop_red(fn):
            S.do("vector", fn)
            S.gap()
        def sop(fn):
            S.do("scalar", fn)

        def top_(fn):
            S.do("tensor", fn)

        def gop(fn):
            S.do("gpsimd", fn)

        def sb_row(ap):
            return ap

        H = {}

        def cross_part_absmax(src_col, dst_row):
            top_(lambda e: e.transpose(H["psx"][0:1, 0:P], src_col, id32[:]))
            vop_red(lambda e: e.reduce_max(dst_row, H["psx"][0:1, 0:P], axis=AX, apply_absolute_value=True))

        def bcast128(src11, slot):
            top_(lambda e: e.matmul(H["pssc"][:, slot:slot + 1], onesr[0:1, :], src11, start=True, stop=True))
            sop(lambda e: e.copy(sc[:, slot:slot + 1], H["pssc"][:, slot:slot + 1]))

        def allreduce_max(n):
            S.gap()
            S.do("gpsimd", lambda e: e.dma_start(out=cc_in.ap(), in_=ccl[:]), 16, imm=True)
            S.gap()
            S.do("gpsimd", lambda e: e.collective_compute(
                "AllReduce", OP.max, replica_groups=[list(range(8))],
                ins=[cc_in.ap().opt()], outs=[cc_out.ap().opt()]), 1, imm=True)
            S.gap()
            S.do("gpsimd", lambda e: e.dma_start(out=ccr[:], in_=cc_out.ap()), 16, imm=True)
            S.gap()

        with nc.Block() as blk:
            S.start_phase()
            dma(id32[:], id32_h)
            dma(idbf[:], idbf_h)
            dma(onesc[:], onesc_h)
            dma(onesr[:], onesr_h)
            dma(onescb[:], onescb_h)
            dma(patt2[:], patt2_h)
            vop_red(lambda e: e.memset(crnd[:], RC))
            vop_red(lambda e: e.memset(c65k[:], 65536.0))
            vop_red(lambda e: e.memset(ceps[:], 1e-5))
            vop_red(lambda e: e.memset(crnd_r[:], RC))
            vop_red(lambda e: e.memset(czero[:], 0.0))
            vop_red(lambda e: e.memset(ccl[:], 0.0))
            for dc in range(DC):
                dma(bq_sb[:, dc:dc + 1], bq_h[dc * P:(dc + 1) * P, 0:1])
                dma(bg_sb[:, dc:dc + 1], bg_h[dc * P:(dc + 1) * P, 0:1])
                dma(bo_sb[:, dc:dc + 1], bo_h[dc * P:(dc + 1) * P, 0:1])
            dma(bkr[:], bk_h)
            dma(bvr[:], bv_h)
            S.end_phase(blk)

        with ExitStack() as _st:
            xtm = _st.enter_context(nc.sbuf_tensor(nm("xtm"), [P, D], F32))
            stf = _st.enter_context(nc.sbuf_tensor(nm("stf"), [P, DC * 1024], F32))
            pst_in = _st.enter_context(nc.psum_tensor(nm("pst_in"), [P, P], F32))
            blk = _st.enter_context(nc.Block())
            S.start_phase()
            for tc in range(TCN):
                dma(xtm[:], x_in[tc * P:(tc + 1) * P, :])
                g = tc % 8
                for dc in range(DC):
                    top_(lambda e, dc=dc: e.transpose(pst_in[:], xtm[:, dc * P:(dc + 1) * P], id32[:]))
                    sop(lambda e, dc=dc, g=g: e.copy(stf[:, dc * 1024 + g * P: dc * 1024 + (g + 1) * P], pst_in[:]))
                if g == 7:
                    base = (tc - 7) * P
                    for dc in range(DC):
                        dma(xfm[dc * P:(dc + 1) * P, base:base + 1024], stf[:, dc * 1024:(dc + 1) * 1024])
            S.end_phase(blk)

        def ln_stats(src, rows_a, rows_b):
            with ExitStack() as _st:
                lnx = _st.enter_context(nc.sbuf_tensor(nm("lnx"), [P, SL * 3], F32))
                lnq = _st.enter_context(nc.sbuf_tensor(nm("lnq"), [P, SL], F32))
                lnr = _st.enter_context(nc.sbuf_tensor(nm("lnr"), [1, SL * 3], F32))
                psS = _st.enter_context(nc.psum_tensor(nm("psS"), [1, SL], F32))
                psQ = _st.enter_context(nc.psum_tensor(nm("psQ"), [1, SL], F32))
                blk = _st.enter_context(nc.Block())
                S.start_phase()
                for sl in range(NSL):
                    for dc in range(DC):
                        xb = lnx[:, (dc % 3) * SL:(dc % 3 + 1) * SL]
                        dma(xb, src[dc * P:(dc + 1) * P, sl * SL:(sl + 1) * SL])
                        sop(lambda e, xb=xb: e.activation(lnq[:], xb, AF.Square))
                        top_(lambda e, xb=xb, dc=dc: e.matmul(
                            psS[0:1, :], onesc[:], xb, start=(dc == 0), stop=(dc == DC - 1)))
                        top_(lambda e, dc=dc: e.matmul(
                            psQ[0:1, :], onesc[:], lnq[:], start=(dc == 0), stop=(dc == DC - 1)))
                    mur = lnr[0:1, 0:SL]
                    tmp = lnr[0:1, SL:2 * SL]
                    sdr = lnr[0:1, 2 * SL:3 * SL]
                    vop(lambda e, mur=mur: e.tensor_scalar(mur, psS[0:1, :], 1.0 / D, None, OP.mult))
                    vop(lambda e, tmp=tmp: e.tensor_scalar(tmp, psQ[0:1, :], 1.0 / D, None, OP.mult))
                    vop(lambda e, mur=mur, sdr=sdr: e.tensor_tensor(sdr, mur, mur, OP.mult))
                    vop(lambda e, tmp=tmp, sdr=sdr: e.tensor_tensor(tmp, tmp, sdr, OP.subtract))
                    sop(lambda e, sdr=sdr, tmp=tmp: e.activation(sdr, tmp, AF.Sqrt, bias=ceps[:]))
                    vop_red(lambda e, sdr=sdr: e.reciprocal(sdr, sdr))
                    vop(lambda e, sdr=sdr, sl=sl: e.tensor_copy(rows_a[0:1, sl * SL:(sl + 1) * SL], sdr))
                    vop(lambda e, mur=mur, sdr=sdr, sl=sl: e.tensor_tensor(
                        rows_b[0:1, sl * SL:(sl + 1) * SL], mur, sdr, OP.mult))
                S.end_phase(blk)

        def ln_absmax_and_ar1(src, rows_a, rows_b):
            with ExitStack() as _st:
                amx = _st.enter_context(nc.sbuf_tensor(nm("amx"), [P, SL * 3], F32))
                amu = _st.enter_context(nc.sbuf_tensor(nm("amu"), [P, SL], F32))
                amt = _st.enter_context(nc.sbuf_tensor(nm("amt"), [P, 2], F32))
                psA = _st.enter_context(nc.psum_tensor(nm("psA"), [P, SL], F32))
                psB = _st.enter_context(nc.psum_tensor(nm("psB"), [P, SL], F32))
                psx_l = _st.enter_context(nc.psum_tensor(nm("psx_"), [1, P], F32))
                pssc_l = _st.enter_context(nc.psum_tensor(nm("pssc_"), [P, 8], F32))
                blk = _st.enter_context(nc.Block())
                H["psx"], H["pssc"] = psx_l, pssc_l
                S.start_phase()
                vop_red(lambda e: e.memset(lmax[:, 0:1], 0.0))
                for sl in range(NSL):
                    top_(lambda e, sl=sl: e.matmul(psA[:], onesr[0:1, :], rows_a[0:1, sl * SL:(sl + 1) * SL], start=True, stop=True))
                    top_(lambda e, sl=sl: e.matmul(psB[:], onesr[0:1, :], rows_b[0:1, sl * SL:(sl + 1) * SL], start=True, stop=True))
                    for dc in range(DC):
                        xb = amx[:, (dc % 3) * SL:(dc % 3 + 1) * SL]
                        dma(xb, src[dc * P:(dc + 1) * P, sl * SL:(sl + 1) * SL])
                        vop(lambda e, xb=xb: e.tensor_tensor(amu[:], xb, psA[:], OP.mult))
                        vop(lambda e: e.tensor_tensor(amu[:], amu[:], psB[:], OP.subtract))
                        vop_red(lambda e: e.reduce_max(amt[:, 0:1], amu[:], axis=AX, apply_absolute_value=True))
                        vop(lambda e: e.tensor_tensor(lmax[:, 0:1], lmax[:, 0:1], amt[:, 0:1], OP.max))
                cross_part_absmax(lmax[:, 0:1], scrow[0:1, IS_T0:IS_T0 + 1])
                vop(lambda e: e.tensor_copy(ccl[0:1, 0:1], scrow[0:1, IS_T0:IS_T0 + 1]))
                allreduce_max(1)
                vop(lambda e: e.tensor_scalar(scrow[0:1, IS_S1:IS_S1 + 1], ccr[0:1, 0:1], 1.0 / QA, 1e-12, OP.mult, OP.max))
                vop_red(lambda e: e.reciprocal(scrow[0:1, IV_S1:IV_S1 + 1], scrow[0:1, IS_S1:IS_S1 + 1]))
                vop(lambda e: e.tensor_scalar(rows_a[:], rows_a[:], scrow[0:1, IV_S1:IV_S1 + 1], None, OP.mult))
                vop(lambda e: e.tensor_scalar(rows_b[:], rows_b[:], scrow[0:1, IV_S1:IV_S1 + 1], crnd_r[:], OP.mult, OP.subtract))
                S.end_phase(blk)

        def xq_pass(src, dst_bf, rows_a, rows_b):
            with ExitStack() as _st:
                xqx = _st.enter_context(nc.sbuf_tensor(nm("xqx"), [P, SL * 3], F32))
                xqu = _st.enter_context(nc.sbuf_tensor(nm("xqu"), [P, SL], F32))
                psA = _st.enter_context(nc.psum_tensor(nm("psA2"), [P, SL], F32))
                psB = _st.enter_context(nc.psum_tensor(nm("psB2"), [P, SL], F32))
                blk = _st.enter_context(nc.Block())
                S.start_phase()
                for sl in range(NSL):
                    top_(lambda e, sl=sl: e.matmul(psA[:], onesr[0:1, :], rows_a[0:1, sl * SL:(sl + 1) * SL], start=True, stop=True))
                    top_(lambda e, sl=sl: e.matmul(psB[:], onesr[0:1, :], rows_b[0:1, sl * SL:(sl + 1) * SL], start=True, stop=True))
                    for dc in range(DC):
                        xb = xqx[:, (dc % 3) * SL:(dc % 3 + 1) * SL]
                        dma(xb, src[dc * P:(dc + 1) * P, sl * SL:(sl + 1) * SL])
                        vop(lambda e, xb=xb: e.tensor_tensor(xqu[:], xb, psA[:], OP.mult))
                        vop(lambda e: e.tensor_tensor(xqu[:], xqu[:], psB[:], OP.subtract))
                        vop(lambda e, dc=dc, sl=sl: e.tensor_scalar(
                            dst_bf[:, dc * T + sl * SL: dc * T + (sl + 1) * SL], xqu[:], RC, None, OP.subtract))
                S.end_phase(blk)

        def xn_pass(src, dst_bf, rows_a, rows_b):
            with ExitStack() as _st:
                xnx = _st.enter_context(nc.sbuf_tensor(nm("xnx"), [P, SL * 3], F32))
                xnu = _st.enter_context(nc.sbuf_tensor(nm("xnu"), [P, SL], F32))
                psA = _st.enter_context(nc.psum_tensor(nm("psA3"), [P, SL], F32))
                psB = _st.enter_context(nc.psum_tensor(nm("psB3"), [P, SL], F32))
                blk = _st.enter_context(nc.Block())
                S.start_phase()
                for sl in range(NSL):
                    top_(lambda e, sl=sl: e.matmul(psA[:], onesr[0:1, :], rows_a[0:1, sl * SL:(sl + 1) * SL], start=True, stop=True))
                    top_(lambda e, sl=sl: e.matmul(psB[:], onesr[0:1, :], rows_b[0:1, sl * SL:(sl + 1) * SL], start=True, stop=True))
                    for dc in range(DC):
                        xb = xnx[:, (dc % 3) * SL:(dc % 3 + 1) * SL]
                        dma(xb, src[dc * P:(dc + 1) * P, sl * SL:(sl + 1) * SL])
                        vop(lambda e, xb=xb: e.tensor_tensor(xnu[:], xb, psA[:], OP.mult))
                        vop(lambda e, dc=dc, sl=sl: e.tensor_tensor(
                            dst_bf[:, dc * T + sl * SL: dc * T + (sl + 1) * SL], xnu[:], psB[:], OP.subtract))
                S.end_phase(blk)

        def ffn(inv, src):
            with nc.sbuf_tensor(f"xqib{inv}", [P, DC * T], BF16) as xqib:
                with nc.sbuf_tensor(nm("rows_a"), [1, T], F32) as rows_a, nc.sbuf_tensor(nm("rows_b"), [1, T], F32) as rows_b:
                    ln_stats(src, rows_a, rows_b)
                    ln_absmax_and_ar1(src, rows_a, rows_b)
                    xq_pass(src, xqib, rows_a, rows_b)

                with ExitStack() as _st:
                    wnat = _st.enter_context(nc.sbuf_tensor(nm("wnat"), [P, D * 2], F32))
                    wqb = _st.enter_context(nc.sbuf_tensor(nm("wqb"), [P, D], F32))
                    wpb = _st.enter_context(nc.sbuf_tensor(nm("wpb"), [P, D], BF16))
                    bm = _st.enter_context(nc.sbuf_tensor(nm("bm"), [P, 64], F32))
                    wT = _st.enter_context(nc.sbuf_tensor(nm("wT"), [P, 2 * DC * P], BF16))
                    mst = _st.enter_context(nc.sbuf_tensor(nm("mst"), [P, T], F32))
                    mtmp = _st.enter_context(nc.sbuf_tensor(nm("mtmp"), [P, 2], F32))
                    psM = _st.enter_context(nc.psum_tensor(nm("psM"), [P, SL], F32))
                    psM2 = _st.enter_context(nc.psum_tensor(nm("psM2"), [P, SL], F32))
                    psW = _st.enter_context(nc.psum_tensor(nm("psW"), [P, P], BF16))
                    psx_l = _st.enter_context(nc.psum_tensor(nm("psx4"), [1, P], F32))
                    pssc_l = _st.enter_context(nc.psum_tensor(nm("pssc4"), [P, 8], F32))
                    blk = _st.enter_context(nc.Block())
                    H["psx"], H["pssc"] = psx_l, pssc_l
                    S.start_phase()
                    for wi, (wsrc, spill, wtc) in enumerate(
                        [(wgate_h, mg, wgT_c), (wup_h, mup, wupT_c)]
                    ):
                        vop_red(lambda e, wi=wi: e.memset(lmax[:, 2 + 2 * wi:3 + 2 * wi], 0.0))
                        for fc in range(FCN):
                            wr = wT[:, (fc % 2) * DC * P:(fc % 2 + 1) * DC * P]
                            if inv == 0:
                                wn = wnat[:, (fc % 2) * D:(fc % 2 + 1) * D]
                                dma(wn, wsrc[fc * P:(fc + 1) * P, :])
                                vop_red(lambda e, wn=wn: e.reduce_max(
                                    bm[:, 0:32], wn.rearrange("p (b e) -> p b e", e=32),
                                    axis=AX, apply_absolute_value=True))
                                vop(lambda e: e.tensor_scalar(bm[:, 0:32], bm[:, 0:32], 1.0 / QW, 1e-12, OP.mult, OP.max))
                                vop_red(lambda e: e.reciprocal(bm[:, 32:64], bm[:, 0:32]))
                                vop(lambda e, wn=wn: e.tensor_tensor(
                                    wqb[:].rearrange("p (b e) -> p b e", e=32),
                                    wn.rearrange("p (b e) -> p b e", e=32),
                                    bm[:, 32:64].unsqueeze(2).broadcast_to([P, 32, 32]), OP.mult))
                                vop(lambda e: e.tensor_scalar(wqb[:], wqb[:], RC, RC, OP.add, OP.subtract))
                                vop(lambda e: e.tensor_tensor(
                                    wpb[:].rearrange("p (b e) -> p b e", e=32),
                                    wqb[:].rearrange("p (b e) -> p b e", e=32),
                                    bm[:, 0:32].unsqueeze(2).broadcast_to([P, 32, 32]), OP.mult))
                                for dc in range(DC):
                                    top_(lambda e, dc=dc: e.transpose(psW[:], wpb[:, dc * P:(dc + 1) * P], idbf[:]))
                                    sop(lambda e, dc=dc, wr=wr: e.copy(wr[:, dc * P:(dc + 1) * P], psW[:]))
                                    dma(wtc[dc * P:(dc + 1) * P, fc * P:(fc + 1) * P], wr[:, dc * P:(dc + 1) * P])
                            else:
                                for dc in range(DC):
                                    dma(wr[:, dc * P:(dc + 1) * P], wtc[dc * P:(dc + 1) * P, fc * P:(fc + 1) * P])
                            for sl in range(NSL):
                                ps = psM if sl % 2 == 0 else psM2
                                for dc in range(DC):
                                    top_(lambda e, dc=dc, sl=sl, wr=wr, ps=ps: e.matmul(
                                        ps[:], wr[:, dc * P:(dc + 1) * P],
                                        xqib[:, dc * T + sl * SL: dc * T + (sl + 1) * SL],
                                        start=(dc == 0), stop=(dc == DC - 1)))
                                vop(lambda e, sl=sl, ps=ps: e.tensor_copy(
                                    mst[:, sl * SL:(sl + 1) * SL], ps[:]))
                            vop_red(lambda e: e.reduce_max(mtmp[:, 0:1], mst[:], axis=AX, apply_absolute_value=True))
                            vop(lambda e, wi=wi: e.tensor_tensor(lmax[:, 2 + 2 * wi:3 + 2 * wi], lmax[:, 2 + 2 * wi:3 + 2 * wi], mtmp[:, 0:1], OP.max))
                            dma(spill[fc * P:(fc + 1) * P, :], mst[:])
                        cross_part_absmax(lmax[:, 2 + 2 * wi:3 + 2 * wi], scrow[0:1, IS_T0 + wi:IS_T0 + wi + 1])
                        vop(lambda e, wi=wi: e.tensor_copy(ccl[0:1, wi:wi + 1], scrow[0:1, IS_T0 + wi:IS_T0 + wi + 1]))

                    if inv == 0:
                        with ExitStack() as _st:
                            wd3 = _st.enter_context(nc.sbuf_tensor(nm("wd3"), [P, FF], F32))
                            wdq = _st.enter_context(nc.sbuf_tensor(nm("wdq"), [P, FF], F32))
                            wdb = _st.enter_context(nc.sbuf_tensor(nm("wdb"), [P, FF], BF16))
                            bm3 = _st.enter_context(nc.sbuf_tensor(nm("bm3"), [P, 192], F32))
                            wds = _st.enter_context(nc.sbuf_tensor(nm("wds"), [P, P], BF16))
                            for r in range(DC):
                                dma(wd3[:], wdown_h[r * P:(r + 1) * P, :])
                                vop_red(lambda e: e.reduce_max(
                                    bm3[:, 0:96], wd3[:].rearrange("p (b e) -> p b e", e=32),
                                    axis=AX, apply_absolute_value=True))
                                vop(lambda e: e.tensor_scalar(bm3[:, 0:96], bm3[:, 0:96], 1.0 / QW, 1e-12, OP.mult, OP.max))
                                vop_red(lambda e: e.reciprocal(bm3[:, 96:192], bm3[:, 0:96]))
                                vop(lambda e: e.tensor_tensor(
                                    wdq[:].rearrange("p (b e) -> p b e", e=32),
                                    wd3[:].rearrange("p (b e) -> p b e", e=32),
                                    bm3[:, 96:192].unsqueeze(2).broadcast_to([P, 96, 32]), OP.mult))
                                vop(lambda e: e.tensor_scalar(wdq[:], wdq[:], RC, RC, OP.add, OP.subtract))
                                vop(lambda e: e.tensor_tensor(
                                    wdb[:].rearrange("p (b e) -> p b e", e=32),
                                    wdq[:].rearrange("p (b e) -> p b e", e=32),
                                    bm3[:, 0:96].unsqueeze(2).broadcast_to([P, 96, 32]), OP.mult))
                                for fc in range(FCN):
                                    top_(lambda e, fc=fc: e.transpose(psW[:], wdb[:, fc * P:(fc + 1) * P], idbf[:]))
                                    sop(lambda e: e.copy(wds[:], psW[:]))
                                    dma(wdT_c[fc * P:(fc + 1) * P, r * P:(r + 1) * P], wds[:])

                    allreduce_max(2)
                    vop(lambda e: e.tensor_scalar(ccr[0:1, 0:2], ccr[0:1, 0:2], scrow[0:1, IS_S1:IS_S1 + 1], None, OP.mult))
                    vop(lambda e: e.tensor_scalar(scrow[0:1, IS_S3:IS_S3 + 1], ccr[0:1, 0:1], 1.0 / QA, 1e-12, OP.mult, OP.max))
                    vop(lambda e: e.tensor_scalar(scrow[0:1, IS_S2:IS_S2 + 1], ccr[0:1, 1:2], 1.0 / QA, 1e-12, OP.mult, OP.max))
                    vop_red(lambda e: e.reciprocal(scrow[0:1, IS_T1:IS_T1 + 1], scrow[0:1, IS_S3:IS_S3 + 1]))
                    vop(lambda e: e.tensor_scalar(scrow[0:1, IS_T1:IS_T1 + 1], scrow[0:1, IS_T1:IS_T1 + 1], scrow[0:1, IS_S1:IS_S1 + 1], None, OP.mult))
                    bcast128(scrow[0:1, IS_T1:IS_T1 + 1], SC_K1)
                    vop_red(lambda e: e.reciprocal(scrow[0:1, IS_T2:IS_T2 + 1], scrow[0:1, IS_S2:IS_S2 + 1]))
                    vop(lambda e: e.tensor_scalar(scrow[0:1, IS_T2:IS_T2 + 1], scrow[0:1, IS_T2:IS_T2 + 1], scrow[0:1, IS_S1:IS_S1 + 1], None, OP.mult))
                    bcast128(scrow[0:1, IS_T2:IS_T2 + 1], SC_K2)
                    bcast128(scrow[0:1, IS_S3:IS_S3 + 1], SC_S3)
                    S.end_phase(blk)

            with ExitStack() as _st:
                e1m = _st.enter_context(nc.sbuf_tensor(nm("e1m"), [P, T], F32))
                e1a = _st.enter_context(nc.sbuf_tensor(nm("e1a"), [P, T], F32))
                e1b = _st.enter_context(nc.sbuf_tensor(nm("e1b"), [P, T], F32))
                e1c = _st.enter_context(nc.sbuf_tensor(nm("e1c"), [P, T], F32))
                psx_l = _st.enter_context(nc.psum_tensor(nm("psx5"), [1, P], F32))
                pssc_l = _st.enter_context(nc.psum_tensor(nm("pssc5"), [P, 8], F32))
                blk = _st.enter_context(nc.Block())
                H["psx"], H["pssc"] = psx_l, pssc_l
                S.start_phase()
                vop_red(lambda e: e.memset(lmax[:, 6:7], 0.0))
                for fc in range(FCN):
                    dma(e1m[:], mg[fc * P:(fc + 1) * P, :])
                    sop(lambda e: e.activation(e1a[:], e1m[:], AF.Identity, bias=crnd[:], scale=sc[:, SC_K1:SC_K1 + 1]))
                    vop(lambda e: e.tensor_scalar(e1a[:], e1a[:], RC, None, OP.subtract))
                    sop(lambda e: e.activation(e1b[:], e1a[:], AF.Sigmoid, bias=czero[:], scale=sc[:, SC_S3:SC_S3 + 1]))
                    sop(lambda e: e.activation(e1b[:], e1b[:], AF.Identity, bias=crnd[:], scale=c65k[:]))
                    vop(lambda e: e.tensor_scalar(e1b[:], e1b[:], RC, 65535.0, OP.subtract, OP.min))
                    vop(lambda e: e.tensor_tensor(e1c[:], e1a[:], e1b[:], OP.mult))
                    vop_red(lambda e: e.reduce_max(lmax[:, 7:8], e1c[:], axis=AX, apply_absolute_value=True))
                    vop(lambda e: e.tensor_tensor(lmax[:, 6:7], lmax[:, 6:7], lmax[:, 7:8], OP.max))
                    dma(ci[fc * P:(fc + 1) * P, :], e1c[:])
                cross_part_absmax(lmax[:, 6:7], scrow[0:1, IS_T0:IS_T0 + 1])
                vop(lambda e: e.tensor_copy(ccl[0:1, 0:1], scrow[0:1, IS_T0:IS_T0 + 1]))
                allreduce_max(1)
                vop(lambda e: e.tensor_scalar(scrow[0:1, IS_T1:IS_T1 + 1], scrow[0:1, IS_S3:IS_S3 + 1], 1.0 / 65536.0, None, OP.mult))
                vop(lambda e: e.tensor_scalar(scrow[0:1, IS_T2:IS_T2 + 1], ccr[0:1, 0:1], scrow[0:1, IS_T1:IS_T1 + 1], None, OP.mult))
                vop(lambda e: e.tensor_scalar(scrow[0:1, IS_S4:IS_S4 + 1], scrow[0:1, IS_T2:IS_T2 + 1], 1.0 / QA, 1e-12, OP.mult, OP.max))
                vop_red(lambda e: e.reciprocal(scrow[0:1, IS_T2:IS_T2 + 1], scrow[0:1, IS_S4:IS_S4 + 1]))
                vop(lambda e: e.tensor_scalar(scrow[0:1, IS_T2:IS_T2 + 1], scrow[0:1, IS_T2:IS_T2 + 1], scrow[0:1, IS_T1:IS_T1 + 1], None, OP.mult))
                bcast128(scrow[0:1, IS_T2:IS_T2 + 1], SC_K3)
                S.end_phase(blk)

            with ExitStack() as _st:
                e2a = _st.enter_context(nc.sbuf_tensor(nm("e2a"), [P, T], F32))
                e2b = _st.enter_context(nc.sbuf_tensor(nm("e2b"), [P, T], F32))
                e2c = _st.enter_context(nc.sbuf_tensor(nm("e2c"), [P, T], F32))
                e2i = _st.enter_context(nc.sbuf_tensor(nm("e2i"), [P, T], I16))
                e2j = _st.enter_context(nc.sbuf_tensor(nm("e2j"), [P, T], I16))
                psx_l = _st.enter_context(nc.psum_tensor(nm("psx6"), [1, P], F32))
                pssc_l = _st.enter_context(nc.psum_tensor(nm("pssc6"), [P, 8], F32))
                blk = _st.enter_context(nc.Block())
                H["psx"], H["pssc"] = psx_l, pssc_l
                S.start_phase()
                vop_red(lambda e: e.memset(lmax[:, 6:7], 0.0))
                for fc in range(FCN):
                    dma(e2a[:], ci[fc * P:(fc + 1) * P, :])
                    sop(lambda e: e.activation(e2a[:], e2a[:], AF.Identity, bias=crnd[:], scale=sc[:, SC_K3:SC_K3 + 1]))
                    vop(lambda e: e.tensor_scalar(e2a[:], e2a[:], RC, None, OP.subtract))
                    vop(lambda e: e.tensor_copy(e2i[:], e2a[:]))
                    dma(g2s[fc * P:(fc + 1) * P, :], e2i[:])
                    dma(e2b[:], mup[fc * P:(fc + 1) * P, :])
                    sop(lambda e: e.activation(e2b[:], e2b[:], AF.Identity, bias=crnd[:], scale=sc[:, SC_K2:SC_K2 + 1]))
                    vop(lambda e: e.tensor_scalar(e2b[:], e2b[:], RC, None, OP.subtract))
                    vop(lambda e: e.tensor_copy(e2j[:], e2b[:]))
                    dma(ups[fc * P:(fc + 1) * P, :], e2j[:])
                    vop(lambda e: e.tensor_tensor(e2c[:], e2a[:], e2b[:], OP.mult))
                    vop_red(lambda e: e.reduce_max(lmax[:, 7:8], e2c[:], axis=AX, apply_absolute_value=True))
                    vop(lambda e: e.tensor_tensor(lmax[:, 6:7], lmax[:, 6:7], lmax[:, 7:8], OP.max))
                cross_part_absmax(lmax[:, 6:7], scrow[0:1, IS_T0:IS_T0 + 1])
                vop(lambda e: e.tensor_copy(ccl[0:1, 0:1], scrow[0:1, IS_T0:IS_T0 + 1]))
                allreduce_max(1)
                vop(lambda e: e.tensor_scalar(scrow[0:1, IS_T1:IS_T1 + 1], scrow[0:1, IS_S4:IS_S4 + 1], scrow[0:1, IS_S2:IS_S2 + 1], None, OP.mult))
                vop(lambda e: e.tensor_scalar(scrow[0:1, IS_T2:IS_T2 + 1], ccr[0:1, 0:1], scrow[0:1, IS_T1:IS_T1 + 1], None, OP.mult))
                vop(lambda e: e.tensor_scalar(scrow[0:1, IS_S5:IS_S5 + 1], scrow[0:1, IS_T2:IS_T2 + 1], 1.0 / QA, 1e-12, OP.mult, OP.max))
                vop_red(lambda e: e.reciprocal(scrow[0:1, IS_T2:IS_T2 + 1], scrow[0:1, IS_S5:IS_S5 + 1]))
                vop(lambda e: e.tensor_scalar(scrow[0:1, IS_T2:IS_T2 + 1], scrow[0:1, IS_T2:IS_T2 + 1], scrow[0:1, IS_T1:IS_T1 + 1], None, OP.mult))
                bcast128(scrow[0:1, IS_T2:IS_T2 + 1], SC_K4)
                vop(lambda e: e.tensor_scalar(scrow[0:1, IS_T1:IS_T1 + 1], scrow[0:1, IS_S5:IS_S5 + 1], 0.5, None, OP.mult))
                bcast128(scrow[0:1, IS_T1:IS_T1 + 1], SC_BETA)
                S.end_phase(blk)

            with ExitStack() as _st:
                wdTs = _st.enter_context(nc.sbuf_tensor(nm("wdTs"), [P, FCN * D], BF16))
                oib = _st.enter_context(nc.sbuf_tensor(nm("oib"), [P, FCN * SL], BF16))
                e3g = _st.enter_context(nc.sbuf_tensor(nm("e3g"), [P, FCN * SL], I16))
                e3u = _st.enter_context(nc.sbuf_tensor(nm("e3u"), [P, FCN * SL], I16))
                e3d = _st.enter_context(nc.sbuf_tensor(nm("e3d"), [P, FCN * SL], F32))
                e3x = _st.enter_context(nc.sbuf_tensor(nm("e3x"), [P, SL], F32))
                e3s = _st.enter_context(nc.sbuf_tensor(nm("e3s"), [P, SL], F32))
                psD = _st.enter_context(nc.psum_tensor(nm("psD"), [P, SL], F32))
                psD2 = _st.enter_context(nc.psum_tensor(nm("psD2"), [P, SL], F32))
                blk = _st.enter_context(nc.Block())
                S.start_phase()
                for fc in range(FCN):
                    dma(wdTs[:, fc * D:(fc + 1) * D], wdT_c[fc * P:(fc + 1) * P, :])
                for sl in range(NSL):
                    for fc in range(FCN):
                        dma(e3g[:, fc * SL:(fc + 1) * SL], g2s[fc * P:(fc + 1) * P, sl * SL:(sl + 1) * SL])
                        dma(e3u[:, fc * SL:(fc + 1) * SL], ups[fc * P:(fc + 1) * P, sl * SL:(sl + 1) * SL])
                    vop(lambda e: e.tensor_tensor(e3d[:], e3g[:], e3u[:], OP.mult))
                    sop(lambda e: e.activation(e3d[:], e3d[:], AF.Identity, bias=crnd[:], scale=sc[:, SC_K4:SC_K4 + 1]))
                    vop(lambda e: e.tensor_scalar(oib[:], e3d[:], RC, None, OP.subtract))
                    for dc in range(DC):
                        ps = psD if dc % 2 == 0 else psD2
                        for fc in range(FCN):
                            top_(lambda e, fc=fc, dc=dc, ps=ps: e.matmul(
                                ps[:], wdTs[:, fc * D + dc * P: fc * D + (dc + 1) * P],
                                oib[:, fc * SL:(fc + 1) * SL],
                                start=(fc == 0), stop=(fc == FCN - 1)))
                        dma(e3x[:], src[dc * P:(dc + 1) * P, sl * SL:(sl + 1) * SL])
                        sop(lambda e, ps=ps: e.activation(e3s[:], ps[:], AF.Identity, bias=czero[:], scale=sc[:, SC_BETA:SC_BETA + 1]))
                        vop(lambda e: e.tensor_tensor(e3s[:], e3s[:], e3x[:], OP.add))
                        dma(src[dc * P:(dc + 1) * P, sl * SL:(sl + 1) * SL], e3s[:])
                S.end_phase(blk)

        def attn(src):
            with ExitStack() as _st:
                xn2b = _st.enter_context(nc.sbuf_tensor(nm("xn2b"), [P, DC * T], BF16))
                kvb = _st.enter_context(nc.sbuf_tensor(nm("kvb"), [P, 16 * P], BF16))
                kdb = _st.enter_context(nc.sbuf_tensor(nm("kdb"), [P, 32], BF16))
                with nc.sbuf_tensor(nm("rows_a"), [1, T], F32) as rows_a, nc.sbuf_tensor(nm("rows_b"), [1, T], F32) as rows_b:
                    ln_stats(src, rows_a, rows_b)
                    xn_pass(src, xn2b, rows_a, rows_b)

                with ExitStack() as _st:
                    wkb = _st.enter_context(nc.sbuf_tensor(nm("wkb"), [P, DC * D], BF16))
                    wvb = _st.enter_context(nc.sbuf_tensor(nm("wvb"), [P, DC * D], BF16))
                    wldf = _st.enter_context(nc.sbuf_tensor(nm("wldf"), [P, D], F32))
                    bkb = _st.enter_context(nc.sbuf_tensor(nm("bkb"), [P, D], F32))
                    bvb = _st.enter_context(nc.sbuf_tensor(nm("bvb"), [P, D], F32))
                    ktf = _st.enter_context(nc.sbuf_tensor(nm("ktf"), [P, SL], F32))
                    kte = _st.enter_context(nc.sbuf_tensor(nm("kte"), [P, SL], F32))
                    ktb = _st.enter_context(nc.sbuf_tensor(nm("ktb"), [P, D], BF16))
                    vtb = _st.enter_context(nc.sbuf_tensor(nm("vtb"), [P, D], BF16))
                    psK = _st.enter_context(nc.psum_tensor(nm("psK"), [P, SL], F32))
                    psV = _st.enter_context(nc.psum_tensor(nm("psV"), [P, SL], F32))
                    psKV0 = _st.enter_context(nc.psum_tensor(nm("psKV0"), [P, SL], F32))
                    psKV1 = _st.enter_context(nc.psum_tensor(nm("psKV1"), [P, SL], F32))
                    psKV2 = _st.enter_context(nc.psum_tensor(nm("psKV2"), [P, SL], F32))
                    psKV3 = _st.enter_context(nc.psum_tensor(nm("psKV3"), [P, SL], F32))
                    psBC = _st.enter_context(nc.psum_tensor(nm("psBC"), [P, SL], F32))
                    blk = _st.enter_context(nc.Block())
                    psKV = [psKV0, psKV1, psKV2, psKV3]
                    S.start_phase()
                    for dk in range(DC):
                        dma(wldf[:], wk_h[dk * P:(dk + 1) * P, :])
                        vop(lambda e, dk=dk: e.tensor_copy(wkb[:, dk * D:(dk + 1) * D], wldf[:]))
                        dma(wldf[:], wv_h[dk * P:(dk + 1) * P, :])
                        vop(lambda e, dk=dk: e.tensor_copy(wvb[:, dk * D:(dk + 1) * D], wldf[:]))
                    for h in range(2):
                        top_(lambda e, h=h: e.matmul(psBC[:], onesr[0:1, :], bkr[0:1, h * SL:(h + 1) * SL], start=True, stop=True))
                        sop(lambda e, h=h: e.copy(bkb[:, h * SL:(h + 1) * SL], psBC[:]))
                        top_(lambda e, h=h: e.matmul(psBC[:], onesr[0:1, :], bvr[0:1, h * SL:(h + 1) * SL], start=True, stop=True))
                        sop(lambda e, h=h: e.copy(bvb[:, h * SL:(h + 1) * SL], psBC[:]))
                    for tc in range(TCN):
                        sq, ti = tc // 16, tc % 16
                        for half in range(2):
                            for dk in range(DC):
                                top_(lambda e, dk=dk, tc=tc, half=half: e.matmul(
                                    psK[:], xn2b[:, dk * T + tc * P:dk * T + (tc + 1) * P],
                                    wkb[:, dk * D + half * SL: dk * D + (half + 1) * SL],
                                    start=(dk == 0), stop=(dk == DC - 1)))
                            vop(lambda e, half=half: e.tensor_tensor(ktf[:], psK[:], bkb[:, half * SL:(half + 1) * SL], OP.add))
                            vop(lambda e: e.tensor_scalar(kte[:], ktf[:], 0.0, None, OP.min))
                            sop(lambda e: e.activation(kte[:], kte[:], AF.Exp))
                            vop(lambda e, half=half: e.scalar_tensor_tensor(
                                ktb[:, half * SL:(half + 1) * SL], ktf[:], 0.0, kte[:], OP.max, OP.add))
                            for dk in range(DC):
                                top_(lambda e, dk=dk, tc=tc, half=half: e.matmul(
                                    psV[:], xn2b[:, dk * T + tc * P:dk * T + (tc + 1) * P],
                                    wvb[:, dk * D + half * SL: dk * D + (half + 1) * SL],
                                    start=(dk == 0), stop=(dk == DC - 1)))
                            vop(lambda e, half=half: e.tensor_tensor(
                                vtb[:, half * SL:(half + 1) * SL], psV[:], bvb[:, half * SL:(half + 1) * SL], OP.add))
                        for p in range(8):
                            ps = psKV[p // 2]
                            off = (p % 2) * 256
                            top_(lambda e, p=p, ps=ps, off=off, ti=ti: e.matmul(
                                ps[:, off:off + P], ktb[:, p * P:(p + 1) * P], vtb[:, p * P:(p + 1) * P],
                                start=(ti == 0), stop=(ti == 15), skip_group_check=True))
                            top_(lambda e, p=p, ps=ps, off=off, ti=ti: e.matmul(
                                ps[:, off + P:off + P + 1], ktb[:, p * P:(p + 1) * P], onescb[:],
                                start=(ti == 0), stop=(ti == 15), skip_group_check=True))
                        if ti == 15:
                            for p in range(8):
                                ps = psKV[p // 2]
                                off = (p % 2) * 256
                                kvt = kvb[:, (sq * 8 + p) * P:(sq * 8 + p + 1) * P]
                                vop_red(lambda e, kvt=kvt: e.memset(kvt, 0.0))
                                vop(lambda e, ps=ps, off=off, kvt=kvt: e.tensor_copy(kvt[0:64, 0:64], ps[0:64, off:off + 64]))
                                vop(lambda e, ps=ps, off=off, kvt=kvt: e.tensor_copy(kvt[64:128, 64:128], ps[64:128, off + 64:off + 128]))
                                kdt = kdb[:, (sq * 8 + p) * 2:(sq * 8 + p) * 2 + 2]
                                vop_red(lambda e, kdt=kdt: e.memset(kdt, 0.0))
                                vop(lambda e, ps=ps, off=off, kdt=kdt: e.tensor_copy(kdt[0:64, 0:1], ps[0:64, off + P:off + P + 1]))
                                vop(lambda e, ps=ps, off=off, kdt=kdt: e.tensor_copy(kdt[64:128, 1:2], ps[64:128, off + P:off + P + 1]))
                    S.end_phase(blk)

                with ExitStack() as _st:
                    wqsb = _st.enter_context(nc.sbuf_tensor(nm("wqsb"), [P, DC * D], BF16))
                    wgsb = _st.enter_context(nc.sbuf_tensor(nm("wgsb"), [P, DC * D], BF16))
                    wosb = _st.enter_context(nc.sbuf_tensor(nm("wosb"), [P, DC * D], BF16))
                    wldf2 = _st.enter_context(nc.sbuf_tensor(nm("wldf2"), [P, D], F32))
                    qrf = _st.enter_context(nc.sbuf_tensor(nm("qrf"), [P, SL], F32))
                    qef = _st.enter_context(nc.sbuf_tensor(nm("qef"), [P, SL], F32))
                    qbf = _st.enter_context(nc.sbuf_tensor(nm("qbf"), [P, DC * SL], BF16))
                    zsb = _st.enter_context(nc.sbuf_tensor(nm("zsb"), [2, SL], F32))
                    zbs = _st.enter_context(nc.sbuf_tensor(nm("zbs"), [P, SL], F32))
                    ssb = _st.enter_context(nc.sbuf_tensor(nm("ssb"), [P, SL], F32))
                    gsb = _st.enter_context(nc.sbuf_tensor(nm("gsb"), [P, DC * SL], F32))
                    abf = _st.enter_context(nc.sbuf_tensor(nm("abf"), [P, DC * SL], BF16))
                    aor = _st.enter_context(nc.sbuf_tensor(nm("aor"), [P, SL], F32))
                    axr = _st.enter_context(nc.sbuf_tensor(nm("axr"), [P, SL], F32))
                    psQ7 = _st.enter_context(nc.psum_tensor(nm("psQ7"), [P, SL], F32))
                    psQ8 = _st.enter_context(nc.psum_tensor(nm("psQ8"), [P, SL], F32))
                    psZr = _st.enter_context(nc.psum_tensor(nm("psZr"), [2, SL], F32))
                    psZb = _st.enter_context(nc.psum_tensor(nm("psZb"), [P, SL], F32))
                    psS7 = _st.enter_context(nc.psum_tensor(nm("psS7"), [P, SL], F32))
                    psO7 = _st.enter_context(nc.psum_tensor(nm("psO7"), [P, SL], F32))
                    psO8 = _st.enter_context(nc.psum_tensor(nm("psO8"), [P, SL], F32))
                    blk2 = _st.enter_context(nc.Block())
                    S.start_phase()
                    for dk in range(DC):
                        dma(wldf2[:], wq_h[dk * P:(dk + 1) * P, :])
                        vop(lambda e, dk=dk: e.tensor_copy(wqsb[:, dk * D:(dk + 1) * D], wldf2[:]))
                        dma(wldf2[:], wg_h[dk * P:(dk + 1) * P, :])
                        vop(lambda e, dk=dk: e.tensor_copy(wgsb[:, dk * D:(dk + 1) * D], wldf2[:]))
                        dma(wldf2[:], wo_h[dk * P:(dk + 1) * P, :])
                        vop(lambda e, dk=dk: e.tensor_copy(wosb[:, dk * D:(dk + 1) * D], wldf2[:]))
                    for gsl in range(NSL):
                        sq = gsl // 4
                        for qc in range(DC):
                            ps = psQ7 if qc % 2 == 0 else psQ8
                            for dk in range(DC):
                                top_(lambda e, dk=dk, qc=qc, gsl=gsl, ps=ps: e.matmul(
                                    ps[:], wqsb[:, dk * D + qc * P: dk * D + (qc + 1) * P],
                                    xn2b[:, dk * T + gsl * SL: dk * T + (gsl + 1) * SL],
                                    start=(dk == 0), stop=(dk == DC - 1)))
                            sop(lambda e, qc=qc, ps=ps: e.activation(qrf[:], ps[:], AF.Identity, bias=bq_sb[:, qc:qc + 1]))
                            vop(lambda e: e.tensor_scalar(qef[:], qrf[:], 0.0, None, OP.min))
                            sop(lambda e: e.activation(qef[:], qef[:], AF.Exp))
                            vop(lambda e, qc=qc: e.scalar_tensor_tensor(
                                qbf[:, qc * SL:(qc + 1) * SL], qrf[:], 0.0, qef[:], OP.max, OP.add))
                        for gc in range(DC):
                            ps = psQ7 if gc % 2 == 0 else psQ8
                            for dk in range(DC):
                                top_(lambda e, dk=dk, gc=gc, gsl=gsl, ps=ps: e.matmul(
                                    ps[:], wgsb[:, dk * D + gc * P: dk * D + (gc + 1) * P],
                                    xn2b[:, dk * T + gsl * SL: dk * T + (gsl + 1) * SL],
                                    start=(dk == 0), stop=(dk == DC - 1)))
                            sop(lambda e, gc=gc, ps=ps: e.activation(
                                gsb[:, gc * SL:(gc + 1) * SL], ps[:], AF.Sigmoid, bias=bg_sb[:, gc:gc + 1]))
                        for pp in range(8):
                            kdt = kdb[:, (sq * 8 + pp) * 2:(sq * 8 + pp) * 2 + 2]
                            top_(lambda e, kdt=kdt, pp=pp: e.matmul(
                                psZr[:], kdt, qbf[:, pp * SL:(pp + 1) * SL], start=True, stop=True))
                            vop(lambda e: e.tensor_scalar(zsb[:], psZr[:], 1e-6, None, OP.add))
                            vop_red(lambda e: e.reciprocal(zsb[:], zsb[:]))
                            top_(lambda e: e.matmul(psZb[:], patt2[:], zsb[:], start=True, stop=True))
                            vop(lambda e: e.tensor_copy(zbs[:], psZb[:]))
                            kvt = kvb[:, (sq * 8 + pp) * P:(sq * 8 + pp + 1) * P]
                            top_(lambda e, kvt=kvt, pp=pp: e.matmul(
                                psS7[:], kvt, qbf[:, pp * SL:(pp + 1) * SL], start=True, stop=True))
                            vop(lambda e: e.tensor_tensor(ssb[:], psS7[:], zbs[:], OP.mult))
                            vop(lambda e, pp=pp: e.tensor_tensor(
                                abf[:, pp * SL:(pp + 1) * SL], ssb[:], gsb[:, pp * SL:(pp + 1) * SL], OP.mult))
                        for oc in range(DC):
                            ps = psO7 if oc % 2 == 0 else psO8
                            for dk in range(DC):
                                top_(lambda e, dk=dk, oc=oc, ps=ps: e.matmul(
                                    ps[:], wosb[:, dk * D + oc * P: dk * D + (oc + 1) * P],
                                    abf[:, dk * SL:(dk + 1) * SL],
                                    start=(dk == 0), stop=(dk == DC - 1)))
                            sop(lambda e, oc=oc, ps=ps: e.activation(aor[:], ps[:], AF.Identity, bias=bo_sb[:, oc:oc + 1]))
                            dma(axr[:], src[oc * P:(oc + 1) * P, gsl * SL:(gsl + 1) * SL])
                            vop(lambda e: e.tensor_tensor(aor[:], aor[:], axr[:], OP.add))
                            dma(src[oc * P:(oc + 1) * P, gsl * SL:(gsl + 1) * SL], aor[:])
                    S.end_phase(blk2)

        ffn(0, xfm)
        if debug:
            with nc.Block() as blk:
                S.start_phase()
                with nc.sbuf_tensor(nm("dbt"), [P, T], F32) as dbt:
                    for dc in range(DC):
                        dma(dbt[:], xfm[dc * P:(dc + 1) * P, :])
                        S.gap()
                        dma(dbg[0][dc * P:(dc + 1) * P, :], dbt[:])
                        S.gap()
                    dma(dbgs[0:1, 0:32], scrow[0:1, 0:32])
                S.end_phase(blk)
        attn(xfm)
        if debug:
            with nc.Block() as blk:
                S.start_phase()
                with nc.sbuf_tensor(nm("dbt2"), [P, T], F32) as dbt:
                    for dc in range(DC):
                        dma(dbt[:], xfm[dc * P:(dc + 1) * P, :])
                        S.gap()
                        dma(dbg[1][dc * P:(dc + 1) * P, :], dbt[:])
                        S.gap()
                S.end_phase(blk)
        ffn(1, xfm)
        if debug:
            with nc.Block() as blk:
                S.start_phase()
                with nc.sbuf_tensor(nm("dbt3"), [P, T], F32) as dbt:
                    for dc in range(DC):
                        dma(dbt[:], xfm[dc * P:(dc + 1) * P, :])
                        S.gap()
                        dma(dbg[2][dc * P:(dc + 1) * P, :], dbt[:])
                        S.gap()
                S.end_phase(blk)

        with ExitStack() as _st:
            ost = _st.enter_context(nc.sbuf_tensor(nm("ost"), [P, DC * 1024], F32))
            oyt = _st.enter_context(nc.sbuf_tensor(nm("oyt"), [P, D], F32))
            pst_o = _st.enter_context(nc.psum_tensor(nm("pst_o"), [P, P], F32))
            blk = _st.enter_context(nc.Block())
            S.start_phase()
            for tg in range(TCN // 8):
                base = tg * 1024
                for dc in range(DC):
                    dma(ost[:, dc * 1024:(dc + 1) * 1024], xfm[dc * P:(dc + 1) * P, base:base + 1024])
                for g in range(8):
                    tc = tg * 8 + g
                    for dc in range(DC):
                        top_(lambda e, dc=dc, g=g: e.transpose(
                            pst_o[:], ost[:, dc * 1024 + g * P: dc * 1024 + (g + 1) * P], id32[:]))
                        sop(lambda e, dc=dc: e.copy(oyt[:, dc * P:(dc + 1) * P], pst_o[:]))
                    dma(y_out[tc * P:(tc + 1) * P, :], oyt[:])
            S.wait_all("sync")
            S.end_phase(blk)

    return nc


_NC_CACHE = {}


def _get_nc(debug=False):
    key = bool(debug)
    if key not in _NC_CACHE:
        _NC_CACHE[key] = build(debug=debug)
    return _NC_CACHE[key]


def _in_maps(inputs, ncores=8):
    f32 = np.float32
    x = np.ascontiguousarray(np.asarray(inputs["L0_attn_x"], dtype=f32))
    base = {
        "wq": np.ascontiguousarray(np.asarray(inputs["L0_attn_wq_w"], f32)),
        "wk": np.ascontiguousarray(np.asarray(inputs["L0_attn_wk_w"], f32)),
        "wv": np.ascontiguousarray(np.asarray(inputs["L0_attn_wv_w"], f32)),
        "wg": np.ascontiguousarray(np.asarray(inputs["L0_attn_wg_w"], f32)),
        "wo": np.ascontiguousarray(np.asarray(inputs["L0_attn_out_w"], f32)),
        "bq": np.ascontiguousarray(np.asarray(inputs["L0_attn_wq_b"], f32).reshape(D, 1)),
        "bk": np.ascontiguousarray(np.asarray(inputs["L0_attn_wk_b"], f32).reshape(1, D)),
        "bv": np.ascontiguousarray(np.asarray(inputs["L0_attn_wv_b"], f32).reshape(1, D)),
        "bg": np.ascontiguousarray(np.asarray(inputs["L0_attn_wg_b"], f32).reshape(D, 1)),
        "bo": np.ascontiguousarray(np.asarray(inputs["L0_attn_out_b"], f32).reshape(D, 1)),
        "wgate": np.ascontiguousarray(np.asarray(inputs["L0_ffn_w_gate"], f32)),
        "wup": np.ascontiguousarray(np.asarray(inputs["L0_ffn_w_up"], f32)),
        "wdown": np.ascontiguousarray(np.asarray(inputs["L0_ffn_w_down"], f32)),
        "id32": np.eye(P, dtype=f32),
        "idbf": np.eye(P).astype(ml_dtypes.bfloat16),
        "onesc": np.ones((P, 1), f32),
        "onesr": np.ones((1, P), f32),
        "onescb": np.ones((P, 1), ml_dtypes.bfloat16),
        "patt2": np.concatenate(
            [np.concatenate([np.ones((1, 64), f32), np.zeros((1, 64), f32)], axis=1),
             np.concatenate([np.zeros((1, 64), f32), np.ones((1, 64), f32)], axis=1)], axis=0),
    }
    maps = []
    for c in range(ncores):
        m = dict(base)
        m["x"] = np.ascontiguousarray(x[2 * c:2 * c + 2].reshape(T, D))
        maps.append(m)
    return maps


def kernel(**inputs):
    nc = _get_nc(debug=False)
    maps = _in_maps(inputs)
    res = run_bass_kernel_spmd(nc, maps, core_ids=list(range(8)))
    out = np.concatenate(
        [np.asarray(res.results[c]["y"]).reshape(2, 2048, D) for c in range(8)], axis=0
    )
    return out


# version 06 (final)
# speedup vs baseline: 1.4457x; 1.0132x over previous
import numpy as np
import ml_dtypes
from contextlib import ExitStack

import concourse.bass as bass
import concourse.mybir as mybir
from concourse.alu_op_type import AluOpType as OP
from concourse.bass_utils import run_bass_kernel_spmd

F32 = mybir.dt.float32
BF16 = mybir.dt.bfloat16
I16 = mybir.dt.int16
AF = mybir.ActivationFunctionType
AX = mybir.AxisListType.X

P = 128
T = 4096
D = 1024
FF = 3072
DC = D // P
FCN = FF // P
SL = 512
NSL = T // SL
TCN = T // P
RC = 1.5 * 2.0 ** 23
QA = 32767.0
QW = 7.0
ENGS = ("sync", "vector", "scalar", "tensor", "gpsimd")


class Ser:

    def __init__(self, nc, chain):
        self.nc = nc
        self.chain = chain
        self.c = 0
        self.gap_c = 0
        self.auto = True
        self.last_eng = None
        self._open = None
        self.q = None

    def _close(self):
        if self._open is not None:
            self._open["inc"] = 1
            self.c += 1
            self._open = None

    def start_phase(self):
        import os
        if os.environ.get("KDBG"):
            print("PHASE vec_i=", getattr(self, "vec_i", 0))
        self.q = {e: [] for e in ENGS}
        self.gap()

    def do(self, ekey, fn, inc=1, imm=False):
        import os
        allow = os.environ.get("KENGS", "sync,tensor,scalar,vector").split(",")
        ok = ekey in allow
        if ekey == "vector":
            self.vec_i = getattr(self, "vec_i", 0) + 1
            if os.environ.get("KDBG") == "2" and 2440 <= self.vec_i <= 2500:
                print("vec_i", self.vec_i, "line", fn.__code__.co_firstlineno)
            lo = int(os.environ.get("KVLO", "0"))
            hi = int(os.environ.get("KVHI", "10**9").replace("10**9", "1000000000"))
            if not (lo <= self.vec_i < hi):
                ok = False
        if self.auto and (ekey != self.last_eng or not ok):
            self._close()
            self.gap_c = self.c
        self.last_eng = ekey
        rec = {"fn": fn, "tgt": self.gap_c, "inc": 0}
        if imm:
            rec["inc"] = inc
            self.c += inc
        else:
            self._open = rec
        self.q[ekey].append(rec)

    def gap(self):
        self._close()
        self.gap_c = self.c
        self.last_eng = None

    def wait_all(self, ekey):
        self._close()
        tgt = self.c
        self.q[ekey].append({"fn": None, "tgt": tgt, "inc": 0})

    def end_phase(self, blk):
        self._close()
        for e in ENGS:
            items = self.q[e]
            if not items:
                continue

            def emit(eng, items=items):
                for rec in items:
                    if rec["tgt"] > 0:
                        eng.wait_ge(self.chain, rec["tgt"])
                    if rec["fn"] is None:
                        continue
                    inst = rec["fn"](eng)
                    if rec["inc"] > 0:
                        inst.then_inc(self.chain, rec["inc"])

            getattr(blk, e)(emit)
        self.q = None


def build(debug=False):
    nc = bass.Bass("TRN2", target_bir_lowering=False, debug=False)
    import itertools
    _uid = itertools.count()

    def nm(s):
        return f"{s}_{next(_uid)}"


    x_in = nc.dram_tensor("x", [T, D], F32, kind="ExternalInput").ap()
    wq_h = nc.dram_tensor("wq", [D, D], F32, kind="ExternalInput").ap()
    wk_h = nc.dram_tensor("wk", [D, D], F32, kind="ExternalInput").ap()
    wv_h = nc.dram_tensor("wv", [D, D], F32, kind="ExternalInput").ap()
    wg_h = nc.dram_tensor("wg", [D, D], F32, kind="ExternalInput").ap()
    wo_h = nc.dram_tensor("wo", [D, D], F32, kind="ExternalInput").ap()
    bq_h = nc.dram_tensor("bq", [D, 1], F32, kind="ExternalInput").ap()
    bk_h = nc.dram_tensor("bk", [1, D], F32, kind="ExternalInput").ap()
    bv_h = nc.dram_tensor("bv", [1, D], F32, kind="ExternalInput").ap()
    bg_h = nc.dram_tensor("bg", [D, 1], F32, kind="ExternalInput").ap()
    bo_h = nc.dram_tensor("bo", [D, 1], F32, kind="ExternalInput").ap()
    wgate_h = nc.dram_tensor("wgate", [FF, D], F32, kind="ExternalInput").ap()
    wup_h = nc.dram_tensor("wup", [FF, D], F32, kind="ExternalInput").ap()
    wdown_h = nc.dram_tensor("wdown", [D, FF], F32, kind="ExternalInput").ap()
    id32_h = nc.dram_tensor("id32", [P, P], F32, kind="ExternalInput").ap()
    idbf_h = nc.dram_tensor("idbf", [P, P], BF16, kind="ExternalInput").ap()
    onesc_h = nc.dram_tensor("onesc", [P, 1], F32, kind="ExternalInput").ap()
    onesr_h = nc.dram_tensor("onesr", [1, P], F32, kind="ExternalInput").ap()
    onescb_h = nc.dram_tensor("onescb", [P, 1], BF16, kind="ExternalInput").ap()
    patt2_h = nc.dram_tensor("patt2", [2, P], F32, kind="ExternalInput").ap()

    y_out = nc.dram_tensor("y", [T, D], F32, kind="ExternalOutput").ap()

    xfm = nc.dram_tensor("xfm", [DC * P, T], F32).ap()
    mg = nc.dram_tensor("mg", [FCN * P, T], F32).ap()
    mup = nc.dram_tensor("mup", [FCN * P, T], F32).ap()
    ci = nc.dram_tensor("ci", [FCN * P, T], F32).ap()
    g2s = nc.dram_tensor("g2s", [FCN * P, T], I16).ap()
    ups = nc.dram_tensor("ups", [FCN * P, T], I16).ap()
    wgT_c = nc.dram_tensor("wgT_c", [DC * P, FF], BF16).ap()
    wupT_c = nc.dram_tensor("wupT_c", [DC * P, FF], BF16).ap()
    wdT_c = nc.dram_tensor("wdT_c", [FCN * P, D], BF16).ap()
    cc_in = nc.dram_tensor("cc_in", [1, 8], F32)
    cc_out = nc.dram_tensor("cc_out", [1, 8], F32)
    if debug:
        dbg = [nc.dram_tensor(f"dbg{i}", [DC * P, T], F32, kind="ExternalOutput").ap() for i in range(3)]
        dbgs = nc.dram_tensor("dbgs", [1, 64], F32, kind="ExternalOutput").ap()

    with ExitStack() as top:
        chain = top.enter_context(nc.semaphore("chain"))
        S = Ser(nc, chain)

        id32 = top.enter_context(nc.sbuf_tensor(nm("id32s"), [P, P], F32))
        idbf = top.enter_context(nc.sbuf_tensor(nm("idbfs"), [P, P], BF16))
        onesc = top.enter_context(nc.sbuf_tensor(nm("onescs"), [P, 1], F32))
        onesr = top.enter_context(nc.sbuf_tensor(nm("onesrs"), [1, P], F32))
        onescb = top.enter_context(nc.sbuf_tensor(nm("onescbs"), [P, 1], BF16))
        patt2 = top.enter_context(nc.sbuf_tensor(nm("patt2s"), [2, P], F32))
        crnd = top.enter_context(nc.sbuf_tensor(nm("crnd"), [P, 1], F32))
        c65k = top.enter_context(nc.sbuf_tensor(nm("c65k"), [P, 1], F32))
        ceps = top.enter_context(nc.sbuf_tensor(nm("ceps"), [1, 1], F32))
        crnd_r = top.enter_context(nc.sbuf_tensor(nm("crnd_r"), [1, 1], F32))
        czero = top.enter_context(nc.sbuf_tensor(nm("czero"), [P, 1], F32))
        scrow = top.enter_context(nc.sbuf_tensor(nm("scrow"), [1, 32], F32))
        sc = top.enter_context(nc.sbuf_tensor(nm("sc"), [P, 16], F32))
        ccl = top.enter_context(nc.sbuf_tensor(nm("ccl"), [1, 8], F32))
        ccr = top.enter_context(nc.sbuf_tensor(nm("ccr"), [1, 8], F32))
        bq_sb = top.enter_context(nc.sbuf_tensor(nm("bq_sb"), [P, DC], F32))
        bg_sb = top.enter_context(nc.sbuf_tensor(nm("bg_sb"), [P, DC], F32))
        bo_sb = top.enter_context(nc.sbuf_tensor(nm("bo_sb"), [P, DC], F32))
        bkr = top.enter_context(nc.sbuf_tensor(nm("bkr"), [1, D], F32))
        bvr = top.enter_context(nc.sbuf_tensor(nm("bvr"), [1, D], F32))
        lmax = top.enter_context(nc.sbuf_tensor(nm("lmax"), [P, 8], F32))
        t128 = top.enter_context(nc.sbuf_tensor(nm("t128"), [P, 2], F32))

        (IV_S1, IS_S1, IS_S3, IS_S2, IS_S4, IS_S5, IS_T0, IS_T1, IS_T2) = range(9)
        (SC_K1, SC_S3, SC_K2, SC_K3, SC_K4, SC_BETA) = range(6)

        def dma(dst, src):
            S.do("sync", lambda e, d=dst, s=src: e.dma_start(out=d, in_=s), 16, imm=True)

        def vop(fn):
            S.do("vector", fn)


        def vop_red(fn):
            S.do("vector", fn)
            S.gap()
        def sop(fn):
            S.do("scalar", fn)

        def top_(fn):
            S.do("tensor", fn)

        def gop(fn):
            S.do("gpsimd", fn)

        def sb_row(ap):
            return ap

        H = {}

        def cross_part_absmax(src_col, dst_row):
            top_(lambda e: e.transpose(H["psx"][0:1, 0:P], src_col, id32[:]))
            vop_red(lambda e: e.reduce_max(dst_row, H["psx"][0:1, 0:P], axis=AX, apply_absolute_value=True))

        def bcast128(src11, slot):
            top_(lambda e: e.matmul(H["pssc"][:, slot:slot + 1], onesr[0:1, :], src11, start=True, stop=True))
            sop(lambda e: e.copy(sc[:, slot:slot + 1], H["pssc"][:, slot:slot + 1]))

        def allreduce_max(n):
            S.gap()
            S.do("gpsimd", lambda e: e.dma_start(out=cc_in.ap(), in_=ccl[:]), 16, imm=True)
            S.gap()
            S.do("gpsimd", lambda e: e.collective_compute(
                "AllReduce", OP.max, replica_groups=[list(range(8))],
                ins=[cc_in.ap().opt()], outs=[cc_out.ap().opt()]), 1, imm=True)
            S.gap()
            S.do("gpsimd", lambda e: e.dma_start(out=ccr[:], in_=cc_out.ap()), 16, imm=True)
            S.gap()

        with nc.Block() as blk:
            S.start_phase()
            dma(id32[:], id32_h)
            dma(idbf[:], idbf_h)
            dma(onesc[:], onesc_h)
            dma(onesr[:], onesr_h)
            dma(onescb[:], onescb_h)
            dma(patt2[:], patt2_h)
            vop_red(lambda e: e.memset(crnd[:], RC))
            vop_red(lambda e: e.memset(c65k[:], 65536.0))
            vop_red(lambda e: e.memset(ceps[:], 1e-5))
            vop_red(lambda e: e.memset(crnd_r[:], RC))
            vop_red(lambda e: e.memset(czero[:], 0.0))
            vop_red(lambda e: e.memset(ccl[:], 0.0))
            for dc in range(DC):
                dma(bq_sb[:, dc:dc + 1], bq_h[dc * P:(dc + 1) * P, 0:1])
                dma(bg_sb[:, dc:dc + 1], bg_h[dc * P:(dc + 1) * P, 0:1])
                dma(bo_sb[:, dc:dc + 1], bo_h[dc * P:(dc + 1) * P, 0:1])
            dma(bkr[:], bk_h)
            dma(bvr[:], bv_h)
            S.end_phase(blk)

        with ExitStack() as _st:
            xtm = _st.enter_context(nc.sbuf_tensor(nm("xtm"), [P, D], F32))
            stf = _st.enter_context(nc.sbuf_tensor(nm("stf"), [P, DC * 1024], F32))
            pst_in = _st.enter_context(nc.psum_tensor(nm("pst_in"), [P, P], F32))
            blk = _st.enter_context(nc.Block())
            S.start_phase()
            for tc in range(TCN):
                dma(xtm[:], x_in[tc * P:(tc + 1) * P, :])
                g = tc % 8
                for dc in range(DC):
                    top_(lambda e, dc=dc: e.transpose(pst_in[:], xtm[:, dc * P:(dc + 1) * P], id32[:]))
                    sop(lambda e, dc=dc, g=g: e.copy(stf[:, dc * 1024 + g * P: dc * 1024 + (g + 1) * P], pst_in[:]))
                if g == 7:
                    base = (tc - 7) * P
                    for dc in range(DC):
                        dma(xfm[dc * P:(dc + 1) * P, base:base + 1024], stf[:, dc * 1024:(dc + 1) * 1024])
            S.end_phase(blk)

        def ln_stats(src, rows_a, rows_b):
            with ExitStack() as _st:
                lnx = _st.enter_context(nc.sbuf_tensor(nm("lnx"), [P, SL * 3], F32))
                lnq = _st.enter_context(nc.sbuf_tensor(nm("lnq"), [P, SL], F32))
                lnr = _st.enter_context(nc.sbuf_tensor(nm("lnr"), [1, SL * 3], F32))
                psS = _st.enter_context(nc.psum_tensor(nm("psS"), [1, SL], F32))
                psQ = _st.enter_context(nc.psum_tensor(nm("psQ"), [1, SL], F32))
                blk = _st.enter_context(nc.Block())
                S.start_phase()
                for sl in range(NSL):
                    for dc in range(DC):
                        xb = lnx[:, (dc % 3) * SL:(dc % 3 + 1) * SL]
                        dma(xb, src[dc * P:(dc + 1) * P, sl * SL:(sl + 1) * SL])
                        sop(lambda e, xb=xb: e.activation(lnq[:], xb, AF.Square))
                        top_(lambda e, xb=xb, dc=dc: e.matmul(
                            psS[0:1, :], onesc[:], xb, start=(dc == 0), stop=(dc == DC - 1)))
                        top_(lambda e, dc=dc: e.matmul(
                            psQ[0:1, :], onesc[:], lnq[:], start=(dc == 0), stop=(dc == DC - 1)))
                    mur = lnr[0:1, 0:SL]
                    tmp = lnr[0:1, SL:2 * SL]
                    sdr = lnr[0:1, 2 * SL:3 * SL]
                    vop(lambda e, mur=mur: e.tensor_scalar(mur, psS[0:1, :], 1.0 / D, None, OP.mult))
                    vop(lambda e, tmp=tmp: e.tensor_scalar(tmp, psQ[0:1, :], 1.0 / D, None, OP.mult))
                    vop(lambda e, mur=mur, sdr=sdr: e.tensor_tensor(sdr, mur, mur, OP.mult))
                    vop(lambda e, tmp=tmp, sdr=sdr: e.tensor_tensor(tmp, tmp, sdr, OP.subtract))
                    sop(lambda e, sdr=sdr, tmp=tmp: e.activation(sdr, tmp, AF.Sqrt, bias=ceps[:]))
                    vop_red(lambda e, sdr=sdr: e.reciprocal(sdr, sdr))
                    vop(lambda e, sdr=sdr, sl=sl: e.tensor_copy(rows_a[0:1, sl * SL:(sl + 1) * SL], sdr))
                    vop(lambda e, mur=mur, sdr=sdr, sl=sl: e.tensor_tensor(
                        rows_b[0:1, sl * SL:(sl + 1) * SL], mur, sdr, OP.mult))
                S.end_phase(blk)

        def ln_absmax_and_ar1(src, rows_a, rows_b):
            with ExitStack() as _st:
                amx = _st.enter_context(nc.sbuf_tensor(nm("amx"), [P, SL * 3], F32))
                amu = _st.enter_context(nc.sbuf_tensor(nm("amu"), [P, SL], F32))
                amt = _st.enter_context(nc.sbuf_tensor(nm("amt"), [P, 2], F32))
                psA = _st.enter_context(nc.psum_tensor(nm("psA"), [P, SL], F32))
                psB = _st.enter_context(nc.psum_tensor(nm("psB"), [P, SL], F32))
                psx_l = _st.enter_context(nc.psum_tensor(nm("psx_"), [1, P], F32))
                pssc_l = _st.enter_context(nc.psum_tensor(nm("pssc_"), [P, 8], F32))
                blk = _st.enter_context(nc.Block())
                H["psx"], H["pssc"] = psx_l, pssc_l
                S.start_phase()
                vop_red(lambda e: e.memset(lmax[:, 0:1], 0.0))
                for sl in range(NSL):
                    top_(lambda e, sl=sl: e.matmul(psA[:], onesr[0:1, :], rows_a[0:1, sl * SL:(sl + 1) * SL], start=True, stop=True))
                    top_(lambda e, sl=sl: e.matmul(psB[:], onesr[0:1, :], rows_b[0:1, sl * SL:(sl + 1) * SL], start=True, stop=True))
                    for dc in range(DC):
                        xb = amx[:, (dc % 3) * SL:(dc % 3 + 1) * SL]
                        dma(xb, src[dc * P:(dc + 1) * P, sl * SL:(sl + 1) * SL])
                        vop(lambda e, xb=xb: e.tensor_tensor(amu[:], xb, psA[:], OP.mult))
                        vop(lambda e: e.tensor_tensor(amu[:], amu[:], psB[:], OP.subtract))
                        vop_red(lambda e: e.reduce_max(amt[:, 0:1], amu[:], axis=AX, apply_absolute_value=True))
                        vop_red(lambda e: e.tensor_tensor(lmax[:, 0:1], lmax[:, 0:1], amt[:, 0:1], OP.max))
                cross_part_absmax(lmax[:, 0:1], scrow[0:1, IS_T0:IS_T0 + 1])
                vop_red(lambda e: e.tensor_copy(ccl[0:1, 0:1], scrow[0:1, IS_T0:IS_T0 + 1]))
                allreduce_max(1)
                vop_red(lambda e: e.tensor_scalar(scrow[0:1, IS_S1:IS_S1 + 1], ccr[0:1, 0:1], 1.0 / QA, 1e-12, OP.mult, OP.max))
                vop_red(lambda e: e.reciprocal(scrow[0:1, IV_S1:IV_S1 + 1], scrow[0:1, IS_S1:IS_S1 + 1]))
                vop_red(lambda e: e.tensor_scalar(rows_a[:], rows_a[:], scrow[0:1, IV_S1:IV_S1 + 1], None, OP.mult))
                vop_red(lambda e: e.tensor_scalar(rows_b[:], rows_b[:], scrow[0:1, IV_S1:IV_S1 + 1], crnd_r[:], OP.mult, OP.subtract))
                S.end_phase(blk)

        def xq_pass(src, dst_bf, rows_a, rows_b):
            with ExitStack() as _st:
                xqx = _st.enter_context(nc.sbuf_tensor(nm("xqx"), [P, SL * 3], F32))
                xqu = _st.enter_context(nc.sbuf_tensor(nm("xqu"), [P, SL], F32))
                psA = _st.enter_context(nc.psum_tensor(nm("psA2"), [P, SL], F32))
                psB = _st.enter_context(nc.psum_tensor(nm("psB2"), [P, SL], F32))
                blk = _st.enter_context(nc.Block())
                S.start_phase()
                for sl in range(NSL):
                    top_(lambda e, sl=sl: e.matmul(psA[:], onesr[0:1, :], rows_a[0:1, sl * SL:(sl + 1) * SL], start=True, stop=True))
                    top_(lambda e, sl=sl: e.matmul(psB[:], onesr[0:1, :], rows_b[0:1, sl * SL:(sl + 1) * SL], start=True, stop=True))
                    for dc in range(DC):
                        xb = xqx[:, (dc % 3) * SL:(dc % 3 + 1) * SL]
                        dma(xb, src[dc * P:(dc + 1) * P, sl * SL:(sl + 1) * SL])
                        vop(lambda e, xb=xb: e.tensor_tensor(xqu[:], xb, psA[:], OP.mult))
                        vop(lambda e: e.tensor_tensor(xqu[:], xqu[:], psB[:], OP.subtract))
                        vop(lambda e, dc=dc, sl=sl: e.tensor_scalar(
                            dst_bf[:, dc * T + sl * SL: dc * T + (sl + 1) * SL], xqu[:], RC, None, OP.subtract))
                S.end_phase(blk)

        def xn_pass(src, dst_bf, rows_a, rows_b):
            with ExitStack() as _st:
                xnx = _st.enter_context(nc.sbuf_tensor(nm("xnx"), [P, SL * 3], F32))
                xnu = _st.enter_context(nc.sbuf_tensor(nm("xnu"), [P, SL], F32))
                psA = _st.enter_context(nc.psum_tensor(nm("psA3"), [P, SL], F32))
                psB = _st.enter_context(nc.psum_tensor(nm("psB3"), [P, SL], F32))
                blk = _st.enter_context(nc.Block())
                S.start_phase()
                for sl in range(NSL):
                    top_(lambda e, sl=sl: e.matmul(psA[:], onesr[0:1, :], rows_a[0:1, sl * SL:(sl + 1) * SL], start=True, stop=True))
                    top_(lambda e, sl=sl: e.matmul(psB[:], onesr[0:1, :], rows_b[0:1, sl * SL:(sl + 1) * SL], start=True, stop=True))
                    for dc in range(DC):
                        xb = xnx[:, (dc % 3) * SL:(dc % 3 + 1) * SL]
                        dma(xb, src[dc * P:(dc + 1) * P, sl * SL:(sl + 1) * SL])
                        vop(lambda e, xb=xb: e.tensor_tensor(xnu[:], xb, psA[:], OP.mult))
                        vop(lambda e, dc=dc, sl=sl: e.tensor_tensor(
                            dst_bf[:, dc * T + sl * SL: dc * T + (sl + 1) * SL], xnu[:], psB[:], OP.subtract))
                S.end_phase(blk)

        def ffn(inv, src):
            with nc.sbuf_tensor(f"xqib{inv}", [P, DC * T], BF16) as xqib:
                with nc.sbuf_tensor(nm("rows_a"), [1, T], F32) as rows_a, nc.sbuf_tensor(nm("rows_b"), [1, T], F32) as rows_b:
                    ln_stats(src, rows_a, rows_b)
                    ln_absmax_and_ar1(src, rows_a, rows_b)
                    xq_pass(src, xqib, rows_a, rows_b)

                with ExitStack() as _st:
                    wnat = _st.enter_context(nc.sbuf_tensor(nm("wnat"), [P, D * 2], F32))
                    wqb = _st.enter_context(nc.sbuf_tensor(nm("wqb"), [P, D], F32))
                    wpb = _st.enter_context(nc.sbuf_tensor(nm("wpb"), [P, D], BF16))
                    bm = _st.enter_context(nc.sbuf_tensor(nm("bm"), [P, 64], F32))
                    wT = _st.enter_context(nc.sbuf_tensor(nm("wT"), [P, 2 * DC * P], BF16))
                    mst = _st.enter_context(nc.sbuf_tensor(nm("mst"), [P, T], F32))
                    mtmp = _st.enter_context(nc.sbuf_tensor(nm("mtmp"), [P, 2], F32))
                    psM = _st.enter_context(nc.psum_tensor(nm("psM"), [P, SL], F32))
                    psM2 = _st.enter_context(nc.psum_tensor(nm("psM2"), [P, SL], F32))
                    psW = _st.enter_context(nc.psum_tensor(nm("psW"), [P, P], BF16))
                    psx_l = _st.enter_context(nc.psum_tensor(nm("psx4"), [1, P], F32))
                    pssc_l = _st.enter_context(nc.psum_tensor(nm("pssc4"), [P, 8], F32))
                    blk = _st.enter_context(nc.Block())
                    H["psx"], H["pssc"] = psx_l, pssc_l
                    S.start_phase()
                    for wi, (wsrc, spill, wtc) in enumerate(
                        [(wgate_h, mg, wgT_c), (wup_h, mup, wupT_c)]
                    ):
                        vop_red(lambda e, wi=wi: e.memset(lmax[:, 2 + 2 * wi:3 + 2 * wi], 0.0))
                        for fc in range(FCN):
                            wr = wT[:, (fc % 2) * DC * P:(fc % 2 + 1) * DC * P]
                            if inv == 0:
                                wn = wnat[:, (fc % 2) * D:(fc % 2 + 1) * D]
                                dma(wn, wsrc[fc * P:(fc + 1) * P, :])
                                vop_red(lambda e, wn=wn: e.reduce_max(
                                    bm[:, 0:32], wn.rearrange("p (b e) -> p b e", e=32),
                                    axis=AX, apply_absolute_value=True))
                                vop(lambda e: e.tensor_scalar(bm[:, 0:32], bm[:, 0:32], 1.0 / QW, 1e-12, OP.mult, OP.max))
                                vop_red(lambda e: e.reciprocal(bm[:, 32:64], bm[:, 0:32]))
                                vop(lambda e, wn=wn: e.tensor_tensor(
                                    wqb[:].rearrange("p (b e) -> p b e", e=32),
                                    wn.rearrange("p (b e) -> p b e", e=32),
                                    bm[:, 32:64].unsqueeze(2).broadcast_to([P, 32, 32]), OP.mult))
                                vop(lambda e: e.tensor_scalar(wqb[:], wqb[:], RC, RC, OP.add, OP.subtract))
                                vop(lambda e: e.tensor_tensor(
                                    wpb[:].rearrange("p (b e) -> p b e", e=32),
                                    wqb[:].rearrange("p (b e) -> p b e", e=32),
                                    bm[:, 0:32].unsqueeze(2).broadcast_to([P, 32, 32]), OP.mult))
                                for dc in range(DC):
                                    top_(lambda e, dc=dc: e.transpose(psW[:], wpb[:, dc * P:(dc + 1) * P], idbf[:]))
                                    sop(lambda e, dc=dc, wr=wr: e.copy(wr[:, dc * P:(dc + 1) * P], psW[:]))
                                    dma(wtc[dc * P:(dc + 1) * P, fc * P:(fc + 1) * P], wr[:, dc * P:(dc + 1) * P])
                            else:
                                for dc in range(DC):
                                    dma(wr[:, dc * P:(dc + 1) * P], wtc[dc * P:(dc + 1) * P, fc * P:(fc + 1) * P])
                            for sl in range(NSL):
                                ps = psM if sl % 2 == 0 else psM2
                                for dc in range(DC):
                                    top_(lambda e, dc=dc, sl=sl, wr=wr, ps=ps: e.matmul(
                                        ps[:], wr[:, dc * P:(dc + 1) * P],
                                        xqib[:, dc * T + sl * SL: dc * T + (sl + 1) * SL],
                                        start=(dc == 0), stop=(dc == DC - 1)))
                                vop(lambda e, sl=sl, ps=ps: e.tensor_copy(
                                    mst[:, sl * SL:(sl + 1) * SL], ps[:]))
                            vop_red(lambda e: e.reduce_max(mtmp[:, 0:1], mst[:], axis=AX, apply_absolute_value=True))
                            vop(lambda e, wi=wi: e.tensor_tensor(lmax[:, 2 + 2 * wi:3 + 2 * wi], lmax[:, 2 + 2 * wi:3 + 2 * wi], mtmp[:, 0:1], OP.max))
                            dma(spill[fc * P:(fc + 1) * P, :], mst[:])
                        cross_part_absmax(lmax[:, 2 + 2 * wi:3 + 2 * wi], scrow[0:1, IS_T0 + wi:IS_T0 + wi + 1])
                        vop_red(lambda e, wi=wi: e.tensor_copy(ccl[0:1, wi:wi + 1], scrow[0:1, IS_T0 + wi:IS_T0 + wi + 1]))

                    if inv == 0:
                        with ExitStack() as _st:
                            wd3 = _st.enter_context(nc.sbuf_tensor(nm("wd3"), [P, FF], F32))
                            wdq = _st.enter_context(nc.sbuf_tensor(nm("wdq"), [P, FF], F32))
                            wdb = _st.enter_context(nc.sbuf_tensor(nm("wdb"), [P, FF], BF16))
                            bm3 = _st.enter_context(nc.sbuf_tensor(nm("bm3"), [P, 192], F32))
                            wds = _st.enter_context(nc.sbuf_tensor(nm("wds"), [P, P], BF16))
                            for r in range(DC):
                                dma(wd3[:], wdown_h[r * P:(r + 1) * P, :])
                                vop_red(lambda e: e.reduce_max(
                                    bm3[:, 0:96], wd3[:].rearrange("p (b e) -> p b e", e=32),
                                    axis=AX, apply_absolute_value=True))
                                vop(lambda e: e.tensor_scalar(bm3[:, 0:96], bm3[:, 0:96], 1.0 / QW, 1e-12, OP.mult, OP.max))
                                vop_red(lambda e: e.reciprocal(bm3[:, 96:192], bm3[:, 0:96]))
                                vop(lambda e: e.tensor_tensor(
                                    wdq[:].rearrange("p (b e) -> p b e", e=32),
                                    wd3[:].rearrange("p (b e) -> p b e", e=32),
                                    bm3[:, 96:192].unsqueeze(2).broadcast_to([P, 96, 32]), OP.mult))
                                vop(lambda e: e.tensor_scalar(wdq[:], wdq[:], RC, RC, OP.add, OP.subtract))
                                vop(lambda e: e.tensor_tensor(
                                    wdb[:].rearrange("p (b e) -> p b e", e=32),
                                    wdq[:].rearrange("p (b e) -> p b e", e=32),
                                    bm3[:, 0:96].unsqueeze(2).broadcast_to([P, 96, 32]), OP.mult))
                                for fc in range(FCN):
                                    top_(lambda e, fc=fc: e.transpose(psW[:], wdb[:, fc * P:(fc + 1) * P], idbf[:]))
                                    sop(lambda e: e.copy(wds[:], psW[:]))
                                    dma(wdT_c[fc * P:(fc + 1) * P, r * P:(r + 1) * P], wds[:])

                    allreduce_max(2)
                    vop_red(lambda e: e.tensor_scalar(ccr[0:1, 0:2], ccr[0:1, 0:2], scrow[0:1, IS_S1:IS_S1 + 1], None, OP.mult))
                    vop_red(lambda e: e.tensor_scalar(scrow[0:1, IS_S3:IS_S3 + 1], ccr[0:1, 0:1], 1.0 / QA, 1e-12, OP.mult, OP.max))
                    vop_red(lambda e: e.tensor_scalar(scrow[0:1, IS_S2:IS_S2 + 1], ccr[0:1, 1:2], 1.0 / QA, 1e-12, OP.mult, OP.max))
                    vop_red(lambda e: e.reciprocal(scrow[0:1, IS_T1:IS_T1 + 1], scrow[0:1, IS_S3:IS_S3 + 1]))
                    vop_red(lambda e: e.tensor_scalar(scrow[0:1, IS_T1:IS_T1 + 1], scrow[0:1, IS_T1:IS_T1 + 1], scrow[0:1, IS_S1:IS_S1 + 1], None, OP.mult))
                    bcast128(scrow[0:1, IS_T1:IS_T1 + 1], SC_K1)
                    vop_red(lambda e: e.reciprocal(scrow[0:1, IS_T2:IS_T2 + 1], scrow[0:1, IS_S2:IS_S2 + 1]))
                    vop_red(lambda e: e.tensor_scalar(scrow[0:1, IS_T2:IS_T2 + 1], scrow[0:1, IS_T2:IS_T2 + 1], scrow[0:1, IS_S1:IS_S1 + 1], None, OP.mult))
                    bcast128(scrow[0:1, IS_T2:IS_T2 + 1], SC_K2)
                    bcast128(scrow[0:1, IS_S3:IS_S3 + 1], SC_S3)
                    S.end_phase(blk)

            with ExitStack() as _st:
                e1m = _st.enter_context(nc.sbuf_tensor(nm("e1m"), [P, T], F32))
                e1a = _st.enter_context(nc.sbuf_tensor(nm("e1a"), [P, T], F32))
                e1b = _st.enter_context(nc.sbuf_tensor(nm("e1b"), [P, T], F32))
                e1c = _st.enter_context(nc.sbuf_tensor(nm("e1c"), [P, T], F32))
                psx_l = _st.enter_context(nc.psum_tensor(nm("psx5"), [1, P], F32))
                pssc_l = _st.enter_context(nc.psum_tensor(nm("pssc5"), [P, 8], F32))
                blk = _st.enter_context(nc.Block())
                H["psx"], H["pssc"] = psx_l, pssc_l
                S.start_phase()
                vop_red(lambda e: e.memset(lmax[:, 6:7], 0.0))
                for fc in range(FCN):
                    dma(e1m[:], mg[fc * P:(fc + 1) * P, :])
                    sop(lambda e: e.activation(e1a[:], e1m[:], AF.Identity, bias=crnd[:], scale=sc[:, SC_K1:SC_K1 + 1]))
                    vop(lambda e: e.tensor_scalar(e1a[:], e1a[:], RC, None, OP.subtract))
                    sop(lambda e: e.activation(e1b[:], e1a[:], AF.Sigmoid, bias=czero[:], scale=sc[:, SC_S3:SC_S3 + 1]))
                    sop(lambda e: e.activation(e1b[:], e1b[:], AF.Identity, bias=crnd[:], scale=c65k[:]))
                    vop(lambda e: e.tensor_scalar(e1b[:], e1b[:], RC, 65535.0, OP.subtract, OP.min))
                    vop(lambda e: e.tensor_tensor(e1c[:], e1a[:], e1b[:], OP.mult))
                    vop_red(lambda e: e.reduce_max(lmax[:, 7:8], e1c[:], axis=AX, apply_absolute_value=True))
                    vop_red(lambda e: e.tensor_tensor(lmax[:, 6:7], lmax[:, 6:7], lmax[:, 7:8], OP.max))
                    dma(ci[fc * P:(fc + 1) * P, :], e1c[:])
                cross_part_absmax(lmax[:, 6:7], scrow[0:1, IS_T0:IS_T0 + 1])
                vop_red(lambda e: e.tensor_copy(ccl[0:1, 0:1], scrow[0:1, IS_T0:IS_T0 + 1]))
                allreduce_max(1)
                vop_red(lambda e: e.tensor_scalar(scrow[0:1, IS_T1:IS_T1 + 1], scrow[0:1, IS_S3:IS_S3 + 1], 1.0 / 65536.0, None, OP.mult))
                vop_red(lambda e: e.tensor_scalar(scrow[0:1, IS_T2:IS_T2 + 1], ccr[0:1, 0:1], scrow[0:1, IS_T1:IS_T1 + 1], None, OP.mult))
                vop_red(lambda e: e.tensor_scalar(scrow[0:1, IS_S4:IS_S4 + 1], scrow[0:1, IS_T2:IS_T2 + 1], 1.0 / QA, 1e-12, OP.mult, OP.max))
                vop_red(lambda e: e.reciprocal(scrow[0:1, IS_T2:IS_T2 + 1], scrow[0:1, IS_S4:IS_S4 + 1]))
                vop_red(lambda e: e.tensor_scalar(scrow[0:1, IS_T2:IS_T2 + 1], scrow[0:1, IS_T2:IS_T2 + 1], scrow[0:1, IS_T1:IS_T1 + 1], None, OP.mult))
                bcast128(scrow[0:1, IS_T2:IS_T2 + 1], SC_K3)
                S.end_phase(blk)

            with ExitStack() as _st:
                e2a = _st.enter_context(nc.sbuf_tensor(nm("e2a"), [P, T], F32))
                e2b = _st.enter_context(nc.sbuf_tensor(nm("e2b"), [P, T], F32))
                e2c = _st.enter_context(nc.sbuf_tensor(nm("e2c"), [P, T], F32))
                e2i = _st.enter_context(nc.sbuf_tensor(nm("e2i"), [P, T], I16))
                e2j = _st.enter_context(nc.sbuf_tensor(nm("e2j"), [P, T], I16))
                psx_l = _st.enter_context(nc.psum_tensor(nm("psx6"), [1, P], F32))
                pssc_l = _st.enter_context(nc.psum_tensor(nm("pssc6"), [P, 8], F32))
                blk = _st.enter_context(nc.Block())
                H["psx"], H["pssc"] = psx_l, pssc_l
                S.start_phase()
                vop_red(lambda e: e.memset(lmax[:, 6:7], 0.0))
                for fc in range(FCN):
                    dma(e2a[:], ci[fc * P:(fc + 1) * P, :])
                    sop(lambda e: e.activation(e2a[:], e2a[:], AF.Identity, bias=crnd[:], scale=sc[:, SC_K3:SC_K3 + 1]))
                    vop(lambda e: e.tensor_scalar(e2a[:], e2a[:], RC, None, OP.subtract))
                    vop(lambda e: e.tensor_copy(e2i[:], e2a[:]))
                    dma(g2s[fc * P:(fc + 1) * P, :], e2i[:])
                    dma(e2b[:], mup[fc * P:(fc + 1) * P, :])
                    sop(lambda e: e.activation(e2b[:], e2b[:], AF.Identity, bias=crnd[:], scale=sc[:, SC_K2:SC_K2 + 1]))
                    vop(lambda e: e.tensor_scalar(e2b[:], e2b[:], RC, None, OP.subtract))
                    vop(lambda e: e.tensor_copy(e2j[:], e2b[:]))
                    dma(ups[fc * P:(fc + 1) * P, :], e2j[:])
                    vop(lambda e: e.tensor_tensor(e2c[:], e2a[:], e2b[:], OP.mult))
                    vop_red(lambda e: e.reduce_max(lmax[:, 7:8], e2c[:], axis=AX, apply_absolute_value=True))
                    vop_red(lambda e: e.tensor_tensor(lmax[:, 6:7], lmax[:, 6:7], lmax[:, 7:8], OP.max))
                cross_part_absmax(lmax[:, 6:7], scrow[0:1, IS_T0:IS_T0 + 1])
                vop_red(lambda e: e.tensor_copy(ccl[0:1, 0:1], scrow[0:1, IS_T0:IS_T0 + 1]))
                allreduce_max(1)
                vop_red(lambda e: e.tensor_scalar(scrow[0:1, IS_T1:IS_T1 + 1], scrow[0:1, IS_S4:IS_S4 + 1], scrow[0:1, IS_S2:IS_S2 + 1], None, OP.mult))
                vop_red(lambda e: e.tensor_scalar(scrow[0:1, IS_T2:IS_T2 + 1], ccr[0:1, 0:1], scrow[0:1, IS_T1:IS_T1 + 1], None, OP.mult))
                vop_red(lambda e: e.tensor_scalar(scrow[0:1, IS_S5:IS_S5 + 1], scrow[0:1, IS_T2:IS_T2 + 1], 1.0 / QA, 1e-12, OP.mult, OP.max))
                vop_red(lambda e: e.reciprocal(scrow[0:1, IS_T2:IS_T2 + 1], scrow[0:1, IS_S5:IS_S5 + 1]))
                vop_red(lambda e: e.tensor_scalar(scrow[0:1, IS_T2:IS_T2 + 1], scrow[0:1, IS_T2:IS_T2 + 1], scrow[0:1, IS_T1:IS_T1 + 1], None, OP.mult))
                bcast128(scrow[0:1, IS_T2:IS_T2 + 1], SC_K4)
                vop_red(lambda e: e.tensor_scalar(scrow[0:1, IS_T1:IS_T1 + 1], scrow[0:1, IS_S5:IS_S5 + 1], 0.5, None, OP.mult))
                bcast128(scrow[0:1, IS_T1:IS_T1 + 1], SC_BETA)
                S.end_phase(blk)

            with ExitStack() as _st:
                wdTs = _st.enter_context(nc.sbuf_tensor(nm("wdTs"), [P, FCN * D], BF16))
                oib = _st.enter_context(nc.sbuf_tensor(nm("oib"), [P, FCN * SL], BF16))
                e3g = _st.enter_context(nc.sbuf_tensor(nm("e3g"), [P, FCN * SL], I16))
                e3u = _st.enter_context(nc.sbuf_tensor(nm("e3u"), [P, FCN * SL], I16))
                e3d = _st.enter_context(nc.sbuf_tensor(nm("e3d"), [P, FCN * SL], F32))
                e3x = _st.enter_context(nc.sbuf_tensor(nm("e3x"), [P, SL], F32))
                e3s = _st.enter_context(nc.sbuf_tensor(nm("e3s"), [P, SL], F32))
                psD = _st.enter_context(nc.psum_tensor(nm("psD"), [P, SL], F32))
                psD2 = _st.enter_context(nc.psum_tensor(nm("psD2"), [P, SL], F32))
                blk = _st.enter_context(nc.Block())
                S.start_phase()
                for fc in range(FCN):
                    dma(wdTs[:, fc * D:(fc + 1) * D], wdT_c[fc * P:(fc + 1) * P, :])
                for sl in range(NSL):
                    for fc in range(FCN):
                        dma(e3g[:, fc * SL:(fc + 1) * SL], g2s[fc * P:(fc + 1) * P, sl * SL:(sl + 1) * SL])
                        dma(e3u[:, fc * SL:(fc + 1) * SL], ups[fc * P:(fc + 1) * P, sl * SL:(sl + 1) * SL])
                    vop(lambda e: e.tensor_tensor(e3d[:], e3g[:], e3u[:], OP.mult))
                    sop(lambda e: e.activation(e3d[:], e3d[:], AF.Identity, bias=crnd[:], scale=sc[:, SC_K4:SC_K4 + 1]))
                    vop(lambda e: e.tensor_scalar(oib[:], e3d[:], RC, None, OP.subtract))
                    for dc in range(DC):
                        ps = psD if dc % 2 == 0 else psD2
                        for fc in range(FCN):
                            top_(lambda e, fc=fc, dc=dc, ps=ps: e.matmul(
                                ps[:], wdTs[:, fc * D + dc * P: fc * D + (dc + 1) * P],
                                oib[:, fc * SL:(fc + 1) * SL],
                                start=(fc == 0), stop=(fc == FCN - 1)))
                        dma(e3x[:], src[dc * P:(dc + 1) * P, sl * SL:(sl + 1) * SL])
                        sop(lambda e, ps=ps: e.activation(e3s[:], ps[:], AF.Identity, bias=czero[:], scale=sc[:, SC_BETA:SC_BETA + 1]))
                        vop(lambda e: e.tensor_tensor(e3s[:], e3s[:], e3x[:], OP.add))
                        dma(src[dc * P:(dc + 1) * P, sl * SL:(sl + 1) * SL], e3s[:])
                S.end_phase(blk)

        def attn(src):
            with ExitStack() as _st:
                xn2b = _st.enter_context(nc.sbuf_tensor(nm("xn2b"), [P, DC * T], BF16))
                kvb = _st.enter_context(nc.sbuf_tensor(nm("kvb"), [P, 16 * P], BF16))
                kdb = _st.enter_context(nc.sbuf_tensor(nm("kdb"), [P, 32], BF16))
                with nc.sbuf_tensor(nm("rows_a"), [1, T], F32) as rows_a, nc.sbuf_tensor(nm("rows_b"), [1, T], F32) as rows_b:
                    ln_stats(src, rows_a, rows_b)
                    xn_pass(src, xn2b, rows_a, rows_b)

                with ExitStack() as _st:
                    wkb = _st.enter_context(nc.sbuf_tensor(nm("wkb"), [P, DC * D], BF16))
                    wvb = _st.enter_context(nc.sbuf_tensor(nm("wvb"), [P, DC * D], BF16))
                    wldf = _st.enter_context(nc.sbuf_tensor(nm("wldf"), [P, D], F32))
                    bkb = _st.enter_context(nc.sbuf_tensor(nm("bkb"), [P, D], F32))
                    bvb = _st.enter_context(nc.sbuf_tensor(nm("bvb"), [P, D], F32))
                    ktf = _st.enter_context(nc.sbuf_tensor(nm("ktf"), [P, SL], F32))
                    kte = _st.enter_context(nc.sbuf_tensor(nm("kte"), [P, SL], F32))
                    ktb = _st.enter_context(nc.sbuf_tensor(nm("ktb"), [P, D], BF16))
                    vtb = _st.enter_context(nc.sbuf_tensor(nm("vtb"), [P, D], BF16))
                    psK = _st.enter_context(nc.psum_tensor(nm("psK"), [P, SL], F32))
                    psV = _st.enter_context(nc.psum_tensor(nm("psV"), [P, SL], F32))
                    psKV0 = _st.enter_context(nc.psum_tensor(nm("psKV0"), [P, SL], F32))
                    psKV1 = _st.enter_context(nc.psum_tensor(nm("psKV1"), [P, SL], F32))
                    psKV2 = _st.enter_context(nc.psum_tensor(nm("psKV2"), [P, SL], F32))
                    psKV3 = _st.enter_context(nc.psum_tensor(nm("psKV3"), [P, SL], F32))
                    psBC = _st.enter_context(nc.psum_tensor(nm("psBC"), [P, SL], F32))
                    blk = _st.enter_context(nc.Block())
                    psKV = [psKV0, psKV1, psKV2, psKV3]
                    S.start_phase()
                    for dk in range(DC):
                        dma(wldf[:], wk_h[dk * P:(dk + 1) * P, :])
                        vop(lambda e, dk=dk: e.tensor_copy(wkb[:, dk * D:(dk + 1) * D], wldf[:]))
                        dma(wldf[:], wv_h[dk * P:(dk + 1) * P, :])
                        vop(lambda e, dk=dk: e.tensor_copy(wvb[:, dk * D:(dk + 1) * D], wldf[:]))
                    for h in range(2):
                        top_(lambda e, h=h: e.matmul(psBC[:], onesr[0:1, :], bkr[0:1, h * SL:(h + 1) * SL], start=True, stop=True))
                        sop(lambda e, h=h: e.copy(bkb[:, h * SL:(h + 1) * SL], psBC[:]))
                        top_(lambda e, h=h: e.matmul(psBC[:], onesr[0:1, :], bvr[0:1, h * SL:(h + 1) * SL], start=True, stop=True))
                        sop(lambda e, h=h: e.copy(bvb[:, h * SL:(h + 1) * SL], psBC[:]))
                    for tc in range(TCN):
                        sq, ti = tc // 16, tc % 16
                        for half in range(2):
                            for dk in range(DC):
                                top_(lambda e, dk=dk, tc=tc, half=half: e.matmul(
                                    psK[:], xn2b[:, dk * T + tc * P:dk * T + (tc + 1) * P],
                                    wkb[:, dk * D + half * SL: dk * D + (half + 1) * SL],
                                    start=(dk == 0), stop=(dk == DC - 1)))
                            vop(lambda e, half=half: e.tensor_tensor(ktf[:], psK[:], bkb[:, half * SL:(half + 1) * SL], OP.add))
                            vop(lambda e: e.tensor_scalar(kte[:], ktf[:], 0.0, None, OP.min))
                            sop(lambda e: e.activation(kte[:], kte[:], AF.Exp))
                            vop(lambda e, half=half: e.scalar_tensor_tensor(
                                ktb[:, half * SL:(half + 1) * SL], ktf[:], 0.0, kte[:], OP.max, OP.add))
                            for dk in range(DC):
                                top_(lambda e, dk=dk, tc=tc, half=half: e.matmul(
                                    psV[:], xn2b[:, dk * T + tc * P:dk * T + (tc + 1) * P],
                                    wvb[:, dk * D + half * SL: dk * D + (half + 1) * SL],
                                    start=(dk == 0), stop=(dk == DC - 1)))
                            vop(lambda e, half=half: e.tensor_tensor(
                                vtb[:, half * SL:(half + 1) * SL], psV[:], bvb[:, half * SL:(half + 1) * SL], OP.add))
                        for p in range(8):
                            ps = psKV[p // 2]
                            off = (p % 2) * 256
                            top_(lambda e, p=p, ps=ps, off=off, ti=ti: e.matmul(
                                ps[:, off:off + P], ktb[:, p * P:(p + 1) * P], vtb[:, p * P:(p + 1) * P],
                                start=(ti == 0), stop=(ti == 15), skip_group_check=True))
                            top_(lambda e, p=p, ps=ps, off=off, ti=ti: e.matmul(
                                ps[:, off + P:off + P + 1], ktb[:, p * P:(p + 1) * P], onescb[:],
                                start=(ti == 0), stop=(ti == 15), skip_group_check=True))
                        if ti == 15:
                            for p in range(8):
                                ps = psKV[p // 2]
                                off = (p % 2) * 256
                                kvt = kvb[:, (sq * 8 + p) * P:(sq * 8 + p + 1) * P]
                                vop_red(lambda e, kvt=kvt: e.memset(kvt, 0.0))
                                vop(lambda e, ps=ps, off=off, kvt=kvt: e.tensor_copy(kvt[0:64, 0:64], ps[0:64, off:off + 64]))
                                vop(lambda e, ps=ps, off=off, kvt=kvt: e.tensor_copy(kvt[64:128, 64:128], ps[64:128, off + 64:off + 128]))
                                kdt = kdb[:, (sq * 8 + p) * 2:(sq * 8 + p) * 2 + 2]
                                vop_red(lambda e, kdt=kdt: e.memset(kdt, 0.0))
                                vop(lambda e, ps=ps, off=off, kdt=kdt: e.tensor_copy(kdt[0:64, 0:1], ps[0:64, off + P:off + P + 1]))
                                vop(lambda e, ps=ps, off=off, kdt=kdt: e.tensor_copy(kdt[64:128, 1:2], ps[64:128, off + P:off + P + 1]))
                    S.end_phase(blk)

                with ExitStack() as _st:
                    wqsb = _st.enter_context(nc.sbuf_tensor(nm("wqsb"), [P, DC * D], BF16))
                    wgsb = _st.enter_context(nc.sbuf_tensor(nm("wgsb"), [P, DC * D], BF16))
                    wosb = _st.enter_context(nc.sbuf_tensor(nm("wosb"), [P, DC * D], BF16))
                    wldf2 = _st.enter_context(nc.sbuf_tensor(nm("wldf2"), [P, D], F32))
                    qrf = _st.enter_context(nc.sbuf_tensor(nm("qrf"), [P, SL], F32))
                    qef = _st.enter_context(nc.sbuf_tensor(nm("qef"), [P, SL], F32))
                    qbf = _st.enter_context(nc.sbuf_tensor(nm("qbf"), [P, DC * SL], BF16))
                    zsb = _st.enter_context(nc.sbuf_tensor(nm("zsb"), [2, SL], F32))
                    zbs = _st.enter_context(nc.sbuf_tensor(nm("zbs"), [P, SL], F32))
                    ssb = _st.enter_context(nc.sbuf_tensor(nm("ssb"), [P, SL], F32))
                    gsb = _st.enter_context(nc.sbuf_tensor(nm("gsb"), [P, DC * SL], F32))
                    abf = _st.enter_context(nc.sbuf_tensor(nm("abf"), [P, DC * SL], BF16))
                    aor = _st.enter_context(nc.sbuf_tensor(nm("aor"), [P, SL], F32))
                    axr = _st.enter_context(nc.sbuf_tensor(nm("axr"), [P, SL], F32))
                    psQ7 = _st.enter_context(nc.psum_tensor(nm("psQ7"), [P, SL], F32))
                    psQ8 = _st.enter_context(nc.psum_tensor(nm("psQ8"), [P, SL], F32))
                    psZr = _st.enter_context(nc.psum_tensor(nm("psZr"), [2, SL], F32))
                    psZb = _st.enter_context(nc.psum_tensor(nm("psZb"), [P, SL], F32))
                    psS7 = _st.enter_context(nc.psum_tensor(nm("psS7"), [P, SL], F32))
                    psO7 = _st.enter_context(nc.psum_tensor(nm("psO7"), [P, SL], F32))
                    psO8 = _st.enter_context(nc.psum_tensor(nm("psO8"), [P, SL], F32))
                    blk2 = _st.enter_context(nc.Block())
                    S.start_phase()
                    for dk in range(DC):
                        dma(wldf2[:], wq_h[dk * P:(dk + 1) * P, :])
                        vop(lambda e, dk=dk: e.tensor_copy(wqsb[:, dk * D:(dk + 1) * D], wldf2[:]))
                        dma(wldf2[:], wg_h[dk * P:(dk + 1) * P, :])
                        vop(lambda e, dk=dk: e.tensor_copy(wgsb[:, dk * D:(dk + 1) * D], wldf2[:]))
                        dma(wldf2[:], wo_h[dk * P:(dk + 1) * P, :])
                        vop(lambda e, dk=dk: e.tensor_copy(wosb[:, dk * D:(dk + 1) * D], wldf2[:]))
                    for gsl in range(NSL):
                        sq = gsl // 4
                        for qc in range(DC):
                            ps = psQ7 if qc % 2 == 0 else psQ8
                            for dk in range(DC):
                                top_(lambda e, dk=dk, qc=qc, gsl=gsl, ps=ps: e.matmul(
                                    ps[:], wqsb[:, dk * D + qc * P: dk * D + (qc + 1) * P],
                                    xn2b[:, dk * T + gsl * SL: dk * T + (gsl + 1) * SL],
                                    start=(dk == 0), stop=(dk == DC - 1)))
                            sop(lambda e, qc=qc, ps=ps: e.activation(qrf[:], ps[:], AF.Identity, bias=bq_sb[:, qc:qc + 1]))
                            vop(lambda e: e.tensor_scalar(qef[:], qrf[:], 0.0, None, OP.min))
                            sop(lambda e: e.activation(qef[:], qef[:], AF.Exp))
                            vop(lambda e, qc=qc: e.scalar_tensor_tensor(
                                qbf[:, qc * SL:(qc + 1) * SL], qrf[:], 0.0, qef[:], OP.max, OP.add))
                        for gc in range(DC):
                            ps = psQ7 if gc % 2 == 0 else psQ8
                            for dk in range(DC):
                                top_(lambda e, dk=dk, gc=gc, gsl=gsl, ps=ps: e.matmul(
                                    ps[:], wgsb[:, dk * D + gc * P: dk * D + (gc + 1) * P],
                                    xn2b[:, dk * T + gsl * SL: dk * T + (gsl + 1) * SL],
                                    start=(dk == 0), stop=(dk == DC - 1)))
                            sop(lambda e, gc=gc, ps=ps: e.activation(
                                gsb[:, gc * SL:(gc + 1) * SL], ps[:], AF.Sigmoid, bias=bg_sb[:, gc:gc + 1]))
                        for pp in range(8):
                            kdt = kdb[:, (sq * 8 + pp) * 2:(sq * 8 + pp) * 2 + 2]
                            top_(lambda e, kdt=kdt, pp=pp: e.matmul(
                                psZr[:], kdt, qbf[:, pp * SL:(pp + 1) * SL], start=True, stop=True))
                            vop(lambda e: e.tensor_scalar(zsb[:], psZr[:], 1e-6, None, OP.add))
                            vop_red(lambda e: e.reciprocal(zsb[:], zsb[:]))
                            top_(lambda e: e.matmul(psZb[:], patt2[:], zsb[:], start=True, stop=True))
                            vop(lambda e: e.tensor_copy(zbs[:], psZb[:]))
                            kvt = kvb[:, (sq * 8 + pp) * P:(sq * 8 + pp + 1) * P]
                            top_(lambda e, kvt=kvt, pp=pp: e.matmul(
                                psS7[:], kvt, qbf[:, pp * SL:(pp + 1) * SL], start=True, stop=True))
                            vop(lambda e: e.tensor_tensor(ssb[:], psS7[:], zbs[:], OP.mult))
                            vop(lambda e, pp=pp: e.tensor_tensor(
                                abf[:, pp * SL:(pp + 1) * SL], ssb[:], gsb[:, pp * SL:(pp + 1) * SL], OP.mult))
                        for oc in range(DC):
                            ps = psO7 if oc % 2 == 0 else psO8
                            for dk in range(DC):
                                top_(lambda e, dk=dk, oc=oc, ps=ps: e.matmul(
                                    ps[:], wosb[:, dk * D + oc * P: dk * D + (oc + 1) * P],
                                    abf[:, dk * SL:(dk + 1) * SL],
                                    start=(dk == 0), stop=(dk == DC - 1)))
                            sop(lambda e, oc=oc, ps=ps: e.activation(aor[:], ps[:], AF.Identity, bias=bo_sb[:, oc:oc + 1]))
                            dma(axr[:], src[oc * P:(oc + 1) * P, gsl * SL:(gsl + 1) * SL])
                            vop(lambda e: e.tensor_tensor(aor[:], aor[:], axr[:], OP.add))
                            dma(src[oc * P:(oc + 1) * P, gsl * SL:(gsl + 1) * SL], aor[:])
                    S.end_phase(blk2)

        ffn(0, xfm)
        if debug:
            with nc.Block() as blk:
                S.start_phase()
                with nc.sbuf_tensor(nm("dbt"), [P, T], F32) as dbt:
                    for dc in range(DC):
                        dma(dbt[:], xfm[dc * P:(dc + 1) * P, :])
                        S.gap()
                        dma(dbg[0][dc * P:(dc + 1) * P, :], dbt[:])
                        S.gap()
                    dma(dbgs[0:1, 0:32], scrow[0:1, 0:32])
                S.end_phase(blk)
        attn(xfm)
        if debug:
            with nc.Block() as blk:
                S.start_phase()
                with nc.sbuf_tensor(nm("dbt2"), [P, T], F32) as dbt:
                    for dc in range(DC):
                        dma(dbt[:], xfm[dc * P:(dc + 1) * P, :])
                        S.gap()
                        dma(dbg[1][dc * P:(dc + 1) * P, :], dbt[:])
                        S.gap()
                S.end_phase(blk)
        ffn(1, xfm)
        if debug:
            with nc.Block() as blk:
                S.start_phase()
                with nc.sbuf_tensor(nm("dbt3"), [P, T], F32) as dbt:
                    for dc in range(DC):
                        dma(dbt[:], xfm[dc * P:(dc + 1) * P, :])
                        S.gap()
                        dma(dbg[2][dc * P:(dc + 1) * P, :], dbt[:])
                        S.gap()
                S.end_phase(blk)

        with ExitStack() as _st:
            ost = _st.enter_context(nc.sbuf_tensor(nm("ost"), [P, DC * 1024], F32))
            oyt = _st.enter_context(nc.sbuf_tensor(nm("oyt"), [P, D], F32))
            pst_o = _st.enter_context(nc.psum_tensor(nm("pst_o"), [P, P], F32))
            blk = _st.enter_context(nc.Block())
            S.start_phase()
            for tg in range(TCN // 8):
                base = tg * 1024
                for dc in range(DC):
                    dma(ost[:, dc * 1024:(dc + 1) * 1024], xfm[dc * P:(dc + 1) * P, base:base + 1024])
                for g in range(8):
                    tc = tg * 8 + g
                    for dc in range(DC):
                        top_(lambda e, dc=dc, g=g: e.transpose(
                            pst_o[:], ost[:, dc * 1024 + g * P: dc * 1024 + (g + 1) * P], id32[:]))
                        sop(lambda e, dc=dc: e.copy(oyt[:, dc * P:(dc + 1) * P], pst_o[:]))
                    dma(y_out[tc * P:(tc + 1) * P, :], oyt[:])
            S.wait_all("sync")
            S.end_phase(blk)

    return nc


_NC_CACHE = {}


def _get_nc(debug=False):
    key = bool(debug)
    if key not in _NC_CACHE:
        _NC_CACHE[key] = build(debug=debug)
    return _NC_CACHE[key]


def _in_maps(inputs, ncores=8):
    f32 = np.float32
    x = np.ascontiguousarray(np.asarray(inputs["L0_attn_x"], dtype=f32))
    base = {
        "wq": np.ascontiguousarray(np.asarray(inputs["L0_attn_wq_w"], f32)),
        "wk": np.ascontiguousarray(np.asarray(inputs["L0_attn_wk_w"], f32)),
        "wv": np.ascontiguousarray(np.asarray(inputs["L0_attn_wv_w"], f32)),
        "wg": np.ascontiguousarray(np.asarray(inputs["L0_attn_wg_w"], f32)),
        "wo": np.ascontiguousarray(np.asarray(inputs["L0_attn_out_w"], f32)),
        "bq": np.ascontiguousarray(np.asarray(inputs["L0_attn_wq_b"], f32).reshape(D, 1)),
        "bk": np.ascontiguousarray(np.asarray(inputs["L0_attn_wk_b"], f32).reshape(1, D)),
        "bv": np.ascontiguousarray(np.asarray(inputs["L0_attn_wv_b"], f32).reshape(1, D)),
        "bg": np.ascontiguousarray(np.asarray(inputs["L0_attn_wg_b"], f32).reshape(D, 1)),
        "bo": np.ascontiguousarray(np.asarray(inputs["L0_attn_out_b"], f32).reshape(D, 1)),
        "wgate": np.ascontiguousarray(np.asarray(inputs["L0_ffn_w_gate"], f32)),
        "wup": np.ascontiguousarray(np.asarray(inputs["L0_ffn_w_up"], f32)),
        "wdown": np.ascontiguousarray(np.asarray(inputs["L0_ffn_w_down"], f32)),
        "id32": np.eye(P, dtype=f32),
        "idbf": np.eye(P).astype(ml_dtypes.bfloat16),
        "onesc": np.ones((P, 1), f32),
        "onesr": np.ones((1, P), f32),
        "onescb": np.ones((P, 1), ml_dtypes.bfloat16),
        "patt2": np.concatenate(
            [np.concatenate([np.ones((1, 64), f32), np.zeros((1, 64), f32)], axis=1),
             np.concatenate([np.zeros((1, 64), f32), np.ones((1, 64), f32)], axis=1)], axis=0),
    }
    maps = []
    for c in range(ncores):
        m = dict(base)
        m["x"] = np.ascontiguousarray(x[2 * c:2 * c + 2].reshape(T, D))
        maps.append(m)
    return maps


def kernel(**inputs):
    nc = _get_nc(debug=False)
    maps = _in_maps(inputs)
    res = run_bass_kernel_spmd(nc, maps, core_ids=list(range(8)))
    out = np.concatenate(
        [np.asarray(res.results[c]["y"]).reshape(2, 2048, D) for c in range(8)], axis=0
    )
    return out
